# version 02
# baseline (speedup 1.0000x reference)
from contextlib import ExitStack
import numpy as np
import ml_dtypes
import concourse.bass as bass
import concourse.mybir as mybir
from concourse.bass_utils import run_bass_kernel_spmd

F32 = mybir.dt.float32
BF16 = mybir.dt.bfloat16
I32 = mybir.dt.int32
AF = mybir.ActivationFunctionType
ALU = mybir.AluOpType

NCORES = 8
S = 8192
D = 2048
NBLK = 16
BLK = 512
BLOCKS_OF_CORE = [(c, 15 - c) for c in range(NCORES)]

ENGS = ("sync", "pe", "dve", "act", "pool")


class Sched:
    def __init__(self):
        self.c = {}

    def inc(self, name, n=1):
        self.c[name] = self.c.get(name, 0) + n
        return self.c[name]

    def val(self, name):
        return self.c.get(name, 0)

    def set(self, name, v):
        self.c[name] = v


def build_blocked(nc, gen, sems=None):
    def mk():
        s = Sched()
        s.sems = sems
        return s
    with nc.Block() as block:
        block.sync(lambda e: gen("sync", e, mk()))
        block.tensor(lambda e: gen("pe", e, mk()))
        block.vector(lambda e: gen("dve", e, mk()))
        block.scalar(lambda e: gen("act", e, mk()))
        block.gpsimd(lambda e: gen("pool", e, mk()))
    return nc


def ceil_div(a, b):
    return -(-a // b)


PREP_MATS = [
    ("gw", 768, 2048, False),
    ("uw", 768, 2048, False),
    ("dw", 256, 5632, False),
    ("bq", 256, 2048, False),
    ("bk", 64, 2048, False),
    ("bv", 64, 2048, False),
    ("bo", 256, 2048, False),
    ("gg", 768, 2048, True),
    ("gu", 768, 2048, True),
    ("gd", 256, 5504, True),
    ("w1a", 1024, 2048, False),
    ("w1b", 1024, 2048, False),
    ("w2a", 256, 8192, False),
    ("w2b", 256, 8192, False),
]


def build_prep():
    nc = bass.Bass("TRN2", target_bir_lowering=False, debug=False)
    ins = {}
    outs = {}
    for name, rows, cols, is_ghost in PREP_MATS:
        dt = I32 if is_ghost else F32
        ins[name] = nc.dram_tensor("in_" + name, [rows, cols], dt, kind="ExternalInput").ap()
        prow = min(rows, 128)
        ns = ceil_div(rows, 128)
        ic = cols // 128
        outs[name] = nc.dram_tensor("out_" + name, [ns, 128, ic * prow], BF16,
                                    kind="ExternalOutput").ap()
    lut = nc.dram_tensor("in_lut", [1, 16], F32, kind="ExternalInput").ap()

    ctx = ExitStack()
    with ctx:
        wrow = ctx.enter_context(nc.sbuf_tensor("wrow", [128, 8192], BF16))
        slab = ctx.enter_context(nc.sbuf_tensor("slab", [128, 8192], BF16))
        i32t = ctx.enter_context(nc.sbuf_tensor("i32t", [128, 5504], I32))
        idxb = ctx.enter_context(nc.sbuf_tensor("idxb", [128, 5504], BF16))
        tmpb = ctx.enter_context(nc.sbuf_tensor("tmpb", [128, 5504], BF16))
        ident = ctx.enter_context(nc.sbuf_tensor("ident", [128, 128], BF16))
        iden_io = ctx.enter_context(nc.sbuf_tensor("iden_io", [128, 128], F32))
        lutrow = ctx.enter_context(nc.sbuf_tensor("lutrow", [1, 16], F32))
        ones1 = ctx.enter_context(nc.sbuf_tensor("ones1", [1, 128], F32))
        lutB = ctx.enter_context(nc.sbuf_tensor("lutB", [128, 16], F32))
        pst = [ctx.enter_context(nc.psum_tensor(f"pst{i}", [128, 128], BF16)) for i in range(4)]
        psb = ctx.enter_context(nc.psum_tensor("psb", [128, 16], F32))

        ld = ctx.enter_context(nc.semaphore("ld"))
        mm = ctx.enter_context(nc.semaphore("mm"))
        ve = ctx.enter_context(nc.semaphore("ve"))
        st = ctx.enter_context(nc.semaphore("st"))

        def gen(eng_name, eng, sch):
            if eng_name == "pool":
                eng.iota(iden_io[:, :], pattern=[[1, 128]], base=0, channel_multiplier=-1,
                         allow_small_or_imprecise_dtypes=True).then_inc(ve, 1)
                eng.dma_start(out=lutrow[:, :], in_=lut[:, :]).then_inc(ld, 16)
            sch.inc("ve"); sch.inc("ld", 16)
            if eng_name == "dve":
                eng.wait_ge(ve, sch.val("ve"))
                eng.memset(ones1[:, :], 1.0)
                eng.tensor_scalar(ident[:, :], iden_io[:, :], 0.0, None, ALU.is_equal).then_inc(ve, 1)
            sch.inc("ve")
            if eng_name == "pe":
                eng.wait_ge(ve, sch.val("ve"))
                eng.wait_ge(ld, sch.val("ld"))
                eng.matmul(psb[:, :], ones1[:, :], lutrow[:, :], start=True, stop=True).then_inc(mm, 1)
            sch.inc("mm")
            if eng_name == "dve":
                eng.wait_ge(mm, sch.val("mm"))
                eng.tensor_copy(lutB[:, :], psb[:, :]).then_inc(ve, 1)
            sch.inc("ve")

            for name, rows, cols, is_ghost in PREP_MATS:
                prow = min(rows, 128)
                ns = ceil_div(rows, 128)
                ic = cols // 128
                for o in range(ns):
                    src = ins[name][o * 128:o * 128 + prow, :]
                    if is_ghost:
                        if eng_name == "pool":
                            eng.wait_ge(ve, sch.val("ve_idx_free"))
                            eng.dma_start(out=i32t[0:prow, 0:cols], in_=src).then_inc(ld, 16)
                        sch.inc("ld", 16)
                        if eng_name == "dve":
                            eng.wait_ge(ld, sch.val("ld"))
                            eng.wait_ge(mm, sch.val("mm"))
                            eng.tensor_copy(idxb[0:prow, 0:cols], i32t[0:prow, 0:cols])
                            eng.memset(wrow[0:prow, 0:cols], 0.0)
                            for l in range(16):
                                eng.tensor_scalar(tmpb[0:prow, 0:cols], idxb[0:prow, 0:cols],
                                                  float(l), lutB[0:prow, l:l + 1],
                                                  ALU.is_equal, ALU.mult)
                                ins_ = eng.tensor_tensor(wrow[0:prow, 0:cols], wrow[0:prow, 0:cols],
                                                         tmpb[0:prow, 0:cols], ALU.add)
                                if l == 15:
                                    ins_.then_inc(ve, 1)
                        sch.inc("ve")
                        sch.set("ve_idx_free", sch.val("ve"))
                        sch.set("wrow_ready_ve", sch.val("ve"))
                    else:
                        if eng_name == "pool":
                            eng.wait_ge(mm, sch.val("mm"))
                            eng.dma_start(out=wrow[0:prow, 0:cols], in_=src).then_inc(ld, 16)
                        sch.inc("ld", 16)

                    for d in range(ic):
                        bank = d % 4
                        if eng_name == "pe":
                            if is_ghost:
                                eng.wait_ge(ve, sch.val("wrow_ready_ve"))
                            else:
                                eng.wait_ge(ld, sch.val("ld"))
                            prev = sch.val(f"bank{bank}_ve")
                            if prev > 0:
                                eng.wait_ge(ve, prev)
                            prevst = sch.val("slab_store_st")
                            if prevst > 0 and d < 4:
                                eng.wait_ge(st, prevst)
                            eng.transpose(pst[bank][:, 0:prow],
                                          wrow[0:prow, d * 128:(d + 1) * 128],
                                          ident[0:prow, 0:prow]).then_inc(mm, 1)
                        sch.inc("mm")
                        if eng_name == "dve":
                            eng.wait_ge(mm, sch.val("mm"))
                            prevst = sch.val("slab_store_st")
                            if prevst > 0:
                                eng.wait_ge(st, prevst)
                            eng.tensor_copy(slab[:, d * prow:(d + 1) * prow],
                                            pst[bank][:, 0:prow]).then_inc(ve, 1)
                        sch.inc("ve")
                        sch.set(f"bank{bank}_ve", sch.val("ve"))
                    if eng_name == "sync":
                        eng.wait_ge(ve, sch.val("ve"))
                        eng.dma_start(out=outs[name][o, :, 0:ic * prow],
                                      in_=slab[:, 0:ic * prow]).then_inc(st, 16)
                    sch.inc("st", 16)
                    sch.set("slab_store_st", sch.val("st"))

            if eng_name == "sync":
                eng.wait_ge(st, sch.val("st"))

        build_blocked(nc, gen)
    return nc


def host_prep_inputs(inp):
    def pad_rows(a, total):
        if a.shape[0] == total:
            return a
        pad = np.zeros((total - a.shape[0],) + a.shape[1:], a.dtype)
        return np.concatenate([a, pad], axis=0)

    full = {
        "gw": pad_rows(np.asarray(inp["L0_ffn_gate_w"]), 6144),
        "uw": pad_rows(np.asarray(inp["L0_ffn_up_w"]), 6144),
        "dw": np.asarray(inp["L0_ffn_down_w"]),
        "bq": np.asarray(inp["L0_attn_b_Wq"]),
        "bk": np.asarray(inp["L0_attn_b_Wk"]),
        "bv": np.asarray(inp["L0_attn_b_Wv"]),
        "bo": np.asarray(inp["L0_attn_b_Wo"]),
        "gg": pad_rows(np.asarray(inp["L1_ffn_gate_idx"]), 6144),
        "gu": pad_rows(np.asarray(inp["L1_ffn_up_idx"]), 6144),
        "gd": np.asarray(inp["L1_ffn_down_idx"]),
        "w1a": np.ascontiguousarray(np.asarray(inp["L1_attn_b_W1"][0]).T),
        "w1b": np.ascontiguousarray(np.asarray(inp["L1_attn_b_W1"][1]).T),
        "w2a": np.ascontiguousarray(np.asarray(inp["L1_attn_b_W2"][0]).T),
        "w2b": np.ascontiguousarray(np.asarray(inp["L1_attn_b_W2"][1]).T),
    }
    lut = np.asarray(inp["L1_ffn_lut"]).reshape(1, 16).astype(np.float32)
    in_maps = []
    for c in range(NCORES):
        m = {"in_lut": lut}
        for name, rows, cols, is_ghost in PREP_MATS:
            a = full[name]
            m["in_" + name] = np.ascontiguousarray(a[c * rows:(c + 1) * rows])
        in_maps.append(m)
    return in_maps


def host_prep_assemble(results):
    tot = {}
    trims = {"gw": 44, "uw": 44, "gg": 43, "gu": 43}
    for name, rows, cols, is_ghost in PREP_MATS:
        parts = [results[c]["out_" + name] for c in range(NCORES)]
        a = np.concatenate(parts, axis=0)
        if name in trims:
            a = a[:trims[name]]
        tot[name] = np.ascontiguousarray(a)
    return tot




def build_ffn(ni, has_scales):
    nc = bass.Bass("TRN2", target_bir_lowering=False, debug=False)
    TW = 512
    NH = 16
    xs = nc.dram_tensor("xs", [D, 1024], F32, kind="ExternalInput").ap()
    gslab = nc.dram_tensor("gslab", [ni, 128, 2048], BF16, kind="ExternalInput").ap()
    uslab = nc.dram_tensor("uslab", [ni, 128, 2048], BF16, kind="ExternalInput").ap()
    dslab = nc.dram_tensor("dslab", [NH, 128, ni * 128], BF16, kind="ExternalInput").ap()
    if has_scales:
        gs_in = nc.dram_tensor("gs", [ni, 128], F32, kind="ExternalInput").ap()
        us_in = nc.dram_tensor("us", [ni, 128], F32, kind="ExternalInput").ap()
        ds_in = nc.dram_tensor("ds", [NH, 128], F32, kind="ExternalInput").ap()
    y = nc.dram_tensor("y", [D, 1024], F32, kind="ExternalOutput").ap()

    ctx = ExitStack()
    with ctx:
        xt = [ctx.enter_context(nc.sbuf_tensor(f"xt{d}", [128, 1024], F32)) for d in range(NH)]
        xn = [ctx.enter_context(nc.sbuf_tensor(f"xn{d}", [128, TW], BF16)) for d in range(NH)]
        h = [ctx.enter_context(nc.sbuf_tensor(f"h{i}", [128, TW], BF16)) for i in range(ni)]
        gbuf = [ctx.enter_context(nc.sbuf_tensor(f"gbuf{p}", [128, 2048], BF16)) for p in range(2)]
        ubuf = [ctx.enter_context(nc.sbuf_tensor(f"ubuf{p}", [128, 2048], BF16)) for p in range(2)]
        dbuf = [ctx.enter_context(nc.sbuf_tensor(f"dbuf{p}", [128, ni * 128], BF16)) for p in range(2)]
        t1 = [ctx.enter_context(nc.sbuf_tensor(f"t1_{p}", [128, TW], F32)) for p in range(2)]
        t2 = [ctx.enter_context(nc.sbuf_tensor(f"t2_{p}", [128, TW], F32)) for p in range(2)]
        yscr = [ctx.enter_context(nc.sbuf_tensor(f"yscr{p}", [128, TW], F32)) for p in range(2)]
        castsq = ctx.enter_context(nc.sbuf_tensor("castsq", [128, 4 * TW], BF16))
        scr = ctx.enter_context(nc.sbuf_tensor("scr", [128, TW], F32))
        row = ctx.enter_context(nc.sbuf_tensor("row", [1, 4 * TW], F32))
        ones_f = ctx.enter_context(nc.sbuf_tensor("ones_f", [1, 128], F32))
        ones_b = ctx.enter_context(nc.sbuf_tensor("ones_b", [128, 1], BF16))
        if has_scales:
            srow = ctx.enter_context(nc.sbuf_tensor("srow", [64, 128], F32))
            identf = ctx.enter_context(nc.sbuf_tensor("identf", [128, 128], F32))
            gsT = ctx.enter_context(nc.sbuf_tensor("gsT", [128, ni], F32))
            usT = ctx.enter_context(nc.sbuf_tensor("usT", [128, ni], F32))
            dsT = ctx.enter_context(nc.sbuf_tensor("dsT", [128, NH], F32))
        iden_io = ctx.enter_context(nc.sbuf_tensor("iden_io", [128, 128], F32))

        ps_stat = ctx.enter_context(nc.psum_tensor("ps_stat", [1, 2 * TW], F32))
        ps_b = ctx.enter_context(nc.psum_tensor("ps_b", [128, 2 * TW], F32))
        psg = [ctx.enter_context(nc.psum_tensor(f"psg{p}", [128, TW], F32)) for p in range(2)]
        psu = [ctx.enter_context(nc.psum_tensor(f"psu{p}", [128, TW], F32)) for p in range(2)]
        psd = psg
        pstr = ps_b

        ld = ctx.enter_context(nc.semaphore("ld"))
        mm = ctx.enter_context(nc.semaphore("mm"))
        ve = ctx.enter_context(nc.semaphore("ve"))
        sc = ctx.enter_context(nc.semaphore("sc"))
        st = ctx.enter_context(nc.semaphore("st"))
        sems = {"ld": ld, "mm": mm, "ve": ve, "sc": sc, "st": st}

        def gen(eng_name, eng, sch):
            if eng_name == "sync":
                for d in range(NH):
                    eng.dma_start(out=xt[d][:, :], in_=xs[d * 128:(d + 1) * 128, :]).then_inc(ld, 16)
            sch.inc("ld", 16 * NH)
            if eng_name == "pool":
                eng.iota(iden_io[:, :], pattern=[[1, 128]], base=0, channel_multiplier=-1,
                         allow_small_or_imprecise_dtypes=True).then_inc(ve, 1)
                if has_scales:
                    eng.dma_start(out=srow[0:ni, 0:128], in_=gs_in[:, :]).then_inc(ld, 16)
            sch.inc("ve")
            if has_scales:
                sch.inc("ld", 16)
            if eng_name == "dve":
                eng.wait_ge(ve, sch.val("ve"))
                eng.memset(ones_f[:, :], 1.0)
                eng.memset(ones_b[:, :], 1.0)
                if has_scales:
                    eng.tensor_scalar(identf[:, :], iden_io[:, :], 0.0, None, ALU.is_equal).then_inc(ve, 1)
                else:
                    eng.memset(ones_f[:, :], 1.0).then_inc(ve, 1)
            sch.inc("ve")
            if has_scales:
                for which in range(3):
                    nrow = [ni, ni, NH][which]
                    src = [gs_in, us_in, ds_in][which]
                    if which > 0:
                        if eng_name == "pool":
                            eng.wait_ge(mm, sch.val("mm"))
                            eng.dma_start(out=srow[0:nrow, 0:128], in_=src[:, :]).then_inc(ld, 16)
                        sch.inc("ld", 16)
                    if eng_name == "pe":
                        eng.wait_ge(ld, sch.val("ld"))
                        eng.wait_ge(ve, sch.val("ve"))
                        if which > 0:
                            eng.wait_ge(ve, sch.val("pstr_ve"))
                        eng.transpose(pstr[0:128, 0:nrow], srow[0:nrow, 0:128], identf[0:nrow, 0:nrow]).then_inc(mm, 1)
                    sch.inc("mm")
                    if eng_name == "dve":
                        eng.wait_ge(mm, sch.val("mm"))
                        dst = [gsT, usT, dsT][which]
                        eng.tensor_copy(dst[:, 0:nrow], pstr[:, 0:nrow]).then_inc(ve, 1)
                    sch.inc("ve")
                    sch.set("pstr_ve", sch.val("ve"))

            for sp in range(2):
                T0 = sp * TW
                if sp > 0:
                    pass
                for d in range(NH):
                    par = d % 2
                    if eng_name == "dve":
                        if d == 0:
                            eng.wait_ge(ld, 16 * NH)
                        g = sch.val(f"lncast{par}_mm")
                        if g > 0:
                            eng.wait_ge(mm, g)
                        base = par * 2 * TW
                        eng.tensor_copy(castsq[:, base:base + TW], xt[d][:, T0:T0 + TW])
                        eng.tensor_tensor(castsq[:, base + TW:base + 2 * TW], xt[d][:, T0:T0 + TW],
                                          xt[d][:, T0:T0 + TW], ALU.mult).then_inc(ve, 1)
                    sch.inc("ve")
                    if eng_name == "pe":
                        eng.wait_ge(ve, sch.val("ve"))
                        if d == 0:
                            g = sch.val("ps_stat_free_ve")
                            if g > 0:
                                eng.wait_ge(ve, g)
                        base = par * 2 * TW
                        eng.matmul(ps_stat[0:1, 0:TW], ones_b[:, :], castsq[:, base:base + TW],
                                   start=(d == 0), stop=(d == NH - 1))
                        eng.matmul(ps_stat[0:1, TW:2 * TW], ones_b[:, :],
                                   castsq[:, base + TW:base + 2 * TW],
                                   start=(d == 0), stop=(d == NH - 1)).then_inc(mm, 1)
                    sch.inc("mm")
                    sch.set(f"lncast{par}_mm", sch.val("mm"))
                if eng_name == "dve":
                    eng.wait_ge(mm, sch.val("mm"))
                    eng.tensor_scalar(row[0:1, 0:TW], ps_stat[0:1, 0:TW], 1.0 / D, None, ALU.mult)
                    eng.tensor_scalar(row[0:1, TW:2 * TW], ps_stat[0:1, TW:2 * TW], 1.0 / D, None, ALU.mult)
                    eng.tensor_tensor(row[0:1, 2 * TW:3 * TW], row[0:1, 0:TW], row[0:1, 0:TW], ALU.mult)
                    eng.tensor_tensor(row[0:1, TW:2 * TW], row[0:1, TW:2 * TW],
                                      row[0:1, 2 * TW:3 * TW], ALU.subtract)
                    eng.tensor_scalar(row[0:1, TW:2 * TW], row[0:1, TW:2 * TW], 1e-5, None,
                                      ALU.add).then_inc(ve, 1)
                sch.inc("ve")
                sch.set("ps_stat_free_ve", sch.val("ve"))
                if eng_name == "act":
                    eng.wait_ge(ve, sch.val("ve"))
                    eng.activation(row[0:1, 2 * TW:3 * TW], row[0:1, TW:2 * TW], AF.Sqrt).then_inc(sc, 1)
                sch.inc("sc")
                if eng_name == "dve":
                    eng.wait_ge(sc, sch.val("sc"))
                    eng.reciprocal(row[0:1, 3 * TW:4 * TW], row[0:1, 2 * TW:3 * TW]).then_inc(ve, 1)
                sch.inc("ve")
                if eng_name == "pe":
                    eng.wait_ge(ve, sch.val("ve"))
                    g = sch.val("ps_b_free_ve")
                    if g > 0:
                        eng.wait_ge(ve, g)
                    eng.matmul(ps_b[:, 0:TW], ones_f[:, :], row[0:1, 0:TW], start=True, stop=True)
                    eng.matmul(ps_b[:, TW:2 * TW], ones_f[:, :], row[0:1, 3 * TW:4 * TW],
                               start=True, stop=True).then_inc(mm, 1)
                sch.inc("mm")
                for d in range(NH):
                    if eng_name == "dve":
                        if d == 0:
                            eng.wait_ge(mm, sch.val("mm"))
                            g = sch.val("xn_free_mm")
                            if g > 0:
                                eng.wait_ge(mm, g)
                        eng.tensor_tensor(scr[:, :], xt[d][:, T0:T0 + TW], ps_b[:, 0:TW], ALU.subtract)
                        ii = eng.tensor_tensor(xn[d][:, :], scr[:, :], ps_b[:, TW:2 * TW], ALU.mult)
                        if d == NH - 1:
                            ii.then_inc(ve, 1)
                sch.inc("ve")
                sch.set("ps_b_free_ve", sch.val("ve"))
                xn_ready_ve = sch.val("ve")

                for oi in range(ni):
                    par = oi % 2
                    if eng_name == "pool":
                        g = sch.val(f"gbuf{par}_mm")
                        if g > 0:
                            eng.wait_ge(mm, g)
                        eng.dma_start(out=gbuf[par][:, :], in_=gslab[oi]).then_inc(ld, 16)
                        eng.dma_start(out=ubuf[par][:, :], in_=uslab[oi]).then_inc(ld, 16)
                    sch.inc("ld", 32)
                    if eng_name == "pe":
                        eng.wait_ge(ld, sch.val("ld"))
                        if oi == 0:
                            eng.wait_ge(ve, xn_ready_ve)
                        gg = sch.val(f"psg{par}_sc")
                        if gg > 0:
                            eng.wait_ge(sc, gg)
                        gg = sch.val(f"psd{par}_ve")
                        if gg > 0:
                            eng.wait_ge(ve, gg)
                        gg = sch.val(f"psu{par}_ve")
                        if gg > 0:
                            eng.wait_ge(ve, gg)
                        for d in range(NH):
                            eng.matmul(psg[par][:, :], gbuf[par][:, d * 128:(d + 1) * 128],
                                       xn[d][:, :], start=(d == 0), stop=(d == NH - 1))
                        for d in range(NH):
                            ii = eng.matmul(psu[par][:, :], ubuf[par][:, d * 128:(d + 1) * 128],
                                            xn[d][:, :], start=(d == 0), stop=(d == NH - 1))
                        ii.then_inc(mm, 1)
                    sch.inc("mm")
                    sch.set(f"gbuf{par}_mm", sch.val("mm"))
                    if eng_name == "act":
                        eng.wait_ge(mm, sch.val("mm"))
                        g = sch.val(f"t1_{par}_ve")
                        if g > 0:
                            eng.wait_ge(ve, g)
                        if has_scales:
                            eng.activation(t1[par][:, :], psg[par][:, :], AF.Silu,
                                           scale=gsT[:, oi:oi + 1]).then_inc(sc, 1)
                        else:
                            eng.activation(t1[par][:, :], psg[par][:, :], AF.Silu).then_inc(sc, 1)
                    sch.inc("sc")
                    sch.set(f"psg{par}_sc", sch.val("sc"))
                    if eng_name == "dve":
                        eng.wait_ge(sc, sch.val("sc"))
                        if has_scales:
                            eng.tensor_scalar(t2[par][:, :], psu[par][:, :], usT[:, oi:oi + 1],
                                              None, ALU.mult)
                            eng.tensor_tensor(h[oi][:, :], t1[par][:, :], t2[par][:, :],
                                              ALU.mult).then_inc(ve, 1)
                        else:
                            eng.tensor_tensor(h[oi][:, :], t1[par][:, :], psu[par][:, :],
                                              ALU.mult).then_inc(ve, 1)
                    sch.inc("ve")
                    sch.set(f"psu{par}_ve", sch.val("ve"))
                    sch.set(f"t1_{par}_ve", sch.val("ve"))
                h_ready_ve = sch.val("ve")

                for oo in range(NH):
                    par = oo % 2
                    if eng_name == "pool":
                        g = sch.val(f"dbuf{par}_mm")
                        if g > 0:
                            eng.wait_ge(mm, g)
                        eng.dma_start(out=dbuf[par][:, :], in_=dslab[oo]).then_inc(ld, 16)
                    sch.inc("ld", 16)
                    if eng_name == "pe":
                        eng.wait_ge(ld, sch.val("ld"))
                        if oo == 0:
                            eng.wait_ge(ve, h_ready_ve)
                        g = sch.val(f"psd{par}_ve")
                        if g > 0:
                            eng.wait_ge(ve, g)
                        g = sch.val(f"psg{par}_sc")
                        if g > 0:
                            eng.wait_ge(sc, g)
                        for i in range(ni):
                            ii = eng.matmul(psd[par][:, :], dbuf[par][:, i * 128:(i + 1) * 128],
                                            h[i][:, :], start=(i == 0), stop=(i == ni - 1))
                        ii.then_inc(mm, 1)
                    sch.inc("mm")
                    sch.set(f"dbuf{par}_mm", sch.val("mm"))
                    if eng_name == "dve":
                        eng.wait_ge(mm, sch.val("mm"))
                        g = sch.val(f"yscr{par}_st")
                        if g > 0:
                            eng.wait_ge(st, g)
                        if has_scales:
                            eng.scalar_tensor_tensor(yscr[par][:, :], psd[par][:, :],
                                                     dsT[:, oo:oo + 1], xt[oo][:, T0:T0 + TW],
                                                     ALU.mult, ALU.add).then_inc(ve, 1)
                        else:
                            eng.tensor_tensor(yscr[par][:, :], psd[par][:, :],
                                              xt[oo][:, T0:T0 + TW], ALU.add).then_inc(ve, 1)
                    sch.inc("ve")
                    sch.set(f"psd{par}_ve", sch.val("ve"))
                    if eng_name == "sync":
                        eng.wait_ge(ve, sch.val("ve"))
                        eng.dma_start(out=y[oo * 128:(oo + 1) * 128, T0:T0 + TW],
                                      in_=yscr[par][:, :]).then_inc(st, 16)
                    sch.inc("st", 16)
                    sch.set(f"yscr{par}_st", sch.val("st"))
                sch.set("xn_free_mm", sch.val("mm"))

            if eng_name == "sync":
                eng.wait_ge(st, sch.val("st"))

        build_blocked(nc, gen)
    return nc



class LNKit:

    _n = 0

    def __init__(self, nc, ctx, tw=512, tag=None, banks=None, dmodel=D, gb=False):
        LNKit._n += 1
        tag = tag or f"ln{LNKit._n}"
        self.tw = tw
        self.dmodel = dmodel
        self.castsq = ctx.enter_context(nc.sbuf_tensor(f"{tag}_castsq", [128, 4 * tw], BF16))
        self.row = ctx.enter_context(nc.sbuf_tensor(f"{tag}_row", [1, 4 * tw], F32))
        self.scr = ctx.enter_context(nc.sbuf_tensor(f"{tag}_scr", [128, tw], F32))
        self.scr4 = (ctx.enter_context(nc.sbuf_tensor(f"{tag}_scr4", [128, 4 * tw], F32))
                     if gb else None)
        self.ones_f = ctx.enter_context(nc.sbuf_tensor(f"{tag}_onesf", [1, 128], F32))
        self.ones_b = ctx.enter_context(nc.sbuf_tensor(f"{tag}_onesb", [128, 1], BF16))
        if banks is None:
            b0 = ctx.enter_context(nc.psum_tensor(f"{tag}_pb0", [128, tw], F32))
            b1 = ctx.enter_context(nc.psum_tensor(f"{tag}_pb1", [128, tw], F32))
            b2 = ctx.enter_context(nc.psum_tensor(f"{tag}_pb2", [128, tw], F32))
            b3 = ctx.enter_context(nc.psum_tensor(f"{tag}_pb3", [128, tw], F32))
            banks = [b0, b1, b2, b3]
        self.banks = banks
        self.tag = tag

    def emit_init(self, eng_name, eng, sch):
        if eng_name == "dve":
            eng.memset(self.ones_f[:, :], 1.0)
            eng.memset(self.ones_b[:, :], 1.0).then_inc(sch.sems["ve"], 1)
        sch.inc("ve")
        sch.set(self.tag + "_init_ve", sch.val("ve"))

    def emit(self, eng_name, eng, sch, xt, t0, xnt, ot0, tw=None, xt_ready=None,
             xn_free_mm=None, gamma_col=None, beta_col=None, nch=16, chunk_cb=None,
             upto=99):
        TW = tw or self.tw
        mm, ve, sc = sch.sems["mm"], sch.sems["ve"], sch.sems["sc"]
        tag = self.tag
        K128 = min(128, xt[0].shape[0]) if hasattr(xt[0], "shape") else 128
        NP = xt[0].shape[0] if hasattr(xt[0], "shape") else 128
        for d in range(nch):
            par = d % 2
            if eng_name == "dve":
                if d == 0:
                    eng.wait_ge(ve, sch.val(tag + "_init_ve"))
                    if xt_ready is not None:
                        eng.wait_ge(sch.sems[xt_ready[0]], xt_ready[1])
                    g = sch.val(tag + "_stat_free_ve")
                g = sch.val(tag + f"_cast{par}_mm")
                if g > 0:
                    eng.wait_ge(mm, g)
                base = par * 2 * TW
                eng.tensor_copy(self.castsq[:, base:base + TW], xt[d][:, t0:t0 + TW])
                eng.tensor_tensor(self.castsq[:, base + TW:base + 2 * TW], xt[d][:, t0:t0 + TW],
                                  xt[d][:, t0:t0 + TW], ALU.mult).then_inc(ve, 1)
            sch.inc("ve")
            if eng_name == "pe":
                eng.wait_ge(ve, sch.val("ve"))
                if d == 0:
                    g = sch.val(tag + "_stat_free_sc")
                    if g > 0:
                        eng.wait_ge(sc, g)
                base = par * 2 * TW
                eng.matmul(self.banks[0][0:1, 0:TW], self.ones_b[0:K128, :],
                           self.castsq[0:K128, base:base + TW], start=(d == 0), stop=(d == nch - 1))
                eng.matmul(self.banks[1][0:1, 0:TW], self.ones_b[0:K128, :],
                           self.castsq[0:K128, base + TW:base + 2 * TW],
                           start=(d == 0), stop=(d == nch - 1)).then_inc(mm, 1)
            sch.inc("mm")
            sch.set(tag + f"_cast{par}_mm", sch.val("mm"))
        if upto <= 1:
            return sch.val("ve")
        row = self.row
        if eng_name == "act":
            eng.wait_ge(mm, sch.val("mm"))
            eng.activation(row[0:1, 0:TW], self.banks[0][0:1, 0:TW], AF.Copy,
                           scale=1.0 / self.dmodel)
            eng.activation(row[0:1, TW:2 * TW], self.banks[1][0:1, 0:TW], AF.Copy,
                           scale=1.0 / self.dmodel).then_inc(sc, 1)
        sch.inc("sc")
        sch.set(tag + "_stat_free_ve", 0)
        sch.set(tag + "_stat_free_sc", sch.val("sc"))
        if eng_name == "dve":
            eng.wait_ge(sc, sch.val("sc"))
            eng.tensor_tensor(row[0:1, 2 * TW:3 * TW], row[0:1, 0:TW], row[0:1, 0:TW], ALU.mult)
            eng.tensor_tensor(row[0:1, TW:2 * TW], row[0:1, TW:2 * TW],
                              row[0:1, 2 * TW:3 * TW], ALU.subtract)
            eng.tensor_scalar(row[0:1, TW:2 * TW], row[0:1, TW:2 * TW], 1e-5, None,
                              ALU.add).then_inc(ve, 1)
        sch.inc("ve")
        if upto <= 2:
            return sch.val("ve")
        if eng_name == "act":
            eng.wait_ge(ve, sch.val("ve"))
            eng.activation(row[0:1, 2 * TW:3 * TW], row[0:1, TW:2 * TW], AF.Sqrt).then_inc(sc, 1)
        sch.inc("sc")
        if eng_name == "dve":
            eng.wait_ge(sc, sch.val("sc"))
            eng.reciprocal(row[0:1, 3 * TW:4 * TW], row[0:1, 2 * TW:3 * TW]).then_inc(ve, 1)
        sch.inc("ve")
        if upto <= 3:
            return sch.val("ve")
        if eng_name == "pe":
            eng.wait_ge(ve, sch.val("ve"))
            g = sch.val(tag + "_psb_free_ve")
            if g > 0:
                eng.wait_ge(ve, g)
            eng.matmul(self.banks[2][0:NP, 0:TW], self.ones_f[0:1, 0:NP], row[0:1, 0:TW],
                       start=True, stop=True)
            eng.matmul(self.banks[3][0:NP, 0:TW], self.ones_f[0:1, 0:NP], row[0:1, 3 * TW:4 * TW],
                       start=True, stop=True).then_inc(mm, 1)
        sch.inc("mm")
        L = len(xnt)
        if gamma_col is None:
            for d in range(nch):
                dd = d % L
                if eng_name == "dve":
                    if d == 0:
                        eng.wait_ge(mm, sch.val("mm"))
                        if xn_free_mm is not None and xn_free_mm > 0:
                            eng.wait_ge(mm, xn_free_mm)
                    g = sch.val(tag + f"_xnb{dd}_free")
                    if g > 0 and L < nch:
                        eng.wait_ge(sch.sems["st"], g)
                    eng.tensor_tensor(self.scr[0:NP, 0:TW], xt[d][:, t0:t0 + TW],
                                      self.banks[2][0:NP, 0:TW], ALU.subtract)
                    ii = eng.tensor_tensor(xnt[dd][:, ot0:ot0 + TW], self.scr[0:NP, 0:TW],
                                           self.banks[3][0:NP, 0:TW], ALU.mult)
                    ii.then_inc(ve, 1)
                sch.inc("ve")
                if chunk_cb is not None:
                    chunk_cb(eng_name, eng, sch, d, dd)
        else:
            def sA(par):
                return self.scr4[0:NP, par * TW:(par + 1) * TW]

            def sB(par):
                return self.scr4[0:NP, (2 + par) * TW:(3 + par) * TW]

            def op3(d2):
                par2 = d2 % 2
                dd2 = d2 % L
                return eng.scalar_tensor_tensor(
                    xnt[dd2][:, ot0:ot0 + TW], sB(par2), gamma_col[d2],
                    beta_col[d2].to_broadcast([NP, TW]), ALU.mult, ALU.add)

            for d in range(nch):
                par = d % 2
                if eng_name == "dve":
                    if d == 0:
                        eng.wait_ge(mm, sch.val("mm"))
                        if xn_free_mm is not None and xn_free_mm > 0:
                            eng.wait_ge(mm, xn_free_mm)
                    eng.tensor_tensor(sA(par), xt[d][:, t0:t0 + TW],
                                      self.banks[2][0:NP, 0:TW], ALU.subtract)
                    eng.tensor_tensor(sB(par), sA(par), self.banks[3][0:NP, 0:TW], ALU.mult)
                    if d > 0:
                        ii = op3(d - 1)
                        if d == nch - 1:
                            pass
                if d > 0:
                    sch.inc("ve") if False else None
            if eng_name == "dve":
                op3(nch - 1).then_inc(ve, 1)
            sch.inc("ve")
        sch.set(tag + "_psb_free_ve", sch.val("ve"))
        return sch.val("ve")


def build_s5():
    nc = bass.Bass("TRN2", target_bir_lowering=False, debug=False)
    NH = 16
    xs = nc.dram_tensor("xs", [D, 1024], F32, kind="ExternalInput").ap()
    xa = nc.dram_tensor("xa", [D, S], F32, kind="ExternalInput").ap()
    wq = nc.dram_tensor("wq", [D, D], F32, kind="ExternalInput").ap()
    wk = nc.dram_tensor("wk", [D, D], F32, kind="ExternalInput").ap()
    wv = nc.dram_tensor("wv", [D, D], F32, kind="ExternalInput").ap()
    wo = nc.dram_tensor("wo", [D, D], F32, kind="ExternalInput").ap()
    bo_in = nc.dram_tensor("bo", [NH, 128], F32, kind="ExternalInput").ap()
    y = nc.dram_tensor("y", [D, 1024], F32, kind="ExternalOutput").ap()
    xnall = nc.dram_tensor("xnall", [D, S], BF16)
    kscr = nc.dram_tensor("kscr", [S, D], BF16)
    vscr = nc.dram_tensor("vscr", [S, D], BF16)

    ctx = ExitStack()
    with ctx:
        AR = 49152
        arena = ctx.enter_context(nc.sbuf_tensor("arena", [128, AR], BF16))
        arena_f = arena.ap().bitcast(F32)
        xn_own = [ctx.enter_context(nc.sbuf_tensor(f"xno{d}", [128, 1024], BF16)) for d in range(NH)]
        q_fm = [ctx.enter_context(nc.sbuf_tensor(f"qfm{d}", [128, 1024], BF16)) for d in range(NH)]
        attn = xn_own
        kv_sb = ctx.enter_context(nc.sbuf_tensor("kv_sb", [128, 1040], F32))
        kv_bf = ctx.enter_context(nc.sbuf_tensor("kv_bf", [128, 1040], BF16))
        rcpb_sb = ctx.enter_context(nc.sbuf_tensor("rcpb_sb", [64, 512], F32))
        rcrow = ctx.enter_context(nc.sbuf_tensor("rcrow", [1, 512], F32))
        xres = [ctx.enter_context(nc.sbuf_tensor(f"xres{p}", [128, 1024], F32)) for p in range(2)]
        yscr = [ctx.enter_context(nc.sbuf_tensor(f"yscr{p}", [128, 512], F32)) for p in range(2)]
        kvout = [ctx.enter_context(nc.sbuf_tensor(f"kvout{p}", [128, 512], BF16)) for p in range(2)]
        boT = ctx.enter_context(nc.sbuf_tensor("boT", [128, NH], F32))
        srow = ctx.enter_context(nc.sbuf_tensor("srow", [NH, 128], F32))
        identf = ctx.enter_context(nc.sbuf_tensor("identf", [128, 128], F32))
        iden_io = ctx.enter_context(nc.sbuf_tensor("iden_io", [128, 128], F32))
        ln = LNKit(nc, ctx, tw=512)

        pp = [ctx.enter_context(nc.psum_tensor(f"pp{p}", [128, 512], F32)) for p in range(2)]
        kv_ps = ctx.enter_context(nc.psum_tensor("kv_ps", [128, 520], F32))

        ld = ctx.enter_context(nc.semaphore("ld"))
        mm = ctx.enter_context(nc.semaphore("mm"))
        ve = ctx.enter_context(nc.semaphore("ve"))
        sc = ctx.enter_context(nc.semaphore("sc"))
        st = ctx.enter_context(nc.semaphore("st"))
        sems = {"ld": ld, "mm": mm, "ve": ve, "sc": sc, "st": st}

        def wtile(d):
            return arena[:, d * 2048:(d + 1) * 2048]

        XS_OFF = 32768
        def xsup(d):
            return arena_f[:, XS_OFF // 2 + d * 512: XS_OFF // 2 + (d + 1) * 512]

        def xnsup(d):
            return arena[:, XS_OFF + d * 512: XS_OFF + (d + 1) * 512]

        def kouts(p):
            return arena[:, XS_OFF + 8192 + p * 512: XS_OFF + 8192 + (p + 1) * 512]

        def ksup(tt):
            return arena[:, XS_OFF + tt * 2048: XS_OFF + (tt + 1) * 2048]

        def vsup(tt):
            return arena[:, XS_OFF + 8192 + tt * 2048: XS_OFF + 8192 + (tt + 1) * 2048]

        def barrier(eng_name, eng, sch):
            for nm in ("mm", "ve", "sc", "ld", "st"):
                v = sch.val(nm)
                if v > 0:
                    eng.wait_ge(sems[nm], v)

        def kvslot(hh):
            return (64 * (hh % 2), (hh // 2) * 65)

        def gen(eng_name, eng, sch):
            ln.emit_init(eng_name, eng, sch)
            if eng_name == "pool":
                eng.iota(iden_io[:, :], pattern=[[1, 128]], base=0, channel_multiplier=-1,
                         allow_small_or_imprecise_dtypes=True)
                eng.dma_start(out=srow[:, :], in_=bo_in[:, :]).then_inc(ld, 16)
            sch.inc("ld", 16)
            if eng_name == "dve":
                eng.wait_ge(ld, sch.val("ld"))
                eng.tensor_scalar(identf[:, :], iden_io[:, :], 0.0, None, ALU.is_equal)
                eng.memset(kv_sb[:, :], 0.0).then_inc(ve, 1)
            sch.inc("ve")
            if eng_name == "pe":
                eng.wait_ge(ve, sch.val("ve"))
                eng.transpose(pp[0][0:128, 0:NH], srow[:, :], identf[0:NH, 0:NH]).then_inc(mm, 1)
            sch.inc("mm")
            if eng_name == "dve":
                eng.wait_ge(mm, sch.val("mm"))
                eng.tensor_copy(boT[:, :], pp[0][0:128, 0:NH]).then_inc(ve, 1)
            sch.inc("ve")

            for sp2 in range(2):
                if eng_name == "sync":
                    barrier(eng_name, eng, sch)
                    for d in range(NH):
                        eng.dma_start(out=xsup(d), in_=xs[d * 128:(d + 1) * 128,
                                                         sp2 * 512:(sp2 + 1) * 512]).then_inc(st, 16)
                sch.inc("st", 16 * NH)
                ln.emit(eng_name, eng, sch, [xsup(d) for d in range(NH)], 0,
                        xn_own, sp2 * 512, xt_ready=("st", sch.val("st")))
                for e in ENGS:
                    pass
                barrier(eng_name, eng, sch)

            if eng_name == "pool":
                for d in range(NH):
                    eng.dma_start(out=wtile(d), in_=wq[d * 128:(d + 1) * 128, :]).then_inc(ld, 16)
            sch.inc("ld", 16 * NH)
            for o in range(NH):
                for sp2 in range(2):
                    par = (o * 2 + sp2) % 2
                    if eng_name == "pe":
                        if o == 0 and sp2 == 0:
                            eng.wait_ge(ld, sch.val("ld"))
                        g = sch.val(f"pp{par}_sc")
                        if g > 0:
                            eng.wait_ge(sc, g)
                        for d in range(NH):
                            ii = eng.matmul(pp[par][:, :], wtile(d)[:, o * 128:(o + 1) * 128],
                                            xn_own[d][:, sp2 * 512:(sp2 + 1) * 512],
                                            start=(d == 0), stop=(d == NH - 1))
                        ii.then_inc(mm, 1)
                    sch.inc("mm")
                    if eng_name == "act":
                        eng.wait_ge(mm, sch.val("mm"))
                        eng.activation(q_fm[o][:, sp2 * 512:(sp2 + 1) * 512], pp[par][:, :],
                                       AF.Relu).then_inc(sc, 1)
                    sch.inc("sc")
                    sch.set(f"pp{par}_sc", sch.val("sc"))
            barrier(eng_name, eng, sch)

            for sup in range(16):
                if eng_name == "sync":
                    barrier(eng_name, eng, sch)
                    for d in range(NH):
                        eng.dma_start(out=xsup(d), in_=xa[d * 128:(d + 1) * 128,
                                                         sup * 512:(sup + 1) * 512]).then_inc(st, 16)
                sch.inc("st", 16 * NH)

                def cb(egn, eg, sh, d, dd, _sup=sup):
                    if egn == "sync":
                        eg.wait_ge(ve, sh.val("ve"))
                        eg.dma_start(out=xnall[d * 128:(d + 1) * 128,
                                               _sup * 512:(_sup + 1) * 512],
                                     in_=kouts(dd)).then_inc(st, 16)
                    sh.inc("st", 16)
                    sh.set(ln.tag + f"_xnb{dd}_free", sh.val("st"))

                ln.emit(eng_name, eng, sch, [xsup(d) for d in range(NH)], 0,
                        [kouts(0), kouts(1)], 0, xt_ready=("st", sch.val("st")), chunk_cb=cb)
                barrier(eng_name, eng, sch)

            for which in range(2):
                wsrc = wk if which == 0 else wv
                dst = kscr if which == 0 else vscr
                if eng_name == "pool":
                    barrier(eng_name, eng, sch)
                    for d in range(NH):
                        eng.dma_start(out=wtile(d), in_=wsrc[d * 128:(d + 1) * 128, :]).then_inc(ld, 16)
                sch.inc("ld", 16 * NH)
                for sup in range(16):
                    if eng_name == "sync":
                        if sup > 0:
                            eng.wait_ge(mm, sch.val(f"xnsup_mm"))
                        for d in range(NH):
                            eng.dma_start(out=xnsup(d),
                                          in_=xnall[d * 128:(d + 1) * 128,
                                                    sup * 512:(sup + 1) * 512]).then_inc(st, 16)
                    sch.inc("st", 16 * NH)
                    for tt in range(4):
                        for oc in range(4):
                            par = (tt * 4 + oc) % 2
                            if eng_name == "pe":
                                if tt == 0 and oc == 0:
                                    eng.wait_ge(st, sch.val("st"))
                                    if sup == 0:
                                        eng.wait_ge(ld, sch.val("ld"))
                                g = sch.val(f"pp{par}_sc")
                                if g > 0:
                                    eng.wait_ge(sc, g)
                                for d in range(NH):
                                    ii = eng.matmul(pp[par][:, :],
                                                    xnsup(d)[:, tt * 128:(tt + 1) * 128],
                                                    wtile(d)[:, oc * 512:(oc + 1) * 512],
                                                    start=(d == 0), stop=(d == NH - 1))
                                ii.then_inc(mm, 1)
                            sch.inc("mm")
                            if eng_name == "act":
                                eng.wait_ge(mm, sch.val("mm"))
                                g = sch.val(f"ko{par}_st")
                                if g > 0:
                                    eng.wait_ge(st, g)
                                if which == 0:
                                    eng.activation(kouts(par), pp[par][:, :], AF.Relu).then_inc(sc, 1)
                                else:
                                    eng.copy(kouts(par), pp[par][:, :]).then_inc(sc, 1)
                            sch.inc("sc")
                            sch.set(f"pp{par}_sc", sch.val("sc"))
                            if eng_name == "sync":
                                eng.wait_ge(sc, sch.val("sc"))
                                eng.dma_start(out=dst[sup * 512 + tt * 128: sup * 512 + (tt + 1) * 128,
                                                      oc * 512:(oc + 1) * 512],
                                              in_=kouts(par)).then_inc(st, 16)
                            sch.inc("st", 16)
                            sch.set(f"ko{par}_st", sch.val("st"))
                    sch.set("xnsup_mm", sch.val("mm"))
                barrier(eng_name, eng, sch)

            for sup in range(16):
                if eng_name == "sync":
                    if sup > 0:
                        eng.wait_ge(mm, sch.val("kvsup_mm"))
                    for tt in range(4):
                        eng.dma_start(out=ksup(tt),
                                      in_=kscr[sup * 512 + tt * 128: sup * 512 + (tt + 1) * 128, :]
                                      ).then_inc(st, 16)
                        eng.dma_start(out=vsup(tt),
                                      in_=vscr[sup * 512 + tt * 128: sup * 512 + (tt + 1) * 128, :]
                                      ).then_inc(st, 16)
                sch.inc("st", 16 * 8)
                for half in range(2):
                    if eng_name == "pe":
                        if half == 0:
                            eng.wait_ge(st, sch.val("st"))
                        g = sch.val("kvps_ve")
                        if g > 0:
                            eng.wait_ge(ve, g)
                        for hh in range(16):
                            h = half * 16 + hh
                            rb, cb0 = kvslot(hh)
                            for tt in range(4):
                                eng.matmul(kv_ps[rb:rb + 64, cb0:cb0 + 64],
                                           ksup(tt)[:, h * 64:(h + 1) * 64],
                                           vsup(tt)[:, h * 64:(h + 1) * 64],
                                           start=(tt == 0), stop=False)
                                ii = eng.matmul(kv_ps[rb:rb + 64, cb0 + 64:cb0 + 65],
                                                ksup(tt)[:, h * 64:(h + 1) * 64],
                                                ln.ones_b[:, :],
                                                start=False, stop=(tt == 3))
                        ii.then_inc(mm, 1)
                    sch.inc("mm")
                    if eng_name == "dve":
                        eng.wait_ge(mm, sch.val("mm"))
                        eng.tensor_tensor(kv_sb[:, half * 520:(half + 1) * 520],
                                          kv_sb[:, half * 520:(half + 1) * 520],
                                          kv_ps[:, 0:520], ALU.add).then_inc(ve, 1)
                    sch.inc("ve")
                    sch.set("kvps_ve", sch.val("ve"))
                sch.set("kvsup_mm", sch.val("mm"))
            if eng_name == "dve":
                eng.tensor_copy(kv_bf[:, :], kv_sb[:, :]).then_inc(ve, 1)
            sch.inc("ve")
            barrier(eng_name, eng, sch)

            for h in range(32):
                half, hh = divmod(h, 16)
                rb = 64 * (h % 2)
                cb0 = half * 520 + (hh // 2) * 65
                for sp2 in range(2):
                    if eng_name == "pe":
                        g = sch.val("attnps_ve")
                        if g > 0:
                            eng.wait_ge(ve, g)
                        eng.matmul(pp[0][0:64, :], kv_bf[rb:rb + 64, cb0:cb0 + 64],
                                   q_fm[h // 2][rb:rb + 64, sp2 * 512:(sp2 + 1) * 512],
                                   start=True, stop=True)
                        eng.matmul(kv_ps[0:1, 0:512], kv_bf[rb:rb + 64, cb0 + 64:cb0 + 65],
                                   q_fm[h // 2][rb:rb + 64, sp2 * 512:(sp2 + 1) * 512],
                                   start=True, stop=True).then_inc(mm, 1)
                    sch.inc("mm")
                    if eng_name == "dve":
                        eng.wait_ge(mm, sch.val("mm"))
                        eng.tensor_scalar(rcrow[0:1, :], kv_ps[0:1, 0:512], 1e-6, None, ALU.max)
                        eng.reciprocal(rcrow[0:1, :], rcrow[0:1, :]).then_inc(ve, 1)
                    sch.inc("ve")
                    if eng_name == "pe":
                        eng.wait_ge(ve, sch.val("ve"))
                        eng.matmul(pp[1][0:64, :], ln.ones_f[0:1, 0:64], rcrow[0:1, :],
                                   start=True, stop=True).then_inc(mm, 1)
                    sch.inc("mm")
                    if eng_name == "dve":
                        eng.wait_ge(mm, sch.val("mm"))
                        eng.tensor_copy(rcpb_sb[:, :], pp[1][0:64, :])
                        eng.tensor_tensor(attn[h // 2][rb:rb + 64, sp2 * 512:(sp2 + 1) * 512],
                                          pp[0][0:64, :], rcpb_sb[:, :], ALU.mult).then_inc(ve, 1)
                    sch.inc("ve")
                    sch.set("attnps_ve", sch.val("ve"))
            barrier(eng_name, eng, sch)

            if eng_name == "pool":
                for d in range(NH):
                    eng.dma_start(out=wtile(d), in_=wo[d * 128:(d + 1) * 128, :]).then_inc(ld, 16)
            sch.inc("ld", 16 * NH)
            for o in range(NH):
                if eng_name == "sync":
                    g = sch.val(f"xres{o % 2}_ve")
                    if g > 0:
                        eng.wait_ge(ve, g)
                    eng.dma_start(out=xres[o % 2][:, :],
                                  in_=xs[o * 128:(o + 1) * 128, :]).then_inc(st, 16)
                sch.inc("st", 16)
                for sp2 in range(2):
                    par = sp2
                    if eng_name == "pe":
                        if o == 0 and sp2 == 0:
                            eng.wait_ge(ld, sch.val("ld"))
                        g = sch.val(f"yps{par}_ve")
                        if g > 0:
                            eng.wait_ge(ve, g)
                        for d in range(NH):
                            ii = eng.matmul(pp[par][:, :], wtile(d)[:, o * 128:(o + 1) * 128],
                                            attn[d][:, sp2 * 512:(sp2 + 1) * 512],
                                            start=(d == 0), stop=(d == NH - 1))
                        ii.then_inc(mm, 1)
                    sch.inc("mm")
                    if eng_name == "dve":
                        eng.wait_ge(mm, sch.val("mm"))
                        eng.wait_ge(st, sch.val("st"))
                        g = sch.val(f"ysc{par}_st")
                        if g > 0:
                            eng.wait_ge(st, g)
                        eng.scalar_tensor_tensor(yscr[par][:, :], pp[par][:, :], boT[:, o:o + 1],
                                                 xres[o % 2][:, sp2 * 512:(sp2 + 1) * 512],
                                                 ALU.add, ALU.add).then_inc(ve, 1)
                    sch.inc("ve")
                    sch.set(f"yps{par}_ve", sch.val("ve"))
                    if eng_name == "sync":
                        eng.wait_ge(ve, sch.val("ve"))
                        eng.dma_start(out=y[o * 128:(o + 1) * 128, sp2 * 512:(sp2 + 1) * 512],
                                      in_=yscr[par][:, :]).then_inc(st, 16)
                    sch.inc("st", 16)
                    sch.set(f"ysc{par}_st", sch.val("st"))
                sch.set(f"xres{o % 2}_ve", sch.val("ve"))

            if eng_name == "sync":
                eng.wait_ge(st, sch.val("st"))

        build_blocked(nc, gen, sems)
    return nc


def build_s3():
    nc = bass.Bass("TRN2", target_bir_lowering=False, debug=False)
    NH = 16
    M0 = 2.0 ** (-0.5)
    xs = nc.dram_tensor("xs", [D, 1024], F32, kind="ExternalInput").ap()
    xa = nc.dram_tensor("xa", [D, S], F32, kind="ExternalInput").ap()
    bq = nc.dram_tensor("bq", [16, 128, 2048], BF16, kind="ExternalInput").ap()
    bk = nc.dram_tensor("bk", [8, 128, 1024], BF16, kind="ExternalInput").ap()
    bv = nc.dram_tensor("bv", [8, 128, 1024], BF16, kind="ExternalInput").ap()
    bo = nc.dram_tensor("bo", [16, 128, 2048], BF16, kind="ExternalInput").ap()
    qpos = nc.dram_tensor("qpos", [1, 1024], F32, kind="ExternalInput").ap()
    y = nc.dram_tensor("y", [D, 1024], F32, kind="ExternalOutput").ap()
    xnall = nc.dram_tensor("xnall", [D, S], BF16)
    kfm = nc.dram_tensor("kfm", [512, S], BF16)
    vsw = nc.dram_tensor("vsw", [4, 128, S], BF16)

    ctx = ExitStack()
    with ctx:
        ps = [ctx.enter_context(nc.psum_tensor(f"ps{i}", [128, 512], F32)) for i in range(8)]
        ln = LNKit(nc, ctx, tw=512, banks=ps[0:4])

        AR = 49152
        arena = ctx.enter_context(nc.sbuf_tensor("arena", [128, AR], BF16))
        arena_f = arena.ap().bitcast(F32)
        q_fm = [ctx.enter_context(nc.sbuf_tensor(f"qfm{d}", [128, 1024], BF16)) for d in range(NH)]
        attn = q_fm
        R0 = ctx.enter_context(nc.sbuf_tensor("R0", [128, 512], F32))
        Dcols = ctx.enter_context(nc.sbuf_tensor("Dcols", [128, 128], F32))
        krow_i = ctx.enter_context(nc.sbuf_tensor("krow_i", [1, 64], F32))
        drow = ctx.enter_context(nc.sbuf_tensor("drow", [1, 128], F32))
        qposb = ctx.enter_context(nc.sbuf_tensor("qposb", [1, 1024], F32))
        a2t = ctx.enter_context(nc.sbuf_tensor("a2t", [128, 512], F32))
        sqt = ctx.enter_context(nc.sbuf_tensor("sqt", [128, 512], F32))
        mbad = ctx.enter_context(nc.sbuf_tensor("mbad", [128, 512], F32))
        s2t = [ctx.enter_context(nc.sbuf_tensor(f"s2t{p}", [128, 512], F32)) for p in range(2)]
        probs = [ctx.enter_context(nc.sbuf_tensor(f"probs{p}", [128, 512], BF16)) for p in range(2)]
        rcrow = ctx.enter_context(nc.sbuf_tensor("rcrow", [1, 1024], F32))
        rcpb = ctx.enter_context(nc.sbuf_tensor("rcpb", [128, 512], F32))
        kevac = [ctx.enter_context(nc.sbuf_tensor(f"kevac{p}", [64, 512], BF16)) for p in range(2)]
        vevac = [ctx.enter_context(nc.sbuf_tensor(f"vevac{p}", [128, 512], BF16)) for p in range(2)]
        xres = [ctx.enter_context(nc.sbuf_tensor(f"xres{p}", [128, 1024], F32)) for p in range(2)]
        yscr = [ctx.enter_context(nc.sbuf_tensor(f"yscr{p}", [128, 512], F32)) for p in range(2)]
        wbuf = [ctx.enter_context(nc.sbuf_tensor(f"wbuf{p}", [128, 2048], BF16)) for p in range(2)]

        ld = ctx.enter_context(nc.semaphore("ld"))
        mm = ctx.enter_context(nc.semaphore("mm"))
        ve = ctx.enter_context(nc.semaphore("ve"))
        sc = ctx.enter_context(nc.semaphore("sc"))
        st = ctx.enter_context(nc.semaphore("st"))
        sems = {"ld": ld, "mm": mm, "ve": ve, "sc": sc, "st": st}

        def xsup(d):
            return arena_f[:, d * 512:(d + 1) * 512]

        def xnsup(d):
            return arena[:, 16384 + d * 512:16384 + (d + 1) * 512]

        def kvw(i):
            return arena[:, 24576 + i * 1024:24576 + (i + 1) * 1024]

        def kgrow():
            return arena[:, 0:8192]

        def vgrow():
            return arena[:, 8192:16384]

        def barrier(eng_name, eng, sch):
            for nm in ("mm", "ve", "sc", "ld", "st"):
                v = sch.val(nm)
                if v > 0:
                    eng.wait_ge(sems[nm], v)

        def gen(eng_name, eng, sch):
            ln.emit_init(eng_name, eng, sch)
            if eng_name == "pool":
                eng.iota(R0[:, :], pattern=[[1, 512]], base=0, channel_multiplier=-1,
                         allow_small_or_imprecise_dtypes=True)
                eng.iota(krow_i[:, :], pattern=[[128, 64]], base=0, channel_multiplier=0,
                         allow_small_or_imprecise_dtypes=True)
                eng.dma_start(out=qposb[:, :], in_=qpos[:, :]).then_inc(ld, 16)
            sch.inc("ld", 16)
            if eng_name == "dve":
                eng.wait_ge(ld, sch.val("ld"))
                ii = None
                for sp2 in range(2):
                    ii = eng.tensor_scalar(drow[0:1, sp2 * 64:(sp2 + 1) * 64], krow_i[0:1, :],
                                           -1.0, qposb[0:1, sp2 * 512:sp2 * 512 + 1],
                                           ALU.mult, ALU.add)
                ii.then_inc(ve, 1)
            sch.inc("ve")
            if eng_name == "pe":
                eng.wait_ge(ve, sch.val("ve"))
                eng.matmul(ps[4][:, 0:128], ln.ones_f[:, :], drow[0:1, :],
                           start=True, stop=True).then_inc(mm, 1)
            sch.inc("mm")
            if eng_name == "dve":
                eng.wait_ge(mm, sch.val("mm"))
                eng.tensor_copy(Dcols[:, :], ps[4][:, 0:128]).then_inc(ve, 1)
            sch.inc("ve")

            for sup in range(16):
                if eng_name == "sync":
                    barrier(eng_name, eng, sch)
                    for d in range(NH):
                        eng.dma_start(out=xsup(d), in_=xa[d * 128:(d + 1) * 128,
                                                         sup * 512:(sup + 1) * 512]).then_inc(st, 16)
                sch.inc("st", 16 * NH)

                def cb(egn, eg, sh, d, dd, _sup=sup):
                    if egn == "sync":
                        eg.wait_ge(ve, sh.val("ve"))
                        eg.dma_start(out=xnall[d * 128:(d + 1) * 128,
                                               _sup * 512:(_sup + 1) * 512],
                                     in_=xnsup(dd)).then_inc(st, 16)
                    sh.inc("st", 16)
                    sh.set(ln.tag + f"_xnb{dd}_free", sh.val("st"))

                ln.emit(eng_name, eng, sch, [xsup(d) for d in range(NH)], 0,
                        [xnsup(0), xnsup(1)], 0, xt_ready=("st", sch.val("st")), chunk_cb=cb)
                barrier(eng_name, eng, sch)

            if eng_name == "pool":
                for i in range(8):
                    eng.dma_start(out=kvw(i), in_=bk[i]).then_inc(ld, 16)
                for i in range(8):
                    eng.dma_start(out=kvw(8 + i), in_=bv[i]).then_inc(ld, 16)
            sch.inc("ld", 16 * 16)
            for sup in range(16):
                if eng_name == "sync":
                    if sup > 0:
                        eng.wait_ge(mm, sch.val("kv_xn_mm"))
                    for d in range(NH):
                        eng.dma_start(out=xnsup(d), in_=xnall[d * 128:(d + 1) * 128,
                                                            sup * 512:(sup + 1) * 512]).then_inc(st, 16)
                sch.inc("st", 16 * NH)
                for oc in range(8):
                    par = oc % 2
                    if eng_name == "pe":
                        if oc == 0:
                            eng.wait_ge(st, sch.val("st"))
                            if sup == 0:
                                eng.wait_ge(ld, sch.val("ld"))
                        g = sch.val(f"kps{par}_sc")
                        if g > 0:
                            eng.wait_ge(sc, g)
                        for d in range(NH):
                            ii = eng.matmul(ps[4 + par][0:64, :], kvw(oc)[:, d * 64:(d + 1) * 64],
                                            xnsup(d), start=(d == 0), stop=(d == NH - 1))
                        ii.then_inc(mm, 1)
                    sch.inc("mm")
                    if eng_name == "act":
                        eng.wait_ge(mm, sch.val("mm"))
                        g = sch.val(f"kev{par}_st")
                        if g > 0:
                            eng.wait_ge(st, g)
                        eng.copy(kevac[par][:, :], ps[4 + par][0:64, :]).then_inc(sc, 1)
                    sch.inc("sc")
                    sch.set(f"kps{par}_sc", sch.val("sc"))
                    if eng_name == "sync":
                        eng.wait_ge(sc, sch.val("sc"))
                        eng.dma_start(out=kfm[oc * 64:(oc + 1) * 64, sup * 512:(sup + 1) * 512],
                                      in_=kevac[par][:, :]).then_inc(st, 16)
                    sch.inc("st", 16)
                    sch.set(f"kev{par}_st", sch.val("st"))
                for tt in range(4):
                    par = tt % 2
                    if eng_name == "pe":
                        g = sch.val(f"vps{par}_sc")
                        if g > 0:
                            eng.wait_ge(sc, g)
                        for oc in range(8):
                            for d in range(NH):
                                ii = eng.matmul(ps[6 + par][:, oc * 64:(oc + 1) * 64],
                                                xnsup(d)[:, tt * 128:(tt + 1) * 128],
                                                kvw(8 + oc)[:, d * 64:(d + 1) * 64],
                                                start=(d == 0), stop=(d == NH - 1))
                        ii.then_inc(mm, 1)
                    sch.inc("mm")
                    if eng_name == "act":
                        eng.wait_ge(mm, sch.val("mm"))
                        g = sch.val(f"vev{par}_st")
                        if g > 0:
                            eng.wait_ge(st, g)
                        eng.copy(vevac[par][:, :], ps[6 + par][:, :]).then_inc(sc, 1)
                    sch.inc("sc")
                    sch.set(f"vps{par}_sc", sch.val("sc"))
                    if eng_name == "sync":
                        eng.wait_ge(sc, sch.val("sc"))
                        for g4 in range(4):
                            eng.dma_start(out=vsw[g4, :, (sup * 4 + tt) * 128:(sup * 4 + tt + 1) * 128],
                                          in_=vevac[par][:, g4 * 128:(g4 + 1) * 128]).then_inc(st, 16)
                    sch.inc("st", 64)
                    sch.set(f"vev{par}_st", sch.val("st"))
                sch.set("kv_xn_mm", sch.val("mm"))
            barrier(eng_name, eng, sch)

            for sp2 in range(2):
                if eng_name == "sync":
                    barrier(eng_name, eng, sch)
                    for d in range(NH):
                        eng.dma_start(out=xsup(d), in_=xs[d * 128:(d + 1) * 128,
                                                         sp2 * 512:(sp2 + 1) * 512]).then_inc(st, 16)
                sch.inc("st", 16 * NH)
                ln.emit(eng_name, eng, sch, [xsup(d) for d in range(NH)], 0,
                        [xnsup(d) for d in range(NH)], sp2 * 0, xt_ready=("st", sch.val("st")))
                for o in range(NH):
                    par = o % 2
                    if eng_name == "pool":
                        g = sch.val(f"wb{par}_mm")
                        if g > 0:
                            eng.wait_ge(mm, g)
                        eng.dma_start(out=wbuf[par][:, :], in_=bq[o]).then_inc(ld, 16)
                    sch.inc("ld", 16)
                    if eng_name == "pe":
                        eng.wait_ge(ld, sch.val("ld"))
                        eng.wait_ge(ve, sch.val("ve"))
                        g = sch.val(f"qps{par}_ve")
                        if g > 0:
                            eng.wait_ge(ve, g)
                        for d in range(NH):
                            ii = eng.matmul(ps[4 + par][:, :], wbuf[par][:, d * 128:(d + 1) * 128],
                                            xnsup(d), start=(d == 0), stop=(d == NH - 1))
                        ii.then_inc(mm, 1)
                    sch.inc("mm")
                    sch.set(f"wb{par}_mm", sch.val("mm"))
                    if eng_name == "dve":
                        eng.wait_ge(mm, sch.val("mm"))
                        eng.tensor_scalar(q_fm[o][:, sp2 * 512:(sp2 + 1) * 512], ps[4 + par][:, :],
                                          0.08838834764831845, None, ALU.mult).then_inc(ve, 1)
                    sch.inc("ve")
                    sch.set(f"qps{par}_ve", sch.val("ve"))
                barrier(eng_name, eng, sch)

            for g4 in range(4):
                if eng_name == "sync":
                    barrier(eng_name, eng, sch)
                    eng.dma_start(out=kgrow(), in_=kfm[g4 * 128:(g4 + 1) * 128, :]).then_inc(st, 16)
                    eng.dma_start(out=vgrow(), in_=vsw[g4]).then_inc(st, 16)
                sch.inc("st", 32)
                for rp in range(2):
                    for sp2 in range(2):
                        for kt in range(64):
                            par = kt % 2
                            if eng_name == "dve":
                                if kt == 0:
                                    barrier(eng_name, eng, sch)
                                g = sch.val("prep_sc")
                                if g > 0:
                                    eng.wait_ge(sc, g)
                                eng.tensor_scalar(a2t[:, :], R0[:, :],
                                                  Dcols[0:128, sp2 * 64 + kt:sp2 * 64 + kt + 1],
                                                  None, ALU.add)
                                eng.tensor_scalar(mbad[:, :], a2t[:, :], 0.0, 1e30,
                                                  ALU.is_lt, ALU.mult)
                                eng.tensor_scalar(a2t[:, :], a2t[:, :], 0.0, None,
                                                  ALU.max).then_inc(ve, 1)
                            sch.inc("ve")
                            if eng_name == "act":
                                eng.wait_ge(ve, sch.val("ve"))
                                eng.activation(sqt[:, :], a2t[:, :], AF.Sqrt).then_inc(sc, 1)
                            sch.inc("sc")
                            if eng_name == "dve":
                                eng.wait_ge(sc, sch.val("sc"))
                                eng.tensor_tensor(sqt[:, :], sqt[:, :], mbad[:, :],
                                                  ALU.add).then_inc(ve, 1)
                            sch.inc("ve")
                            sch.set("prep_sc", sch.val("sc"))
                            prep_ve = sch.val("ve")
                            for rr in range(2):
                                r = rp * 2 + rr
                                h = g4 * 4 + r
                                slope = M0 ** (h + 1)
                                if eng_name == "pe":
                                    if kt == 0 and rr == 0:
                                        eng.wait_ge(st, sch.val("st"))
                                    g = sch.val(f"sps{par}_sc")
                                    if g > 0:
                                        eng.wait_ge(sc, g)
                                    eng.matmul(ps[par][:, :], kgrow()[:, kt * 128:(kt + 1) * 128],
                                               q_fm[h][:, sp2 * 512:(sp2 + 1) * 512],
                                               start=True, stop=True).then_inc(mm, 1)
                                sch.inc("mm")
                                if eng_name == "dve":
                                    eng.wait_ge(mm, sch.val("mm"))
                                    eng.wait_ge(ve, prep_ve)
                                    g = sch.val(f"s2{par}_sc")
                                    if g > 0:
                                        eng.wait_ge(sc, g)
                                    eng.scalar_tensor_tensor(s2t[par][:, :], sqt[:, :], -slope,
                                                             ps[par][:, :], ALU.mult,
                                                             ALU.add).then_inc(ve, 1)
                                sch.inc("ve")
                                if eng_name == "act":
                                    eng.wait_ge(ve, sch.val("ve"))
                                    g = sch.val(f"pb{par}_mm")
                                    if g > 0:
                                        eng.wait_ge(mm, g)
                                    eng.activation(probs[par][:, :], s2t[par][:, :],
                                                   AF.Exp).then_inc(sc, 1)
                                sch.inc("sc")
                                sch.set(f"sps{par}_sc", sch.val("sc"))
                                sch.set(f"s2{par}_sc", sch.val("sc"))
                                if eng_name == "pe":
                                    eng.wait_ge(sc, sch.val("sc"))
                                    if kt == 0:
                                        g = sch.val(f"avout{rr}_ve")
                                        if g > 0:
                                            eng.wait_ge(ve, g)
                                    eng.matmul(ps[2 + rr][:, :], vgrow()[:, kt * 128:(kt + 1) * 128],
                                               probs[par][:, :], start=(kt == 0), stop=(kt == 63))
                                    ii = eng.matmul(ps[6 + rr][0:1, :], ln.ones_b[:, :],
                                                    probs[par][:, :], start=(kt == 0), stop=(kt == 63))
                                    if kt == 63:
                                        ii.then_inc(mm, 1)
                                if kt == 63:
                                    sch.inc("mm")
                                sch.set(f"pb{par}_mm", sch.val("mm"))
                        for rr in range(2):
                            r = rp * 2 + rr
                            h = g4 * 4 + r
                            if eng_name == "dve":
                                eng.wait_ge(mm, sch.val("mm"))
                                eng.reciprocal(rcrow[0:1, 0:512], ps[6 + rr][0:1, :]).then_inc(ve, 1)
                            sch.inc("ve")
                            if eng_name == "pe":
                                eng.wait_ge(ve, sch.val("ve"))
                                g = sch.val("bc4_ve")
                                if g > 0:
                                    eng.wait_ge(ve, g)
                                eng.matmul(ps[4][:, :], ln.ones_f[:, :], rcrow[0:1, 0:512],
                                           start=True, stop=True).then_inc(mm, 1)
                            sch.inc("mm")
                            if eng_name == "dve":
                                eng.wait_ge(mm, sch.val("mm"))
                                eng.tensor_copy(rcpb[:, :], ps[4][:, :])
                                eng.tensor_tensor(attn[h][:, sp2 * 512:(sp2 + 1) * 512],
                                                  ps[2 + rr][:, :], rcpb[:, :],
                                                  ALU.mult).then_inc(ve, 1)
                            sch.inc("ve")
                            sch.set(f"avout{rr}_ve", sch.val("ve"))
                            sch.set("bc4_ve", sch.val("ve"))
            barrier(eng_name, eng, sch)

            for o in range(NH):
                if eng_name == "sync":
                    g = sch.val(f"xr{o % 2}_ve")
                    if g > 0:
                        eng.wait_ge(ve, g)
                    eng.dma_start(out=xres[o % 2][:, :], in_=xs[o * 128:(o + 1) * 128, :]).then_inc(st, 16)
                sch.inc("st", 16)
                if eng_name == "pool":
                    g = sch.val(f"wb{o % 2}_mm")
                    if g > 0:
                        eng.wait_ge(mm, g)
                    eng.dma_start(out=wbuf[o % 2][:, :], in_=bo[o]).then_inc(ld, 16)
                sch.inc("ld", 16)
                for sp2 in range(2):
                    par = sp2
                    if eng_name == "pe":
                        eng.wait_ge(ld, sch.val("ld"))
                        g = sch.val(f"yps{par}_ve")
                        if g > 0:
                            eng.wait_ge(ve, g)
                        for d in range(NH):
                            ii = eng.matmul(ps[4 + par][:, :], wbuf[o % 2][:, d * 128:(d + 1) * 128],
                                            attn[d][:, sp2 * 512:(sp2 + 1) * 512],
                                            start=(d == 0), stop=(d == NH - 1))
                        ii.then_inc(mm, 1)
                    sch.inc("mm")
                    sch.set(f"wb{o % 2}_mm", sch.val("mm"))
                    if eng_name == "dve":
                        eng.wait_ge(mm, sch.val("mm"))
                        eng.wait_ge(st, sch.val("st"))
                        g = sch.val(f"ysc{par}_st")
                        if g > 0:
                            eng.wait_ge(st, g)
                        eng.tensor_tensor(yscr[par][:, :], ps[4 + par][:, :],
                                          xres[o % 2][:, sp2 * 512:(sp2 + 1) * 512],
                                          ALU.add).then_inc(ve, 1)
                    sch.inc("ve")
                    sch.set(f"yps{par}_ve", sch.val("ve"))
                    if eng_name == "sync":
                        eng.wait_ge(ve, sch.val("ve"))
                        eng.dma_start(out=y[o * 128:(o + 1) * 128, sp2 * 512:(sp2 + 1) * 512],
                                      in_=yscr[par][:, :]).then_inc(st, 16)
                    sch.inc("st", 16)
                    sch.set(f"ysc{par}_st", sch.val("st"))
                sch.set(f"xr{o % 2}_ve", sch.val("ve"))

            if eng_name == "sync":
                eng.wait_ge(st, sch.val("st"))

        build_blocked(nc, gen, sems)
    return nc



PI = 3.141592653589793
TWO_PI = 6.283185307179586


def emit_rrsin(eng_name, eng, sch, sems, in_ap, out_ap, i32t, f32t, scrA, phase):
    ve, sc = sems["ve"], sems["sc"]
    if eng_name == "dve":
        eng.wait_ge(sems["mm"], sch.val("mm"))
        g = sch.val("rrs_sc")
        if g > 0:
            eng.wait_ge(sc, g)
        eng.tensor_scalar(scrA, in_ap, 1.0 / TWO_PI, phase / TWO_PI, ALU.mult, ALU.add)
        eng.tensor_copy(i32t, scrA)
        eng.tensor_copy(f32t, i32t)
        eng.tensor_scalar(f32t, f32t, -TWO_PI, phase, ALU.mult, ALU.add)
        eng.tensor_tensor(scrA, in_ap, f32t, ALU.add)
        eng.tensor_scalar(f32t, scrA, PI, None, ALU.is_gt)
        eng.scalar_tensor_tensor(scrA, f32t, -TWO_PI, scrA, ALU.mult, ALU.add)
        eng.tensor_scalar(f32t, scrA, -PI, None, ALU.is_lt)
        eng.scalar_tensor_tensor(scrA, f32t, TWO_PI, scrA, ALU.mult, ALU.add).then_inc(ve, 1)
    sch.inc("ve")
    if eng_name == "act":
        eng.wait_ge(ve, sch.val("ve"))
        eng.activation(out_ap, scrA, AF.Sin).then_inc(sc, 1)
    sch.inc("sc")
    sch.set("rrs_sc", sch.val("sc"))


def build_s1(dbg=False):
    nc = bass.Bass("TRN2", target_bir_lowering=False, debug=False)
    NH = 16
    M0 = 2.0 ** (-0.5)
    LN1E4 = 9.210340371976184
    xs = nc.dram_tensor("xs", [D, 1024], F32, kind="ExternalInput").ap()
    xa = nc.dram_tensor("xa", [D, S], F32, kind="ExternalInput").ap()
    wq = nc.dram_tensor("wq", [D, D], F32, kind="ExternalInput").ap()
    wkv = nc.dram_tensor("wkv", [D, D], F32, kind="ExternalInput").ap()
    wo = nc.dram_tensor("wo", [D, D], F32, kind="ExternalInput").ap()
    bo_in = nc.dram_tensor("bo", [NH, 128], F32, kind="ExternalInput").ap()
    qpos = nc.dram_tensor("qpos", [1, 1024], F32, kind="ExternalInput").ap()
    y = nc.dram_tensor("y", [D, 1024], F32, kind="ExternalOutput").ap()
    kind2 = "ExternalOutput" if dbg else "Internal"
    xnall = nc.dram_tensor("xnall", [D, S], BF16, kind=kind2)
    kfm = nc.dram_tensor("kfm", [1024, S], BF16, kind=kind2)
    vsw = nc.dram_tensor("vsw", [8, 128, S], BF16, kind=kind2)
    if dbg:
        qdbg = nc.dram_tensor("qdbg", [D, 1024], BF16, kind="ExternalOutput")
        adbg = nc.dram_tensor("adbg", [D, 1024], BF16, kind="ExternalOutput")
        sdbg = nc.dram_tensor("sdbg", [128, 256], BF16, kind="ExternalOutput")
        cdbg = nc.dram_tensor("cdbg", [128, 256], BF16, kind="ExternalOutput")
        ktdbg = nc.dram_tensor("ktdbg", [128, 1024], BF16, kind="ExternalOutput")

    ctx = ExitStack()
    with ctx:
        ps = [ctx.enter_context(nc.psum_tensor(f"ps{i}", [128, 512], F32)) for i in range(8)]
        ln = LNKit(nc, ctx, tw=512, banks=ps[0:4])

        AR = 49152
        arena = ctx.enter_context(nc.sbuf_tensor("arena", [128, AR], BF16))
        arena_f = arena.ap().bitcast(F32)
        xn_own = [arena[:, 32768 + d * 1024:32768 + (d + 1) * 1024] for d in range(NH)]
        q_fm = [ctx.enter_context(nc.sbuf_tensor(f"qfm{d}", [128, 1024], BF16)) for d in range(NH)]
        attn = q_fm
        R0 = ctx.enter_context(nc.sbuf_tensor("R0", [128, 512], F32))
        R64 = ctx.enter_context(nc.sbuf_tensor("R64", [128, 512], F32))
        Dcols = ctx.enter_context(nc.sbuf_tensor("Dcols", [128, 128], F32))
        krow_i = ctx.enter_context(nc.sbuf_tensor("krow_i", [1, 64], F32))
        drow = ctx.enter_context(nc.sbuf_tensor("drow", [1, 128], F32))
        qposb = ctx.enter_context(nc.sbuf_tensor("qposb", [1, 1024], F32))
        permM = ctx.enter_context(nc.sbuf_tensor("permM", [128, 128], BF16))
        freqR = ctx.enter_context(nc.sbuf_tensor("freqR", [1, 128], F32))
        freqB = ctx.enter_context(nc.sbuf_tensor("freqB", [128, 64], F32))
        cosq = ctx.enter_context(nc.sbuf_tensor("cosq", [128, 1024], BF16))
        sinsq = ctx.enter_context(nc.sbuf_tensor("sinsq", [128, 1024], BF16))
        coskt = ctx.enter_context(nc.sbuf_tensor("coskt", [128, 256], BF16))
        sinkt = ctx.enter_context(nc.sbuf_tensor("sinkt", [128, 256], BF16))
        rrF = ctx.enter_context(nc.sbuf_tensor("rrF", [128, 1024], F32))
        rrI = ctx.enter_context(nc.sbuf_tensor("rrI", [128, 512], I32))
        bscr = ctx.enter_context(nc.sbuf_tensor("bscr", [128, 1024], BF16))
        cB = ctx.enter_context(nc.sbuf_tensor("cB", [128, 1024], BF16))
        crow = ctx.enter_context(nc.sbuf_tensor("crow", [1, 1024], F32))
        mrow = ctx.enter_context(nc.sbuf_tensor("mrow", [1, 1024], F32))
        mqB = ctx.enter_context(nc.sbuf_tensor("mqB", [128, 1024], BF16))
        xnsum = ctx.enter_context(nc.sbuf_tensor("xnsum", [128, 17], F32))
        xnmb = ctx.enter_context(nc.sbuf_tensor("xnmb", [128, 16], BF16))
        vmean = ctx.enter_context(nc.sbuf_tensor("vmean", [128, 8], F32))
        a2t = ctx.enter_context(nc.sbuf_tensor("a2t", [128, 512], F32))
        abt = ctx.enter_context(nc.sbuf_tensor("abt", [128, 512], F32))
        mbt = a2t
        s2t = [ctx.enter_context(nc.sbuf_tensor("s2t0", [128, 512], F32))] * 2
        probs = [ctx.enter_context(nc.sbuf_tensor(f"probs{p}", [128, 512], BF16)) for p in range(2)]
        rcrow = ctx.enter_context(nc.sbuf_tensor("rcrow", [1, 512], F32))
        rcpb = ctx.enter_context(nc.sbuf_tensor("rcpb", [128, 512], F32))
        ktt = ctx.enter_context(nc.sbuf_tensor("ktt", [128, 1024], BF16))
        kt2 = ctx.enter_context(nc.sbuf_tensor("kt2", [128, 1024], BF16))
        kevac = [ctx.enter_context(nc.sbuf_tensor(f"kevac{p}", [128, 128], BF16)) for p in range(2)]
        vevac = [ctx.enter_context(nc.sbuf_tensor(f"vevac{p}", [128, 512], BF16)) for p in range(2)]
        xres = [ctx.enter_context(nc.sbuf_tensor(f"xres{p}", [128, 512], F32)) for p in range(2)]
        yscr = [ctx.enter_context(nc.sbuf_tensor(f"yscr{p}", [128, 512], F32)) for p in range(2)]
        boT = ctx.enter_context(nc.sbuf_tensor("boT", [128, NH], F32))
        srow = ctx.enter_context(nc.sbuf_tensor("srow", [NH, 128], F32))
        identf = ctx.enter_context(nc.sbuf_tensor("identf", [128, 128], F32))
        identb = ctx.enter_context(nc.sbuf_tensor("identb", [128, 128], BF16))
        iden_io = ctx.enter_context(nc.sbuf_tensor("iden_io", [128, 128], F32))

        ld = ctx.enter_context(nc.semaphore("ld"))
        mm = ctx.enter_context(nc.semaphore("mm"))
        ve = ctx.enter_context(nc.semaphore("ve"))
        sc = ctx.enter_context(nc.semaphore("sc"))
        st = ctx.enter_context(nc.semaphore("st"))
        gp = ctx.enter_context(nc.semaphore("gp"))
        sems = {"ld": ld, "mm": mm, "ve": ve, "sc": sc, "st": st, "gp": gp}

        def wtile(d):
            return arena[:, d * 2048:(d + 1) * 2048]

        def xsup(d):
            return arena_f[:, d * 512:(d + 1) * 512]

        def xnsup(d):
            return arena[:, 32768 + d * 512:32768 + (d + 1) * 512]

        def kgrow():
            return arena[:, 0:8192]

        def vgrow():
            return arena[:, 8192:16384]

        def barrier(eng_name, eng, sch):
            for nm in ("mm", "ve", "sc", "ld", "st", "gp"):
                v = sch.val(nm)
                if v > 0:
                    eng.wait_ge(sems[nm], v)

        def gen(eng_name, eng, sch):
            ln.emit_init(eng_name, eng, sch)
            if eng_name == "pool":
                eng.iota(R0[:, :], pattern=[[1, 512]], base=0, channel_multiplier=-1,
                         allow_small_or_imprecise_dtypes=True)
                eng.iota(R64[:, :], pattern=[[0, 512]], base=63, channel_multiplier=-1,
                         allow_small_or_imprecise_dtypes=True)
                eng.iota(krow_i[:, :], pattern=[[128, 64]], base=0, channel_multiplier=0,
                         allow_small_or_imprecise_dtypes=True)
                eng.iota(iden_io[:, :], pattern=[[1, 128]], base=0, channel_multiplier=-1,
                         allow_small_or_imprecise_dtypes=True)
                eng.iota(freqR[:, :], pattern=[[1, 128]], base=0, channel_multiplier=0,
                         allow_small_or_imprecise_dtypes=True).then_inc(gp, 1)
                eng.dma_start(out=qposb[:, :], in_=qpos[:, :]).then_inc(ld, 16)
                eng.dma_start(out=srow[:, :], in_=bo_in[:, :]).then_inc(ld, 16)
            sch.inc("gp")
            sch.inc("ld", 32)
            if eng_name == "dve":
                eng.wait_ge(ld, sch.val("ld"))
                eng.wait_ge(gp, sch.val("gp"))
                eng.tensor_scalar(identf[:, :], iden_io[:, :], 0.0, None, ALU.is_equal)
                eng.tensor_copy(identb[:, :], identf[:, :])
                eng.tensor_scalar(rrF[:, 0:128], iden_io[:, :], 64.0, None, ALU.is_equal)
                eng.tensor_scalar(rrF[:, 128:256], iden_io[:, :], -64.0, None, ALU.is_equal)
                eng.tensor_tensor(rrF[:, 0:128], rrF[:, 0:128], rrF[:, 128:256], ALU.add)
                eng.tensor_copy(permM[:, :], rrF[:, 0:128])
                eng.tensor_scalar(R64[:, :], R64[:, :], 0.0, None, ALU.is_ge)
                eng.tensor_scalar(freqR[0:1, 0:64], freqR[0:1, 0:64], -LN1E4 / 64.0, None,
                                  ALU.mult).then_inc(ve, 1)
            sch.inc("ve")
            if eng_name == "act":
                eng.wait_ge(ve, sch.val("ve"))
                eng.activation(freqR[0:1, 0:64], freqR[0:1, 0:64], AF.Exp).then_inc(sc, 1)
            sch.inc("sc")
            if eng_name == "dve":
                eng.wait_ge(sc, sch.val("sc"))
                eng.tensor_copy(freqR[0:1, 64:128], freqR[0:1, 0:64]).then_inc(ve, 1)
            sch.inc("ve")
            if eng_name == "dve":
                eng.wait_ge(sc, sch.val("sc"))
                for sp2 in range(2):
                    eng.tensor_scalar(drow[0:1, sp2 * 64:(sp2 + 1) * 64], krow_i[0:1, :],
                                      -1.0, qposb[0:1, sp2 * 512:sp2 * 512 + 1],
                                      ALU.mult, ALU.add)
                eng.tensor_scalar(mrow[0:1, :], qposb[0:1, :], 63.5, None, ALU.is_lt).then_inc(ve, 1)
            sch.inc("ve")
            if eng_name == "pe":
                eng.wait_ge(ve, sch.val("ve"))
                eng.matmul(ps[4][:, 0:128], ln.ones_f[:, :], drow[0:1, :], start=True, stop=True)
                eng.matmul(ps[5][:, :], ln.ones_f[:, :], mrow[0:1, 0:512], start=True, stop=True)
                eng.matmul(ps[6][:, :], ln.ones_f[:, :], mrow[0:1, 512:1024], start=True, stop=True)
                eng.matmul(ps[7][:, 0:64], ln.ones_f[:, :], freqR[0:1, 0:64], start=True, stop=True)
                eng.transpose(ps[3][0:128, 0:NH], srow[:, :], identf[0:NH, 0:NH]).then_inc(mm, 1)
            sch.inc("mm")
            if eng_name == "dve":
                eng.wait_ge(mm, sch.val("mm"))
                eng.tensor_copy(Dcols[:, :], ps[4][:, 0:128])
                eng.tensor_copy(mqB[:, 0:512], ps[5][:, :])
                eng.tensor_copy(mqB[:, 512:1024], ps[6][:, :])
                eng.tensor_copy(freqB[:, :], ps[7][:, 0:64])
                eng.tensor_copy(boT[:, :], ps[3][0:128, 0:NH])
                eng.memset(xnsum[:, :], 0.0).then_inc(ve, 1)
            sch.inc("ve")

            for half in range(2):
                if eng_name == "pe":
                    eng.wait_ge(ve, sch.val("ve"))
                    g = sch.val("rr_ve")
                    if g > 0:
                        eng.wait_ge(ve, g)
                    eng.matmul(ps[4 + half][:, :], freqR[0:1, 0:128],
                               qposb[0:1, half * 512:(half + 1) * 512],
                               start=True, stop=True).then_inc(mm, 1)
                sch.inc("mm")
                emit_rrsin(eng_name, eng, sch, sems, ps[4 + half][:, :],
                           sinsq[:, half * 512:(half + 1) * 512],
                           rrI[:, :], rrF[:, 0:512], rrF[:, 512:1024], 0.0)
                emit_rrsin(eng_name, eng, sch, sems, ps[4 + half][:, :],
                           cosq[:, half * 512:(half + 1) * 512],
                           rrI[:, :], rrF[:, 0:512], rrF[:, 512:1024], PI / 2)
                sch.set("rr_ve", sch.val("ve"))
            if eng_name == "dve":
                eng.wait_ge(sc, sch.val("sc"))
                eng.tensor_scalar(sinsq[0:64, :], sinsq[0:64, :], -1.0, None, ALU.mult).then_inc(ve, 1)
            sch.inc("ve")

            for sp2 in range(2):
                if eng_name == "sync":
                    barrier(eng_name, eng, sch)
                    for d in range(NH):
                        eng.dma_start(out=xsup(d), in_=xs[d * 128:(d + 1) * 128,
                                                         sp2 * 512:(sp2 + 1) * 512]).then_inc(st, 16)
                sch.inc("st", 16 * NH)
                ln.emit(eng_name, eng, sch, [xsup(d) for d in range(NH)], 0,
                        xn_own, sp2 * 512, xt_ready=("st", sch.val("st")))
                for d in range(NH):
                    if eng_name == "act":
                        eng.wait_ge(ve, sch.val("ve"))
                        g = sch.val("abs_mm")
                        if g > 0:
                            eng.wait_ge(mm, g)
                        eng.activation(probs[0][:, :], xn_own[d][:, sp2 * 512:(sp2 + 1) * 512],
                                       AF.Abs).then_inc(sc, 1)
                    sch.inc("sc")
                    if eng_name == "pe":
                        eng.wait_ge(sc, sch.val("sc"))
                        ii = eng.matmul(ps[4][0:1, :], ln.ones_b[:, :], probs[0][:, :],
                                        start=(d == 0), stop=(d == NH - 1))
                        if d == NH - 1:
                            ii.then_inc(mm, 1)
                        else:
                            ii.then_inc(mm, 1)
                    sch.inc("mm")
                    sch.set("abs_mm", sch.val("mm"))
                if eng_name == "dve":
                    eng.wait_ge(mm, sch.val("mm"))
                    eng.tensor_copy(crow[0:1, sp2 * 512:(sp2 + 1) * 512], ps[4][0:1, :]).then_inc(ve, 1)
                sch.inc("ve")
                barrier(eng_name, eng, sch)
            if eng_name == "act":
                eng.activation(crow[0:1, :], crow[0:1, :], AF.Sigmoid).then_inc(sc, 1)
            sch.inc("sc")
            if eng_name == "dve":
                eng.wait_ge(sc, sch.val("sc"))
                eng.tensor_scalar(crow[0:1, :], crow[0:1, :], 0.08838834764831845, None,
                                  ALU.mult).then_inc(ve, 1)
            sch.inc("ve")
            for half in range(2):
                if eng_name == "pe":
                    eng.wait_ge(ve, sch.val("ve"))
                    eng.matmul(ps[4 + half][:, :], ln.ones_f[:, :],
                               crow[0:1, half * 512:(half + 1) * 512],
                               start=True, stop=True).then_inc(mm, 1)
                sch.inc("mm")
            if eng_name == "dve":
                eng.wait_ge(mm, sch.val("mm"))
                eng.tensor_copy(cB[:, 0:512], ps[4][:, :])
                eng.tensor_copy(cB[:, 512:1024], ps[5][:, :]).then_inc(ve, 1)
            sch.inc("ve")

            if eng_name == "pool":
                barrier(eng_name, eng, sch)
                for d in range(NH):
                    eng.dma_start(out=wtile(d), in_=wq[d * 128:(d + 1) * 128, :]).then_inc(ld, 16)
            sch.inc("ld", 16 * NH)
            for o in range(NH):
                for sp2 in range(2):
                    par = sp2
                    if eng_name == "pe":
                        if o == 0 and sp2 == 0:
                            eng.wait_ge(ld, sch.val("ld"))
                        g = sch.val(f"qps{par}_ve")
                        if g > 0:
                            eng.wait_ge(ve, g)
                        for d in range(NH):
                            ii = eng.matmul(ps[4 + par][:, :], wtile(d)[:, o * 128:(o + 1) * 128],
                                            xn_own[d][:, sp2 * 512:(sp2 + 1) * 512],
                                            start=(d == 0), stop=(d == NH - 1))
                        ii.then_inc(mm, 1)
                    sch.inc("mm")
                    if eng_name == "dve":
                        eng.wait_ge(mm, sch.val("mm"))
                        eng.tensor_copy(q_fm[o][:, sp2 * 512:(sp2 + 1) * 512],
                                        ps[4 + par][:, :]).then_inc(ve, 1)
                    sch.inc("ve")
                    sch.set(f"qps{par}_ve", sch.val("ve"))
                if eng_name == "pe":
                    eng.wait_ge(ve, sch.val("ve"))
                    g = sch.val("ropeps_ve")
                    if g > 0:
                        eng.wait_ge(ve, g)
                    eng.matmul(ps[6][:, :], permM[:, :], q_fm[o][:, 0:512], start=True, stop=True)
                    eng.matmul(ps[7][:, :], permM[:, :], q_fm[o][:, 512:1024],
                               start=True, stop=True).then_inc(mm, 1)
                sch.inc("mm")
                if eng_name == "dve":
                    eng.wait_ge(mm, sch.val("mm"))
                    for half in range(2):
                        sl = slice(half * 512, (half + 1) * 512)
                        eng.tensor_copy(bscr[:, 0:512], ps[6 + half][:, :])
                        eng.tensor_tensor(bscr[:, 512:1024], q_fm[o][:, sl], cosq[:, sl], ALU.mult)
                        eng.tensor_tensor(bscr[:, 0:512], bscr[:, 0:512], sinsq[:, sl], ALU.mult)
                        eng.tensor_tensor(bscr[:, 0:512], bscr[:, 0:512],
                                          bscr[:, 512:1024], ALU.add)
                        ii = eng.tensor_tensor(q_fm[o][:, sl], bscr[:, 0:512], cB[:, sl], ALU.mult)
                    ii.then_inc(ve, 1)
                sch.inc("ve")
                sch.set("ropeps_ve", sch.val("ve"))
            barrier(eng_name, eng, sch)
            if dbg:
                if eng_name == "sync":
                    for d in range(NH):
                        eng.dma_start(out=qdbg[d * 128:(d + 1) * 128, :],
                                      in_=q_fm[d][:, :]).then_inc(st, 16)
                sch.inc("st", 16 * NH)
                barrier(eng_name, eng, sch)

            for sup in range(16):
                if eng_name == "sync":
                    barrier(eng_name, eng, sch)
                    for d in range(NH):
                        eng.dma_start(out=xsup(d), in_=xa[d * 128:(d + 1) * 128,
                                                         sup * 512:(sup + 1) * 512]).then_inc(st, 16)
                sch.inc("st", 16 * NH)

                def cb(egn, eg, sh, d, dd, _sup=sup):
                    if egn == "sync":
                        eg.wait_ge(ve, sh.val("ve"))
                        eg.dma_start(out=xnall[d * 128:(d + 1) * 128,
                                               _sup * 512:(_sup + 1) * 512],
                                     in_=xnsup(dd)).then_inc(st, 16)
                    sh.inc("st", 16)
                    sh.set(ln.tag + f"_xnb{dd}_free", sh.val("st"))
                    if egn == "dve":
                        eg.tensor_reduce(xnsum[:, 16:17], xnsup(dd), mybir.AxisListType.X, ALU.add)
                        eg.tensor_tensor(xnsum[:, d:d + 1], xnsum[:, d:d + 1],
                                         xnsum[:, 16:17], ALU.add)

                ln.emit(eng_name, eng, sch, [xsup(d) for d in range(NH)], 0,
                        [xnsup(0), xnsup(1)], 0, xt_ready=("st", sch.val("st")), chunk_cb=cb)
                barrier(eng_name, eng, sch)

            if eng_name == "pool":
                for d in range(NH):
                    eng.dma_start(out=wtile(d), in_=wkv[d * 128:(d + 1) * 128, :]).then_inc(ld, 16)
            sch.inc("ld", 16 * NH)
            for sup in range(16):
                if eng_name == "sync":
                    if sup > 0:
                        eng.wait_ge(mm, sch.val("kv_xn_mm"))
                    for d in range(NH):
                        eng.dma_start(out=xnsup(d), in_=xnall[d * 128:(d + 1) * 128,
                                                            sup * 512:(sup + 1) * 512]).then_inc(st, 16)
                sch.inc("st", 16 * NH)
                for tt in range(4):
                    if eng_name == "pool":
                        g = sch.val("krr_ve")
                        if g > 0:
                            eng.wait_ge(ve, g)
                        eng.iota(rrF[:, 0:1], pattern=[[0, 1]], base=sup * 512 + tt * 128,
                                 channel_multiplier=1,
                                 allow_small_or_imprecise_dtypes=True).then_inc(gp, 1)
                    sch.inc("gp")
                    if eng_name == "dve":
                        eng.wait_ge(gp, sch.val("gp"))
                        g = sch.val("rrs_sc")
                        if g > 0:
                            eng.wait_ge(sc, g)
                        eng.tensor_scalar(rrF[:, 64:128], freqB[:, :], rrF[:, 0:1],
                                          None, ALU.mult).then_inc(ve, 1)
                    sch.inc("ve")
                    emit_rrsin(eng_name, eng, sch, sems, rrF[:, 64:128],
                               sinkt[:, tt * 64:(tt + 1) * 64],
                               rrI[:, 0:64], rrF[:, 128:192], rrF[:, 192:256], 0.0)
                    emit_rrsin(eng_name, eng, sch, sems, rrF[:, 64:128],
                               coskt[:, tt * 64:(tt + 1) * 64],
                               rrI[:, 0:64], rrF[:, 128:192], rrF[:, 192:256], PI / 2)
                    sch.set("krr_ve", sch.val("ve"))
                for tt in range(4):
                    for oc2 in range(2):
                        if eng_name == "pe":
                            if tt == 0 and oc2 == 0:
                                eng.wait_ge(st, sch.val("st"))
                                eng.wait_ge(sc, sch.val("sc"))
                                if sup == 0:
                                    eng.wait_ge(ld, sch.val("ld"))
                            g = sch.val(f"kps{oc2}_sc")
                            if g > 0:
                                eng.wait_ge(sc, g)
                            for d in range(NH):
                                ii = eng.matmul(ps[4 + oc2][:, :],
                                                xnsup(d)[:, tt * 128:(tt + 1) * 128],
                                                wtile(d)[:, oc2 * 512:(oc2 + 1) * 512],
                                                start=(d == 0), stop=(d == NH - 1))
                            ii.then_inc(mm, 1)
                        sch.inc("mm")
                        if eng_name == "act":
                            eng.wait_ge(mm, sch.val("mm"))
                            g = sch.val("ktt_ve")
                            if g > 0 and oc2 == 0:
                                eng.wait_ge(ve, g)
                            g = sch.val("ktr_mm")
                            if g > 0 and oc2 == 0:
                                eng.wait_ge(mm, g)
                            eng.copy(ktt[:, oc2 * 512:(oc2 + 1) * 512],
                                     ps[4 + oc2][:, :]).then_inc(sc, 1)
                        sch.inc("sc")
                        sch.set(f"kps{oc2}_sc", sch.val("sc"))
                    if eng_name == "dve":
                        eng.wait_ge(sc, sch.val("sc"))
                        g = sch.val("ktr_mm")
                        if g > 0:
                            eng.wait_ge(mm, g)
                        for kvh in range(8):
                            c0 = kvh * 128
                            cs = slice(tt * 64, (tt + 1) * 64)
                            eng.tensor_tensor(kt2[:, c0:c0 + 64], ktt[:, c0:c0 + 64],
                                              coskt[:, cs], ALU.mult)
                            eng.tensor_tensor(bscr[:, 0:64], ktt[:, c0 + 64:c0 + 128],
                                              sinkt[:, cs], ALU.mult)
                            eng.tensor_tensor(kt2[:, c0:c0 + 64], kt2[:, c0:c0 + 64],
                                              bscr[:, 0:64], ALU.subtract)
                            eng.tensor_tensor(kt2[:, c0 + 64:c0 + 128], ktt[:, c0 + 64:c0 + 128],
                                              coskt[:, cs], ALU.mult)
                            eng.tensor_tensor(bscr[:, 64:128], ktt[:, c0:c0 + 64],
                                              sinkt[:, cs], ALU.mult)
                            ii = eng.tensor_tensor(kt2[:, c0 + 64:c0 + 128],
                                                   kt2[:, c0 + 64:c0 + 128],
                                                   bscr[:, 64:128], ALU.add)
                        ii.then_inc(ve, 1)
                    sch.inc("ve")
                    sch.set("ktt_ve", sch.val("ve"))
                    for kvh in range(8):
                        par = kvh % 2
                        if eng_name == "pe":
                            eng.wait_ge(ve, sch.val("ve"))
                            g = sch.val(f"ktr{par}_sc")
                            if g > 0:
                                eng.wait_ge(sc, g)
                            eng.transpose(ps[6 + par].ap().bitcast(BF16)[:, 0:128],
                                          kt2[:, kvh * 128:(kvh + 1) * 128],
                                          identb[:, :]).then_inc(mm, 1)
                        sch.inc("mm")
                        if eng_name == "act":
                            eng.wait_ge(mm, sch.val("mm"))
                            g = sch.val(f"kev{par}_st")
                            if g > 0:
                                eng.wait_ge(st, g)
                            eng.copy(kevac[par][:, :],
                                     ps[6 + par].ap().bitcast(BF16)[:, 0:128]).then_inc(sc, 1)
                        sch.inc("sc")
                        sch.set(f"ktr{par}_sc", sch.val("sc"))
                        if eng_name == "sync":
                            eng.wait_ge(sc, sch.val("sc"))
                            eng.dma_start(out=kfm[kvh * 128:(kvh + 1) * 128,
                                                  sup * 512 + tt * 128:sup * 512 + (tt + 1) * 128],
                                          in_=kevac[par][:, :]).then_inc(st, 16)
                        sch.inc("st", 16)
                        sch.set(f"kev{par}_st", sch.val("st"))
                    sch.set("ktr_mm", sch.val("mm"))
                    for oc2 in range(2):
                        par = oc2
                        if eng_name == "pe":
                            g = sch.val(f"vps{par}_sc")
                            if g > 0:
                                eng.wait_ge(sc, g)
                            for d in range(NH):
                                ii = eng.matmul(ps[4 + par][:, :],
                                                xnsup(d)[:, tt * 128:(tt + 1) * 128],
                                                wtile(d)[:, 1024 + oc2 * 512:1024 + (oc2 + 1) * 512],
                                                start=(d == 0), stop=(d == NH - 1))
                            ii.then_inc(mm, 1)
                        sch.inc("mm")
                        if eng_name == "act":
                            eng.wait_ge(mm, sch.val("mm"))
                            g = sch.val(f"vev{par}_st")
                            if g > 0:
                                eng.wait_ge(st, g)
                            eng.copy(vevac[par][:, :], ps[4 + par][:, :]).then_inc(sc, 1)
                        sch.inc("sc")
                        sch.set(f"vps{par}_sc", sch.val("sc"))
                        if eng_name == "sync":
                            eng.wait_ge(sc, sch.val("sc"))
                            for g4 in range(4):
                                kvh = oc2 * 4 + g4
                                eng.dma_start(out=vsw[kvh, :, (sup * 4 + tt) * 128:(sup * 4 + tt + 1) * 128],
                                              in_=vevac[par][:, g4 * 128:(g4 + 1) * 128]).then_inc(st, 16)
                        sch.inc("st", 64)
                        sch.set(f"vev{par}_st", sch.val("st"))
                sch.set("kv_xn_mm", sch.val("mm"))
            if dbg:
                if eng_name == "sync":
                    barrier(eng_name, eng, sch)
                    eng.dma_start(out=sdbg.ap(), in_=sinkt[:, :]).then_inc(st, 16)
                    eng.dma_start(out=cdbg.ap(), in_=coskt[:, :]).then_inc(st, 16)
                    eng.dma_start(out=ktdbg.ap(), in_=kt2[:, :]).then_inc(st, 16)
                sch.inc("st", 48)
            if eng_name == "dve":
                eng.wait_ge(ve, sch.val("ve"))
                eng.tensor_scalar(xnmb[:, :], xnsum[:, 0:16], 1.0 / 8192.0, None,
                                  ALU.mult).then_inc(ve, 1)
            sch.inc("ve")
            for o in range(8):
                if eng_name == "pe":
                    eng.wait_ge(ve, sch.val("ve"))
                    for d in range(NH):
                        ii = eng.matmul(ps[4][:, o:o + 1],
                                        wtile(d)[:, 1024 + o * 128:1024 + (o + 1) * 128],
                                        xnmb[:, d:d + 1], start=(d == 0), stop=(d == NH - 1))
                    ii.then_inc(mm, 1)
                sch.inc("mm")
            if eng_name == "dve":
                eng.wait_ge(mm, sch.val("mm"))
                eng.tensor_copy(vmean[:, :], ps[4][:, 0:8]).then_inc(ve, 1)
            sch.inc("ve")
            barrier(eng_name, eng, sch)

            for kvh in range(8):
                if eng_name == "sync":
                    barrier(eng_name, eng, sch)
                    eng.dma_start(out=kgrow(), in_=kfm[kvh * 128:(kvh + 1) * 128, :]).then_inc(st, 16)
                    eng.dma_start(out=vgrow(), in_=vsw[kvh]).then_inc(st, 16)
                sch.inc("st", 32)
                for sp2 in range(2):
                    for kt in range(64):
                        par = kt % 2
                        if eng_name == "dve":
                            if kt == 0:
                                barrier(eng_name, eng, sch)
                            g = sch.val("prep_sc")
                            if g > 0:
                                eng.wait_ge(sc, g)
                            eng.tensor_scalar(a2t[:, :], R0[:, :],
                                              Dcols[0:128, sp2 * 64 + kt:sp2 * 64 + kt + 1],
                                              None, ALU.add).then_inc(ve, 1)
                        sch.inc("ve")
                        if eng_name == "act":
                            eng.wait_ge(ve, sch.val("ve"))
                            eng.activation(abt[:, :], a2t[:, :], AF.Abs).then_inc(sc, 1)
                        sch.inc("sc")
                        if eng_name == "dve":
                            eng.wait_ge(sc, sch.val("sc"))
                            eng.tensor_scalar(mbt[:, :], abt[:, :], 256.5, 1e30, ALU.is_le, ALU.mult)
                            eng.tensor_scalar(abt[:, :], abt[:, :], -257.0, None, ALU.add)
                            ii = eng.tensor_tensor(abt[:, :], abt[:, :], mbt[:, :], ALU.add)
                            if kt == 0:
                                ii = eng.scalar_tensor_tensor(abt[:, :], R64[:, :], 1e30,
                                                              abt[:, :], ALU.mult, ALU.add)
                            ii.then_inc(ve, 1)
                        sch.inc("ve")
                        sch.set("prep_sc", sch.val("sc"))
                        prep_ve = sch.val("ve")
                        for rep in range(2):
                            h = kvh * 2 + rep
                            slope = M0 ** (h + 1)
                            if eng_name == "pe":
                                if kt == 0 and rep == 0:
                                    eng.wait_ge(st, sch.val("st"))
                                g = sch.val(f"sps{par}_sc")
                                if g > 0:
                                    eng.wait_ge(sc, g)
                                eng.matmul(ps[par][:, :], kgrow()[:, kt * 128:(kt + 1) * 128],
                                           q_fm[h][:, sp2 * 512:(sp2 + 1) * 512],
                                           start=True, stop=True).then_inc(mm, 1)
                            sch.inc("mm")
                            if eng_name == "dve":
                                eng.wait_ge(mm, sch.val("mm"))
                                eng.wait_ge(ve, prep_ve)
                                g = sch.val("s2s_sc")
                                if g > 0:
                                    eng.wait_ge(sc, g)
                                eng.scalar_tensor_tensor(s2t[par][:, :], abt[:, :], -slope,
                                                         ps[par][:, :], ALU.mult,
                                                         ALU.add).then_inc(ve, 1)
                            sch.inc("ve")
                            if eng_name == "act":
                                eng.wait_ge(ve, sch.val("ve"))
                                eng.activation(probs[par][:, :], s2t[par][:, :],
                                               AF.Exp).then_inc(sc, 1)
                            sch.inc("sc")
                            sch.set(f"sps{par}_sc", sch.val("sc"))
                            sch.set("s2s_sc", sch.val("sc"))
                            if eng_name == "pe":
                                eng.wait_ge(sc, sch.val("sc"))
                                if kt == 0:
                                    g = sch.val(f"avout{rep}_ve")
                                    if g > 0:
                                        eng.wait_ge(ve, g)
                                eng.matmul(ps[2 + rep][:, :], vgrow()[:, kt * 128:(kt + 1) * 128],
                                           probs[par][:, :], start=(kt == 0), stop=(kt == 63))
                                ii = eng.matmul(ps[6 + rep][0:1, :], ln.ones_b[:, :],
                                                probs[par][:, :], start=(kt == 0), stop=(kt == 63))
                                if kt == 63:
                                    ii.then_inc(mm, 1)
                            if kt == 63:
                                sch.inc("mm")
                    for rep in range(2):
                        h = kvh * 2 + rep
                        if eng_name == "dve":
                            eng.wait_ge(mm, sch.val("mm"))
                            eng.reciprocal(rcrow[0:1, :], ps[6 + rep][0:1, :]).then_inc(ve, 1)
                        sch.inc("ve")
                        if eng_name == "pe":
                            eng.wait_ge(ve, sch.val("ve"))
                            g = sch.val("bc4_ve")
                            if g > 0:
                                eng.wait_ge(ve, g)
                            eng.matmul(ps[4][:, :], ln.ones_f[:, :], rcrow[0:1, :],
                                       start=True, stop=True).then_inc(mm, 1)
                        sch.inc("mm")
                        if eng_name == "dve":
                            eng.wait_ge(mm, sch.val("mm"))
                            eng.tensor_copy(rcpb[:, :], ps[4][:, :])
                            eng.tensor_tensor(attn[h][:, sp2 * 512:(sp2 + 1) * 512],
                                              ps[2 + rep][:, :], rcpb[:, :], ALU.mult).then_inc(ve, 1)
                        sch.inc("ve")
                        sch.set(f"avout{rep}_ve", sch.val("ve"))
                        sch.set("bc4_ve", sch.val("ve"))
            barrier(eng_name, eng, sch)
            for h in range(NH):
                if eng_name == "dve":
                    eng.tensor_tensor(bscr[:, 0:1024], attn[h][:, :], mqB[:, :], ALU.mult)
                    eng.tensor_tensor(attn[h][:, :], attn[h][:, :], bscr[:, 0:1024], ALU.subtract)
                    eng.tensor_scalar(bscr[:, 0:1024], mqB[:, :],
                                      vmean[:, h // 2:h // 2 + 1], None, ALU.mult)
                    ii = eng.tensor_tensor(attn[h][:, :], attn[h][:, :],
                                           bscr[:, 0:1024], ALU.add)
                    if h == NH - 1:
                        ii.then_inc(ve, 1)
            sch.inc("ve")
            barrier(eng_name, eng, sch)
            if dbg:
                if eng_name == "sync":
                    for d in range(NH):
                        eng.dma_start(out=adbg[d * 128:(d + 1) * 128, :],
                                      in_=attn[d][:, :]).then_inc(st, 16)
                sch.inc("st", 16 * NH)
                barrier(eng_name, eng, sch)

            if eng_name == "pool":
                for d in range(NH):
                    eng.dma_start(out=wtile(d), in_=wo[d * 128:(d + 1) * 128, :]).then_inc(ld, 16)
            sch.inc("ld", 16 * NH)
            for o in range(NH):
                for sp2 in range(2):
                    par = sp2
                    if eng_name == "sync":
                        g = sch.val(f"xr{par}_ve")
                        if g > 0:
                            eng.wait_ge(ve, g)
                        eng.dma_start(out=xres[par][:, :],
                                      in_=xs[o * 128:(o + 1) * 128,
                                             sp2 * 512:(sp2 + 1) * 512]).then_inc(st, 16)
                    sch.inc("st", 16)
                    if eng_name == "pe":
                        if o == 0 and sp2 == 0:
                            eng.wait_ge(ld, sch.val("ld"))
                        g = sch.val(f"yps{par}_ve")
                        if g > 0:
                            eng.wait_ge(ve, g)
                        for d in range(NH):
                            ii = eng.matmul(ps[4 + par][:, :], wtile(d)[:, o * 128:(o + 1) * 128],
                                            attn[d][:, sp2 * 512:(sp2 + 1) * 512],
                                            start=(d == 0), stop=(d == NH - 1))
                        ii.then_inc(mm, 1)
                    sch.inc("mm")
                    if eng_name == "dve":
                        eng.wait_ge(mm, sch.val("mm"))
                        eng.wait_ge(st, sch.val("st"))
                        g = sch.val(f"ysc{par}_st")
                        if g > 0:
                            eng.wait_ge(st, g)
                        eng.scalar_tensor_tensor(yscr[par][:, :], ps[4 + par][:, :], boT[:, o:o + 1],
                                                 xres[par][:, :],
                                                 ALU.add, ALU.add).then_inc(ve, 1)
                    sch.inc("ve")
                    sch.set(f"yps{par}_ve", sch.val("ve"))
                    sch.set(f"xr{par}_ve", sch.val("ve"))
                    if eng_name == "sync":
                        eng.wait_ge(ve, sch.val("ve"))
                        eng.dma_start(out=y[o * 128:(o + 1) * 128, sp2 * 512:(sp2 + 1) * 512],
                                      in_=yscr[par][:, :]).then_inc(st, 16)
                    sch.inc("st", 16)
                    sch.set(f"ysc{par}_st", sch.val("st"))

            if eng_name == "sync":
                eng.wait_ge(st, sch.val("st"))

        build_blocked(nc, gen, sems)
    return nc


S7A_SUPS = 32
S7A_CHUNKS = 64


def build_s7a(plain_first):
    nc = bass.Bass("TRN2", target_bir_lowering=False, debug=False)
    NH = 16
    xfull = nc.dram_tensor("xfull", [D, S], F32, kind="ExternalInput").ap()
    wqh = nc.dram_tensor("wqh", [D, 256], F32, kind="ExternalInput").ap()
    wkh = nc.dram_tensor("wkh", [D, 256], F32, kind="ExternalInput").ap()
    wvh = nc.dram_tensor("wvh", [D, 256], F32, kind="ExternalInput").ap()
    gng = nc.dram_tensor("gng", [2, 128], F32, kind="ExternalInput").ap()
    gnb = nc.dram_tensor("gnb", [2, 128], F32, kind="ExternalInput").ap()
    lng = nc.dram_tensor("lng", [NH, 128], F32, kind="ExternalInput").ap()
    lnb = nc.dram_tensor("lnb", [NH, 128], F32, kind="ExternalInput").ap()
    gam = nc.dram_tensor("gam", [1, 1], F32, kind="ExternalInput").ap()
    gny = nc.dram_tensor("gny", [256, S], BF16, kind="ExternalOutput").ap()

    ctx = ExitStack()
    with ctx:
        ps = [ctx.enter_context(nc.psum_tensor(f"ps{i}", [128, 512], F32)) for i in range(8)]
        ln = LNKit(nc, ctx, tw=512, banks=ps[0:4], gb=True)
        gn = LNKit(nc, ctx, tw=128, banks=ps[0:4], dmodel=256, tag="gn", gb=True)

        arena = ctx.enter_context(nc.sbuf_tensor("arena", [128, 24576], BF16))
        arena_f = arena.ap().bitcast(F32)
        q_fm = [ctx.enter_context(nc.sbuf_tensor(f"qfm{h2}", [128, S], BF16)) for h2 in range(2)]
        k_fm = [ctx.enter_context(nc.sbuf_tensor(f"kfm{h2}", [128, S], BF16)) for h2 in range(2)]
        v_tok = ctx.enter_context(nc.sbuf_tensor("v_tok", [128, 64 * 256], BF16))
        wq_sb = ctx.enter_context(nc.sbuf_tensor("wq_sb", [128, NH * 256], BF16))
        wk_sb = ctx.enter_context(nc.sbuf_tensor("wk_sb", [128, NH * 256], BF16))
        wv_sb = ctx.enter_context(nc.sbuf_tensor("wv_sb", [128, NH * 256], BF16))
        lngT = ctx.enter_context(nc.sbuf_tensor("lngT", [128, NH], F32))
        lnbT = ctx.enter_context(nc.sbuf_tensor("lnbT", [128, NH], F32))
        gngT = ctx.enter_context(nc.sbuf_tensor("gngT", [128, 2], F32))
        gnbT = ctx.enter_context(nc.sbuf_tensor("gnbT", [128, 2], F32))
        srow = ctx.enter_context(nc.sbuf_tensor("srow", [NH, 128], F32))
        identf = ctx.enter_context(nc.sbuf_tensor("identf", [128, 128], F32))
        identb = ctx.enter_context(nc.sbuf_tensor("identb", [128, 128], BF16))
        iden_io = ctx.enter_context(nc.sbuf_tensor("iden_io", [128, 128], F32))
        gamrow = ctx.enter_context(nc.sbuf_tensor("gamrow", [1, 8], F32))
        grow = ctx.enter_context(nc.sbuf_tensor("grow", [1, 256], F32))
        GQ = ctx.enter_context(nc.sbuf_tensor("GQ", [128, 128], BF16))
        KG = ctx.enter_context(nc.sbuf_tensor("KG", [128, 128], BF16))
        DTf = ctx.enter_context(nc.sbuf_tensor("DTf", [128, 128], F32))
        DTb = ctx.enter_context(nc.sbuf_tensor("DTb", [128, 128], BF16))
        R0 = ctx.enter_context(nc.sbuf_tensor("R0", [128, 128], F32))
        g128c = ctx.enter_context(nc.sbuf_tensor("g128c", [128, 1], F32))
        lngc = ctx.enter_context(nc.sbuf_tensor("lngc", [128, 1], F32))
        Ssb = ctx.enter_context(nc.sbuf_tensor("Ssb", [128, 512], F32))
        Sbf = ctx.enter_context(nc.sbuf_tensor("Sbf", [128, 512], BF16))
        qtb = ctx.enter_context(nc.sbuf_tensor("qtb", [128, 256], BF16))
        ktb = ctx.enter_context(nc.sbuf_tensor("ktb", [128, 256], BF16))
        kttk = ctx.enter_context(nc.sbuf_tensor("kttk", [128, 256], BF16))
        pbT = ctx.enter_context(nc.sbuf_tensor("pbT", [128, 128], BF16))
        ysb = ctx.enter_context(nc.sbuf_tensor("ysb", [128, 256], F32))
        gnt = ctx.enter_context(nc.sbuf_tensor("gnt", [128, 256], BF16))

        ld = ctx.enter_context(nc.semaphore("ld"))
        mm = ctx.enter_context(nc.semaphore("mm"))
        ve = ctx.enter_context(nc.semaphore("ve"))
        sc = ctx.enter_context(nc.semaphore("sc"))
        st = ctx.enter_context(nc.semaphore("st"))
        gp = ctx.enter_context(nc.semaphore("gp"))
        sems = {"ld": ld, "mm": mm, "ve": ve, "sc": sc, "st": st, "gp": gp}

        SW = 256
        def xsup(d):
            return arena_f[:, d * SW:(d + 1) * SW]

        def xnsup(d):
            return arena[:, 16384 + d * SW:16384 + (d + 1) * SW]

        def barrier(eng_name, eng, sch):
            for nm in ("mm", "ve", "sc", "ld", "st", "gp"):
                v = sch.val(nm)
                if v > 0:
                    eng.wait_ge(sems[nm], v)

        def gen(eng_name, eng, sch):
            ln.emit_init(eng_name, eng, sch)
            gn.emit_init(eng_name, eng, sch)
            if eng_name == "pool":
                eng.iota(R0[:, :], pattern=[[1, 128]], base=0, channel_multiplier=-1,
                         allow_small_or_imprecise_dtypes=True)
                eng.iota(iden_io[:, :], pattern=[[1, 128]], base=0, channel_multiplier=-1,
                         allow_small_or_imprecise_dtypes=True)
                eng.iota(grow[0:1, 0:128], pattern=[[1, 128]], base=1, channel_multiplier=0,
                         allow_small_or_imprecise_dtypes=True)
                eng.iota(grow[0:1, 128:256], pattern=[[-1, 128]], base=127, channel_multiplier=0,
                         allow_small_or_imprecise_dtypes=True).then_inc(gp, 1)
                eng.dma_start(out=gamrow[0:1, 0:1], in_=gam[:, :]).then_inc(ld, 16)
                eng.dma_start(out=srow[:, :], in_=lng[:, :]).then_inc(ld, 16)
                for d in range(NH):
                    eng.dma_start(out=wq_sb[:, d * 256:(d + 1) * 256],
                                  in_=wqh[d * 128:(d + 1) * 128, :]).then_inc(ld, 16)
                    eng.dma_start(out=wk_sb[:, d * 256:(d + 1) * 256],
                                  in_=wkh[d * 128:(d + 1) * 128, :]).then_inc(ld, 16)
                    eng.dma_start(out=wv_sb[:, d * 256:(d + 1) * 256],
                                  in_=wvh[d * 128:(d + 1) * 128, :]).then_inc(ld, 16)
            sch.inc("gp"); sch.inc("ld", 16 * (2 + 3 * NH))
            if eng_name == "act":
                eng.wait_ge(ld, sch.val("ld"))
                eng.wait_ge(gp, sch.val("gp"))
                eng.activation(gamrow[0:1, 1:2], gamrow[0:1, 0:1], AF.Ln).then_inc(sc, 1)
            sch.inc("sc")
            if eng_name == "dve":
                eng.wait_ge(sc, sch.val("sc"))
                eng.tensor_scalar(identf[:, :], iden_io[:, :], 0.0, None, ALU.is_equal)
                eng.tensor_copy(identb[:, :], identf[:, :])
                eng.tensor_scalar(grow[0:1, 0:128], grow[0:1, 0:128], gamrow[0:1, 1:2], None, ALU.mult)
                eng.tensor_scalar(grow[0:1, 128:256], grow[0:1, 128:256], gamrow[0:1, 1:2], None, ALU.mult)
                eng.tensor_scalar(gamrow[0:1, 2:3], gamrow[0:1, 1:2], 128.0, None,
                                  ALU.mult).then_inc(ve, 1)
            sch.inc("ve")
            if eng_name == "act":
                eng.wait_ge(ve, sch.val("ve"))
                eng.activation(grow[0:1, 0:128], grow[0:1, 0:128], AF.Exp)
                eng.activation(grow[0:1, 128:256], grow[0:1, 128:256], AF.Exp)
                eng.activation(gamrow[0:1, 3:4], gamrow[0:1, 2:3], AF.Exp).then_inc(sc, 1)
            sch.inc("sc")
            if eng_name == "pe":
                eng.wait_ge(sc, sch.val("sc"))
                eng.matmul(ps[4][:, 0:128], ln.ones_f[:, :], grow[0:1, 0:128], start=True, stop=True)
                eng.matmul(ps[5][:, 0:128], ln.ones_f[:, :], grow[0:1, 128:256], start=True, stop=True)
                eng.matmul(ps[6][:, 0:1], ln.ones_f[:, :], gamrow[0:1, 3:4], start=True, stop=True)
                eng.matmul(ps[6][:, 1:2], ln.ones_f[:, :], gamrow[0:1, 1:2], start=True, stop=True)
                eng.transpose(ps[7][0:128, 0:NH], srow[:, :], identf[0:NH, 0:NH]).then_inc(mm, 1)
            sch.inc("mm")
            if eng_name == "dve":
                eng.wait_ge(mm, sch.val("mm"))
                eng.tensor_copy(GQ[:, :], ps[4][:, 0:128])
                eng.tensor_copy(KG[:, :], ps[5][:, 0:128])
                eng.tensor_copy(g128c[:, :], ps[6][:, 0:1])
                eng.tensor_copy(lngc[:, :], ps[6][:, 1:2])
                eng.tensor_copy(lngT[:, :], ps[7][0:128, 0:NH])
                eng.memset(Ssb[:, :], 0.0)
                eng.tensor_scalar(DTf[:, :], R0[:, :], lngc[:, 0:1], None, ALU.mult).then_inc(ve, 1)
            sch.inc("ve")
            if eng_name == "act":
                eng.wait_ge(ve, sch.val("ve"))
                eng.activation(DTf[:, :], DTf[:, :], AF.Exp).then_inc(sc, 1)
            sch.inc("sc")
            if eng_name == "pool":
                eng.wait_ge(sc, sch.val("sc"))
                eng.affine_select(DTf[:, :], DTf[:, :], pattern=[[1, 128]], base=0,
                                  channel_multiplier=-1, compare_op=ALU.is_ge,
                                  fill=0.0).then_inc(gp, 1)
            sch.inc("gp")
            if eng_name == "dve":
                eng.wait_ge(gp, sch.val("gp"))
                eng.tensor_copy(DTb[:, :], DTf[:, :]).then_inc(ve, 1)
            sch.inc("ve")
            for which, (srcp, dstp, nr) in enumerate([(lnb, lnbT, NH), (gng, gngT, 2), (gnb, gnbT, 2)]):
                if eng_name == "pool":
                    eng.wait_ge(mm, sch.val("mm"))
                    eng.dma_start(out=srow[0:nr, :], in_=srcp[:, :]).then_inc(ld, 16)
                sch.inc("ld", 16)
                if eng_name == "pe":
                    eng.wait_ge(ld, sch.val("ld"))
                    eng.wait_ge(ve, sch.val("ve"))
                    eng.transpose(ps[7][0:128, 0:nr], srow[0:nr, :], identf[0:nr, 0:nr]).then_inc(mm, 1)
                sch.inc("mm")
                if eng_name == "dve":
                    eng.wait_ge(mm, sch.val("mm"))
                    eng.tensor_copy(dstp[:, 0:nr], ps[7][0:128, 0:nr]).then_inc(ve, 1)
                sch.inc("ve")

            lg_cols = [lngT[:, d:d + 1] for d in range(NH)]
            lb_cols = [lnbT[:, d:d + 1] for d in range(NH)]
            for sup in range(S7A_SUPS):
                barrier(eng_name, eng, sch)
                if eng_name == "sync":
                    for d in range(NH):
                        eng.dma_start(out=xsup(d), in_=xfull[d * 128:(d + 1) * 128,
                                                           sup * SW:(sup + 1) * SW]).then_inc(st, 16)
                sch.inc("st", 16 * NH)
                if plain_first:
                    ln.emit(eng_name, eng, sch, [xsup(d) for d in range(NH)], 0,
                            [xsup(d) for d in range(NH)], 0, tw=SW, xt_ready=("st", sch.val("st")))
                    barrier(eng_name, eng, sch)
                ln.emit(eng_name, eng, sch, [xsup(d) for d in range(NH)], 0,
                        [xnsup(d) for d in range(NH)], 0, tw=SW, xt_ready=("st", sch.val("st")),
                        gamma_col=lg_cols, beta_col=lb_cols)
                barrier(eng_name, eng, sch)
                for which in range(2):
                    wsb = wq_sb if which == 0 else wk_sb
                    dst = q_fm if which == 0 else k_fm
                    for h2 in range(2):
                        par = (which * 2 + h2) % 2
                        if eng_name == "pe":
                            eng.wait_ge(ve, sch.val("ve"))
                            g = sch.val(f"qk{par}_sc")
                            if g > 0:
                                eng.wait_ge(sc, g)
                            for d in range(NH):
                                ii = eng.matmul(ps[4 + par][:, 0:SW],
                                                wsb[:, d * 256 + h2 * 128:d * 256 + (h2 + 1) * 128],
                                                xnsup(d), start=(d == 0), stop=(d == NH - 1))
                            ii.then_inc(mm, 1)
                        sch.inc("mm")
                        if eng_name == "act":
                            eng.wait_ge(mm, sch.val("mm"))
                            eng.copy(dst[h2][:, sup * SW:(sup + 1) * SW],
                                     ps[4 + par][:, 0:SW]).then_inc(sc, 1)
                        sch.inc("sc")
                        sch.set(f"qk{par}_sc", sch.val("sc"))
                for tt in range(2):
                    par = tt % 2
                    if eng_name == "pe":
                        g = sch.val(f"vv{par}_sc")
                        if g > 0:
                            eng.wait_ge(sc, g)
                        for d in range(NH):
                            ii = eng.matmul(ps[6 + par][:, 0:256],
                                            xnsup(d)[:, tt * 128:(tt + 1) * 128],
                                            wv_sb[:, d * 256:(d + 1) * 256],
                                            start=(d == 0), stop=(d == NH - 1))
                        ii.then_inc(mm, 1)
                    sch.inc("mm")
                    if eng_name == "act":
                        eng.wait_ge(mm, sch.val("mm"))
                        eng.copy(v_tok[:, (sup * 2 + tt) * 256:(sup * 2 + tt + 1) * 256],
                                 ps[6 + par][:, 0:256]).then_inc(sc, 1)
                    sch.inc("sc")
                    sch.set(f"vv{par}_sc", sch.val("sc"))
            barrier(eng_name, eng, sch)

            gg_cols = [gngT[:, 0:1], gngT[:, 1:2]]
            gb_cols = [gnbT[:, 0:1], gnbT[:, 1:2]]
            for n in range(S7A_CHUNKS):
                barrier(eng_name, eng, sch)
                cn = slice(n * 128, (n + 1) * 128)
                if eng_name == "dve":
                    if n > 0:
                        eng.wait_ge(mm, sch.val("ret_mm"))
                    eng.tensor_tensor(qtb[:, 0:128], q_fm[0][:, cn], GQ[:, :], ALU.mult)
                    eng.tensor_tensor(ktb[:, 0:128], k_fm[0][:, cn], KG[:, :], ALU.mult)
                    eng.tensor_tensor(qtb[:, 128:256], q_fm[1][:, cn], GQ[:, :], ALU.mult)
                    eng.tensor_tensor(ktb[:, 128:256], k_fm[1][:, cn], KG[:, :], ALU.mult)
                    eng.tensor_copy(Sbf[:, :], Ssb[:, :]).then_inc(ve, 1)
                sch.inc("ve")
                if eng_name == "pe":
                    eng.wait_ge(ve, sch.val("ve"))
                    g = sch.val("ps4_ve")
                    if g > 0:
                        eng.wait_ge(ve, g)
                    eng.matmul(ps[4][:, 0:128], k_fm[0][:, cn], q_fm[0][:, cn], start=True, stop=False)
                    eng.matmul(ps[4][:, 0:128], k_fm[1][:, cn], q_fm[1][:, cn],
                               start=False, stop=True).then_inc(mm, 1)
                sch.inc("mm")
                if eng_name == "dve":
                    eng.wait_ge(mm, sch.val("mm"))
                    eng.tensor_tensor(pbT[:, :], ps[4][:, 0:128], DTb[:, :], ALU.mult).then_inc(ve, 1)
                sch.inc("ve")
                if eng_name == "pe":
                    eng.wait_ge(ve, sch.val("ve"))
                    for dvh in range(2):
                        r0 = dvh * 128
                        eng.matmul(ps[4][:, r0:r0 + 128],
                                   v_tok[:, n * 256 + dvh * 128:n * 256 + (dvh + 1) * 128],
                                   pbT[:, :], start=True, stop=False)
                        eng.matmul(ps[4][:, r0:r0 + 128],
                                   Sbf[:, 0 * 256 + dvh * 128:0 * 256 + (dvh + 1) * 128],
                                   qtb[:, 0:128], start=False, stop=False)
                        ii = eng.matmul(ps[4][:, r0:r0 + 128],
                                        Sbf[:, 1 * 256 + dvh * 128:1 * 256 + (dvh + 1) * 128],
                                        qtb[:, 128:256], start=False, stop=True)
                    ii.then_inc(mm, 1)
                sch.inc("mm")
                if eng_name == "act":
                    eng.wait_ge(mm, sch.val("mm"))
                    eng.copy(ysb[:, 0:256], ps[4][:, 0:256]).then_inc(sc, 1)
                sch.inc("sc")
                if eng_name == "dve":
                    g = sch.val("gnt_st")
                    if g > 0:
                        eng.wait_ge(st, g)
                gn.emit(eng_name, eng, sch, [ysb[:, 0:128], ysb[:, 128:256]], 0,
                        [gnt[:, 0:128], gnt[:, 128:256]], 0, tw=128, nch=2,
                        gamma_col=gg_cols, beta_col=gb_cols,
                        xt_ready=("sc", sch.val("sc")))
                if eng_name == "sync":
                    eng.wait_ge(ve, sch.val("ve"))
                    eng.dma_start(out=gny[0:128, cn], in_=gnt[:, 0:128]).then_inc(st, 16)
                    eng.dma_start(out=gny[128:256, cn], in_=gnt[:, 128:256]).then_inc(st, 16)
                sch.inc("st", 32)
                sch.set("gnt_st", sch.val("st"))
                for h2 in range(2):
                    if eng_name == "pe":
                        eng.wait_ge(ve, sch.val("ve"))
                        g = sch.val("ktr7_sc")
                        if g > 0:
                            eng.wait_ge(sc, g)
                        eng.transpose(ps[7].ap().bitcast(BF16)[:, 0:128],
                                      ktb[:, h2 * 128:(h2 + 1) * 128], identb[:, :]).then_inc(mm, 1)
                    sch.inc("mm")
                    if eng_name == "act":
                        eng.wait_ge(mm, sch.val("mm"))
                        eng.copy(kttk[:, h2 * 128:(h2 + 1) * 128],
                                 ps[7].ap().bitcast(BF16)[:, 0:128]).then_inc(sc, 1)
                    sch.inc("sc")
                    sch.set("ktr7_sc", sch.val("sc"))
                if eng_name == "pe":
                    eng.wait_ge(sc, sch.val("sc"))
                    for dkh in range(2):
                        ii = eng.matmul(ps[4][:, dkh * 256:(dkh + 1) * 256],
                                        kttk[:, dkh * 128:(dkh + 1) * 128],
                                        v_tok[:, n * 256:(n + 1) * 256],
                                        start=True, stop=True)
                    ii.then_inc(mm, 1)
                sch.inc("mm")
                if eng_name == "dve":
                    eng.wait_ge(mm, sch.val("mm"))
                    eng.scalar_tensor_tensor(Ssb[:, :], Ssb[:, :], g128c[:, 0:1], ps[4][:, 0:512],
                                             ALU.mult, ALU.add).then_inc(ve, 1)
                sch.inc("ve")
                sch.set("ps4_ve", sch.val("ve"))
                sch.set("ret_mm", sch.val("mm"))

            if eng_name == "sync":
                eng.wait_ge(st, sch.val("st"))

        build_blocked(nc, gen, sems)
    return nc


def build_s7b(plain_first, add_x):
    nc = bass.Bass("TRN2", target_bir_lowering=False, debug=False)
    NH = 16
    xin = nc.dram_tensor("xin", [D, 1024], F32, kind="ExternalInput").ap()
    gnyin = nc.dram_tensor("gnyin", [D, 1024], BF16, kind="ExternalInput").ap()
    wgslab = nc.dram_tensor("wgslab", [NH, 128, 2048], F32, kind="ExternalInput").ap()
    woslab = nc.dram_tensor("woslab", [NH, 128, 2048], F32, kind="ExternalInput").ap()
    w1slab = nc.dram_tensor("w1slab", [64, 128, 2048], BF16, kind="ExternalInput").ap()
    w2slab = nc.dram_tensor("w2slab", [NH, 128, 8192], BF16, kind="ExternalInput").ap()
    b1in = nc.dram_tensor("b1in", [64, 128], F32, kind="ExternalInput").ap()
    b2in = nc.dram_tensor("b2in", [NH, 128], F32, kind="ExternalInput").ap()
    lng = nc.dram_tensor("lng", [NH, 128], F32, kind="ExternalInput").ap()
    lnb = nc.dram_tensor("lnb", [NH, 128], F32, kind="ExternalInput").ap()
    if add_x:
        xres_in = nc.dram_tensor("xres", [D, 1024], F32, kind="ExternalInput").ap()
    y = nc.dram_tensor("y", [D, 1024], F32, kind="ExternalOutput").ap()

    ctx = ExitStack()
    with ctx:
        ps = [ctx.enter_context(nc.psum_tensor(f"ps{i}", [128, 512], F32)) for i in range(8)]
        ln = LNKit(nc, ctx, tw=512, banks=ps[0:4], gb=True)

        curs = [ctx.enter_context(nc.sbuf_tensor(f"curs{d}", [128, 1024], F32)) for d in range(NH)]
        xn2 = [ctx.enter_context(nc.sbuf_tensor(f"xn2_{d}", [128, 1024], BF16)) for d in range(NH)]
        prod = [ctx.enter_context(nc.sbuf_tensor(f"prod{d}", [128, 1024], BF16)) for d in range(NH)]
        warena = ctx.enter_context(nc.sbuf_tensor("warena", [128, 20480], BF16))
        gnyb = [ctx.enter_context(nc.sbuf_tensor(f"gnyb{p}", [128, 1024], BF16)) for p in range(2)]
        xrb = [ctx.enter_context(nc.sbuf_tensor(f"xrb{p}", [128, 256], F32)) for p in range(2)]
        gsil = ctx.enter_context(nc.sbuf_tensor("gsil", [128, 1024], BF16))
        yscr = [ctx.enter_context(nc.sbuf_tensor(f"yscr{p}", [128, 256], F32)) for p in range(2)]
        b1T = ctx.enter_context(nc.sbuf_tensor("b1T", [128, 64], F32))
        b2T = ctx.enter_context(nc.sbuf_tensor("b2T", [128, NH], F32))
        lngT = ctx.enter_context(nc.sbuf_tensor("lngT", [128, NH], F32))
        lnbT = ctx.enter_context(nc.sbuf_tensor("lnbT", [128, NH], F32))
        srow = ctx.enter_context(nc.sbuf_tensor("srow", [64, 128], F32))
        iden_io = ctx.enter_context(nc.sbuf_tensor("iden_io", [128, 128], F32))
        identf = iden_io

        ld = ctx.enter_context(nc.semaphore("ld"))
        mm = ctx.enter_context(nc.semaphore("mm"))
        ve = ctx.enter_context(nc.semaphore("ve"))
        sc = ctx.enter_context(nc.semaphore("sc"))
        st = ctx.enter_context(nc.semaphore("st"))
        gp = ctx.enter_context(nc.semaphore("gp"))
        sems = {"ld": ld, "mm": mm, "ve": ve, "sc": sc, "st": st, "gp": gp}

        def wg_buf(p):
            return warena[:, p * 2048:(p + 1) * 2048]

        def w1buf(p):
            return warena[:, p * 2048:(p + 1) * 2048]

        def w2buf(p):
            return warena[:, 4096 + p * 8192:4096 + (p + 1) * 8192]

        def h1t(o):
            return prod[o // 4][:, (o % 4) * 256:(o % 4 + 1) * 256]

        def barrier(eng_name, eng, sch):
            for nm in ("mm", "ve", "sc", "ld", "st", "gp"):
                v = sch.val(nm)
                if v > 0:
                    eng.wait_ge(sems[nm], v)

        def gen(eng_name, eng, sch):
            ln.emit_init(eng_name, eng, sch)
            if eng_name == "pool":
                eng.iota(iden_io[:, :], pattern=[[1, 128]], base=0, channel_multiplier=-1,
                         allow_small_or_imprecise_dtypes=True).then_inc(gp, 1)
            sch.inc("gp")
            if eng_name == "dve":
                eng.wait_ge(gp, sch.val("gp"))
                eng.tensor_scalar(iden_io[:, :], iden_io[:, :], 0.0, None, ALU.is_equal).then_inc(ve, 1)
            sch.inc("ve")
            for which, (srcp, dstp, nr) in enumerate([(b1in, b1T, 64), (b2in, b2T, NH),
                                                      (lng, lngT, NH), (lnb, lnbT, NH)]):
                if eng_name == "pool":
                    if which > 0:
                        eng.wait_ge(mm, sch.val("mm"))
                    eng.dma_start(out=srow[0:nr, :], in_=srcp[:, :]).then_inc(ld, 16)
                sch.inc("ld", 16)
                if eng_name == "pe":
                    eng.wait_ge(ld, sch.val("ld"))
                    eng.wait_ge(ve, sch.val("ve"))
                    g = sch.val("tr_ve")
                    if g > 0:
                        eng.wait_ge(ve, g)
                    eng.transpose(ps[4][0:128, 0:nr], srow[0:nr, :], identf[0:nr, 0:nr]).then_inc(mm, 1)
                sch.inc("mm")
                if eng_name == "dve":
                    eng.wait_ge(mm, sch.val("mm"))
                    eng.tensor_copy(dstp[:, 0:nr], ps[4][0:128, 0:nr]).then_inc(ve, 1)
                sch.inc("ve")
                sch.set("tr_ve", sch.val("ve"))
            if eng_name == "sync":
                for d in range(NH):
                    eng.dma_start(out=curs[d][:, :], in_=xin[d * 128:(d + 1) * 128, :]).then_inc(st, 16)
            sch.inc("st", 16 * NH)

            lg_cols = [lngT[:, d:d + 1] for d in range(NH)]
            lb_cols = [lnbT[:, d:d + 1] for d in range(NH)]
            for sp2 in range(2):
                if plain_first:
                    ln.emit(eng_name, eng, sch, curs, sp2 * 512, curs, sp2 * 512,
                            xt_ready=("st", sch.val("st")))
                ln.emit(eng_name, eng, sch, curs, sp2 * 512, xn2, sp2 * 512,
                        xt_ready=("st", sch.val("st")), gamma_col=lg_cols, beta_col=lb_cols)
            barrier(eng_name, eng, sch)

            for o in range(NH):
                par = o % 2
                if eng_name == "pool":
                    g = sch.val(f"wgb{par}_mm")
                    if g > 0:
                        eng.wait_ge(mm, g)
                    eng.dma_start(out=wg_buf(par), in_=wgslab[o]).then_inc(ld, 16)
                sch.inc("ld", 16)
                if eng_name == "sync":
                    g = sch.val(f"gnyb{par}_ve")
                    if g > 0:
                        eng.wait_ge(ve, g)
                    eng.dma_start(out=gnyb[par][:, :], in_=gnyin[o * 128:(o + 1) * 128, :]).then_inc(st, 16)
                sch.inc("st", 16)
                for sp2 in range(2):
                    pp = 4 + sp2
                    if eng_name == "pe":
                        eng.wait_ge(ld, sch.val("ld"))
                        g = sch.val(f"gps{sp2}_sc")
                        if g > 0:
                            eng.wait_ge(sc, g)
                        for d in range(NH):
                            ii = eng.matmul(ps[pp][:, :], wg_buf(par)[:, d * 128:(d + 1) * 128],
                                            xn2[d][:, sp2 * 512:(sp2 + 1) * 512],
                                            start=(d == 0), stop=(d == NH - 1))
                        ii.then_inc(mm, 1)
                    sch.inc("mm")
                    sch.set(f"wgb{par}_mm", sch.val("mm"))
                    if eng_name == "act":
                        eng.wait_ge(mm, sch.val("mm"))
                        g = sch.val(f"gsil{sp2}_ve")
                        if g > 0:
                            eng.wait_ge(ve, g)
                        eng.activation(gsil[:, sp2 * 512:(sp2 + 1) * 512], ps[pp][:, :],
                                       AF.Silu).then_inc(sc, 1)
                    sch.inc("sc")
                    sch.set(f"gps{sp2}_sc", sch.val("sc"))
                    if eng_name == "dve":
                        eng.wait_ge(sc, sch.val("sc"))
                        eng.wait_ge(st, sch.val("st"))
                        eng.tensor_tensor(prod[o][:, sp2 * 512:(sp2 + 1) * 512],
                                          gsil[:, sp2 * 512:(sp2 + 1) * 512],
                                          gnyb[par][:, sp2 * 512:(sp2 + 1) * 512],
                                          ALU.mult).then_inc(ve, 1)
                    sch.inc("ve")
                    sch.set(f"gsil{sp2}_ve", sch.val("ve"))
                sch.set(f"gnyb{par}_ve", sch.val("ve"))
            barrier(eng_name, eng, sch)

            for o in range(NH):
                par = o % 2
                if eng_name == "pool":
                    g = sch.val(f"wgb{par}_mm")
                    if g > 0:
                        eng.wait_ge(mm, g)
                    eng.dma_start(out=wg_buf(par), in_=woslab[o]).then_inc(ld, 16)
                sch.inc("ld", 16)
                for sp2 in range(2):
                    pp = 4 + sp2
                    if eng_name == "pe":
                        eng.wait_ge(ld, sch.val("ld"))
                        g = sch.val(f"gps{sp2}_sc")
                        if g > 0:
                            eng.wait_ge(ve, g)
                        for d in range(NH):
                            ii = eng.matmul(ps[pp][:, :], wg_buf(par)[:, d * 128:(d + 1) * 128],
                                            prod[d][:, sp2 * 512:(sp2 + 1) * 512],
                                            start=(d == 0), stop=(d == NH - 1))
                        ii.then_inc(mm, 1)
                    sch.inc("mm")
                    sch.set(f"wgb{par}_mm", sch.val("mm"))
                    if eng_name == "dve":
                        eng.wait_ge(mm, sch.val("mm"))
                        eng.tensor_tensor(curs[o][:, sp2 * 512:(sp2 + 1) * 512],
                                          curs[o][:, sp2 * 512:(sp2 + 1) * 512],
                                          ps[pp][:, :], ALU.add).then_inc(ve, 1)
                    sch.inc("ve")
                    sch.set(f"gps{sp2}_sc", sch.val("ve"))
            barrier(eng_name, eng, sch)
            sch.set("gps0_sc", 0); sch.set("gps1_sc", 0)

            for sp2 in range(2):
                ln.emit(eng_name, eng, sch, curs, sp2 * 512, xn2, sp2 * 512,
                        xn_free_mm=sch.val("mm"), gamma_col=lg_cols, beta_col=lb_cols)
            barrier(eng_name, eng, sch)

            for sp in range(4):
                c0, c1 = sp * 256, (sp + 1) * 256
                for o in range(64):
                    par = o % 2
                    if eng_name == "pool":
                        g = sch.val(f"w1b{par}_mm")
                        if g > 0:
                            eng.wait_ge(mm, g)
                        eng.dma_start(out=w1buf(par), in_=w1slab[o]).then_inc(ld, 16)
                    sch.inc("ld", 16)
                    if eng_name == "pe":
                        eng.wait_ge(ld, sch.val("ld"))
                        g = sch.val(f"h1ps{par}_sc")
                        if g > 0:
                            eng.wait_ge(sc, g)
                        if sp == 0 and o == 0:
                            eng.wait_ge(ve, sch.val("ve"))
                        for d in range(NH):
                            ii = eng.matmul(ps[4 + par][:, 0:256], w1buf(par)[:, d * 128:(d + 1) * 128],
                                            xn2[d][:, c0:c1], start=(d == 0), stop=(d == NH - 1))
                        ii.then_inc(mm, 1)
                    sch.inc("mm")
                    sch.set(f"w1b{par}_mm", sch.val("mm"))
                    if eng_name == "act":
                        eng.wait_ge(mm, sch.val("mm"))
                        g = sch.val(f"h1c_mm")
                        if g > 0 and o < 2:
                            eng.wait_ge(mm, g)
                        eng.activation(h1t(o), ps[4 + par][:, 0:256], AF.Gelu_apprx_tanh,
                                       bias=b1T[:, o:o + 1]).then_inc(sc, 1)
                    sch.inc("sc")
                    sch.set(f"h1ps{par}_sc", sch.val("sc"))
                for o2 in range(NH):
                    par = o2 % 2
                    if eng_name == "pool":
                        g = sch.val(f"w2b{par}_mm")
                        if g > 0:
                            eng.wait_ge(mm, g)
                        eng.dma_start(out=w2buf(par), in_=w2slab[o2]).then_inc(ld, 16)
                    sch.inc("ld", 16)
                    if add_x:
                        if eng_name == "sync":
                            g = sch.val(f"xrb{par}_ve")
                            if g > 0:
                                eng.wait_ge(ve, g)
                            eng.dma_start(out=xrb[par][:, 0:256],
                                          in_=xres_in[o2 * 128:(o2 + 1) * 128, c0:c1]).then_inc(st, 16)
                        sch.inc("st", 16)
                    if eng_name == "pe":
                        eng.wait_ge(ld, sch.val("ld"))
                        if o2 == 0:
                            eng.wait_ge(sc, sch.val("sc"))
                        g = sch.val(f"y2ps{par}_ve")
                        if g > 0:
                            eng.wait_ge(ve, g)
                        for dd in range(64):
                            ii = eng.matmul(ps[6 + par][:, 0:256], w2buf(par)[:, dd * 128:(dd + 1) * 128],
                                            h1t(dd), start=(dd == 0), stop=(dd == 63))
                        ii.then_inc(mm, 1)
                    sch.inc("mm")
                    sch.set(f"w2b{par}_mm", sch.val("mm"))
                    if eng_name == "dve":
                        eng.wait_ge(mm, sch.val("mm"))
                        if add_x:
                            eng.wait_ge(st, sch.val("st"))
                        g = sch.val(f"ys{par}_st")
                        if g > 0:
                            eng.wait_ge(st, g)
                        eng.scalar_tensor_tensor(yscr[par][:, :], ps[6 + par][:, 0:256],
                                                 b2T[:, o2:o2 + 1], curs[o2][:, c0:c1],
                                                 ALU.add, ALU.add)
                        if add_x:
                            ii = eng.tensor_tensor(yscr[par][:, :], yscr[par][:, :],
                                                   xrb[par][:, 0:256], ALU.add)
                            ii.then_inc(ve, 1)
                        else:
                            eng.tensor_copy(yscr[par][:, :], yscr[par][:, :]).then_inc(ve, 1)
                    sch.inc("ve")
                    sch.set(f"y2ps{par}_ve", sch.val("ve"))
                    if add_x:
                        sch.set(f"xrb{par}_ve", sch.val("ve"))
                    if eng_name == "sync":
                        eng.wait_ge(ve, sch.val("ve"))
                        eng.dma_start(out=y[o2 * 128:(o2 + 1) * 128, c0:c1],
                                      in_=yscr[par][:, :]).then_inc(st, 16)
                    sch.inc("st", 16)
                    sch.set(f"ys{par}_st", sch.val("st"))
                sch.set("h1c_mm", sch.val("mm"))

            if eng_name == "sync":
                eng.wait_ge(st, sch.val("st"))

        build_blocked(nc, gen, sems)
    return nc

GAMMAS_RET = np.asarray(1.0 - np.exp(np.linspace(np.log(1.0 / 32), np.log(1.0 / 512), 8)),
                        np.float32)


def core_cols(c):
    a, b = BLOCKS_OF_CORE[c]
    return [(a * BLK, (a + 1) * BLK), (b * BLK, (b + 1) * BLK)]


def scatter_full(per_core):
    out = np.zeros((D, S), np.float32)
    for c in range(NCORES):
        (a0, a1), (b0, b1) = core_cols(c)
        out[:, a0:a1] = per_core[c][:, 0:BLK]
        out[:, b0:b1] = per_core[c][:, BLK:2 * BLK]
    return out


def slice_core(xfm, c):
    (a0, a1), (b0, b1) = core_cols(c)
    return np.ascontiguousarray(np.concatenate([xfm[:, a0:a1], xfm[:, b0:b1]], axis=1))


def host_slab16(W):
    return np.ascontiguousarray(
        np.asarray(W).reshape(16, 128, 16, 128).transpose(2, 1, 0, 3).reshape(16, 128, 2048))


_NC_CACHE = {}


def _get_nc(name, builder, *args):
    if name not in _NC_CACHE:
        _NC_CACHE[name] = builder(*args)
    return _NC_CACHE[name]


def _qpos(c):
    (a0, a1), (b0, b1) = core_cols(c)
    return np.concatenate([np.arange(a0, a1), np.arange(b0, b1)]).astype(np.float32).reshape(1, 1024)


def _launch(nc, in_maps):
    res = run_bass_kernel_spmd(nc, in_maps, core_ids=list(range(NCORES)))
    return res.results


def _run_s7_layer(i, cur_fm, plain_first, add_x, inp, slabs, x_res_fm, lng, lnb):
    nca = _get_nc(f"s7a_{plain_first}", build_s7a, plain_first)
    Wq = np.asarray(inp["L1_attn_b_Wq"][i])
    Wk = np.asarray(inp["L1_attn_b_Wk"][i])
    Wv = np.asarray(inp["L1_attn_b_Wv"][i])
    gg = np.asarray(inp["L1_attn_b_gn_gamma"][i]).reshape(8, 2, 128).astype(np.float32)
    gb = np.asarray(inp["L1_attn_b_gn_beta"][i]).reshape(8, 2, 128).astype(np.float32)
    in_maps = []
    for c in range(NCORES):
        in_maps.append({
            "xfull": cur_fm,
            "wqh": np.ascontiguousarray(Wq[:, c * 256:(c + 1) * 256]),
            "wkh": np.ascontiguousarray(Wk[:, c * 256:(c + 1) * 256]),
            "wvh": np.ascontiguousarray(Wv[:, c * 256:(c + 1) * 256]),
            "gng": np.ascontiguousarray(gg[c]), "gnb": np.ascontiguousarray(gb[c]),
            "lng": lng, "lnb": lnb,
            "gam": np.array([[GAMMAS_RET[c]]], np.float32),
        })
    res = _launch(nca, in_maps)
    gnY = np.concatenate([res[c]["gny"] for c in range(NCORES)], axis=0)

    ncb = _get_nc(f"s7b_{plain_first}_{add_x}", build_s7b, plain_first, add_x)
    wgs = host_slab16(inp["L1_attn_b_Wg"][i])
    wos = host_slab16(inp["L1_attn_b_Wo"][i])
    w1s = slabs["w1a" if i == 0 else "w1b"]
    w2s = slabs["w2a" if i == 0 else "w2b"]
    b1 = np.asarray(inp["L1_attn_b_b1"][i]).reshape(64, 128).astype(np.float32)
    b2 = np.asarray(inp["L1_attn_b_b2"][i]).reshape(16, 128).astype(np.float32)
    in_maps2 = []
    for c in range(NCORES):
        (ca0, ca1), (cb0, cb1) = core_cols(c)
        gslice = np.ascontiguousarray(np.concatenate([gnY[:, ca0:ca1], gnY[:, cb0:cb1]], axis=1))
        m = {
            "xin": slice_core(cur_fm, c), "gnyin": gslice,
            "wgslab": wgs, "woslab": wos, "w1slab": w1s, "w2slab": w2s,
            "b1in": b1, "b2in": b2, "lng": lng, "lnb": lnb,
        }
        if add_x:
            m["xres"] = slice_core(x_res_fm, c)
        in_maps2.append(m)
    res2 = _launch(ncb, in_maps2)
    return scatter_full([res2[c]["y"] for c in range(NCORES)])


def kernel(**inputs):
    inp = {k: np.asarray(v) for k, v in inputs.items()}

    ncp = _get_nc("prep", build_prep)
    slabs = host_prep_assemble(_launch(ncp, host_prep_inputs(inp)))

    xfm = np.ascontiguousarray(inp["L0_attn_a_x"][0].astype(np.float32).T)

    nc1 = _get_nc("s1", build_s1)
    bo1 = np.asarray(inp["L0_attn_a_bo"]).reshape(16, 128).astype(np.float32)
    in_maps = [{"xs": slice_core(xfm, c), "xa": xfm, "qpos": _qpos(c),
                "wq": inp["L0_attn_a_Wq"], "wkv": inp["L0_attn_a_Wkv"],
                "wo": inp["L0_attn_a_Wo"], "bo": bo1} for c in range(NCORES)]
    xfm = scatter_full([r["y"] for r in _launch(nc1, in_maps)])

    ncf0 = _get_nc("ffn44", build_ffn, 44, True)
    gs = np.asarray(inp["L0_ffn_gate_s"]).reshape(44, 128).astype(np.float32)
    us = np.asarray(inp["L0_ffn_up_s"]).reshape(44, 128).astype(np.float32)
    ds = np.asarray(inp["L0_ffn_down_s"]).reshape(16, 128).astype(np.float32)
    in_maps = [{"xs": slice_core(xfm, c), "gslab": slabs["gw"], "uslab": slabs["uw"],
                "dslab": slabs["dw"], "gs": gs, "us": us, "ds": ds} for c in range(NCORES)]
    xfm = scatter_full([r["y"] for r in _launch(ncf0, in_maps)])

    nc3 = _get_nc("s3", build_s3)
    in_maps = [{"xs": slice_core(xfm, c), "xa": xfm, "qpos": _qpos(c),
                "bq": slabs["bq"], "bk": slabs["bk"], "bv": slabs["bv"],
                "bo": slabs["bo"]} for c in range(NCORES)]
    xfm = scatter_full([r["y"] for r in _launch(nc3, in_maps)])

    in_maps = [{"xs": slice_core(xfm, c), "gslab": slabs["gw"], "uslab": slabs["uw"],
                "dslab": slabs["dw"], "gs": gs, "us": us, "ds": ds} for c in range(NCORES)]
    xfm = scatter_full([r["y"] for r in _launch(ncf0, in_maps)])

    nc5 = _get_nc("s5", build_s5)
    bo5 = np.asarray(inp["L1_attn_a_bo"]).reshape(16, 128).astype(np.float32)
    in_maps = [{"xs": slice_core(xfm, c), "xa": xfm,
                "wq": inp["L1_attn_a_Wq"], "wk": inp["L1_attn_a_Wk"],
                "wv": inp["L1_attn_a_Wv"], "wo": inp["L1_attn_a_Wo"],
                "bo": bo5} for c in range(NCORES)]
    xfm = scatter_full([r["y"] for r in _launch(nc5, in_maps)])

    ncf1 = _get_nc("ffn43", build_ffn, 43, False)
    in_maps = [{"xs": slice_core(xfm, c), "gslab": slabs["gg"], "uslab": slabs["gu"],
                "dslab": slabs["gd"]} for c in range(NCORES)]
    xfm = scatter_full([r["y"] for r in _launch(ncf1, in_maps)])

    lng = np.asarray(inp["L1_attn_b_ln_gamma"]).reshape(16, 128).astype(np.float32)
    lnb = np.asarray(inp["L1_attn_b_ln_beta"]).reshape(16, 128).astype(np.float32)
    x6fm = xfm
    cur = _run_s7_layer(0, x6fm, True, False, inp, slabs, None, lng, lnb)
    xfm = _run_s7_layer(1, np.ascontiguousarray(cur), False, True, inp, slabs, x6fm, lng, lnb)

    in_maps = [{"xs": slice_core(xfm, c), "gslab": slabs["gg"], "uslab": slabs["gu"],
                "dslab": slabs["gd"]} for c in range(NCORES)]
    xfm = scatter_full([r["y"] for r in _launch(ncf1, in_maps)])

    return np.ascontiguousarray(xfm.T).reshape(1, S, D).astype(np.float32)


# version 04
# speedup vs baseline: 1.1487x; 1.1487x over previous
from contextlib import ExitStack
import numpy as np
import ml_dtypes
import concourse.bass as bass
import concourse.mybir as mybir
from concourse.bass_utils import run_bass_kernel_spmd

F32 = mybir.dt.float32
BF16 = mybir.dt.bfloat16
I32 = mybir.dt.int32
AF = mybir.ActivationFunctionType
ALU = mybir.AluOpType

NCORES = 8
S = 8192
D = 2048
NBLK = 16
BLK = 512
BLOCKS_OF_CORE = [(c, 15 - c) for c in range(NCORES)]

ENGS = ("sync", "pe", "dve", "act", "pool")


class Sched:
    def __init__(self):
        self.c = {}

    def inc(self, name, n=1):
        self.c[name] = self.c.get(name, 0) + n
        return self.c[name]

    def val(self, name):
        return self.c.get(name, 0)

    def set(self, name, v):
        self.c[name] = v


def build_blocked(nc, gen, sems=None):
    def mk():
        s = Sched()
        s.sems = sems
        return s
    with nc.Block() as block:
        block.sync(lambda e: gen("sync", e, mk()))
        block.tensor(lambda e: gen("pe", e, mk()))
        block.vector(lambda e: gen("dve", e, mk()))
        block.scalar(lambda e: gen("act", e, mk()))
        block.gpsimd(lambda e: gen("pool", e, mk()))
    return nc


def ceil_div(a, b):
    return -(-a // b)


PREP_MATS = [
    ("gw", 768, 2048, False),
    ("uw", 768, 2048, False),
    ("dw", 256, 5632, False),
    ("bq", 256, 2048, False),
    ("bk", 64, 2048, False),
    ("bv", 64, 2048, False),
    ("bo", 256, 2048, False),
    ("gg", 768, 2048, True),
    ("gu", 768, 2048, True),
    ("gd", 256, 5504, True),
    ("w1a", 1024, 2048, False),
    ("w1b", 1024, 2048, False),
    ("w2a", 256, 8192, False),
    ("w2b", 256, 8192, False),
]


def build_prep():
    nc = bass.Bass("TRN2", target_bir_lowering=False, debug=False)
    ins = {}
    outs = {}
    for name, rows, cols, is_ghost in PREP_MATS:
        dt = I32 if is_ghost else F32
        ins[name] = nc.dram_tensor("in_" + name, [rows, cols], dt, kind="ExternalInput").ap()
        prow = min(rows, 128)
        ns = ceil_div(rows, 128)
        ic = cols // 128
        outs[name] = nc.dram_tensor("out_" + name, [ns, 128, ic * prow], BF16,
                                    kind="ExternalOutput").ap()
    lut = nc.dram_tensor("in_lut", [1, 16], F32, kind="ExternalInput").ap()

    ctx = ExitStack()
    with ctx:
        wrow = ctx.enter_context(nc.sbuf_tensor("wrow", [128, 8192], BF16))
        slab = ctx.enter_context(nc.sbuf_tensor("slab", [128, 8192], BF16))
        i32t = ctx.enter_context(nc.sbuf_tensor("i32t", [128, 5504], I32))
        idxb = ctx.enter_context(nc.sbuf_tensor("idxb", [128, 5504], BF16))
        tmpb = ctx.enter_context(nc.sbuf_tensor("tmpb", [128, 5504], BF16))
        ident = ctx.enter_context(nc.sbuf_tensor("ident", [128, 128], BF16))
        iden_io = ctx.enter_context(nc.sbuf_tensor("iden_io", [128, 128], F32))
        lutrow = ctx.enter_context(nc.sbuf_tensor("lutrow", [1, 16], F32))
        ones1 = ctx.enter_context(nc.sbuf_tensor("ones1", [1, 128], F32))
        lutB = ctx.enter_context(nc.sbuf_tensor("lutB", [128, 16], F32))
        pst = [ctx.enter_context(nc.psum_tensor(f"pst{i}", [128, 128], BF16)) for i in range(4)]
        psb = ctx.enter_context(nc.psum_tensor("psb", [128, 16], F32))

        ld = ctx.enter_context(nc.semaphore("ld"))
        mm = ctx.enter_context(nc.semaphore("mm"))
        ve = ctx.enter_context(nc.semaphore("ve"))
        st = ctx.enter_context(nc.semaphore("st"))

        def gen(eng_name, eng, sch):
            if eng_name == "pool":
                eng.iota(iden_io[:, :], pattern=[[1, 128]], base=0, channel_multiplier=-1,
                         allow_small_or_imprecise_dtypes=True).then_inc(ve, 1)
                eng.dma_start(out=lutrow[:, :], in_=lut[:, :]).then_inc(ld, 16)
            sch.inc("ve"); sch.inc("ld", 16)
            if eng_name == "dve":
                eng.wait_ge(ve, sch.val("ve"))
                eng.memset(ones1[:, :], 1.0)
                eng.tensor_scalar(ident[:, :], iden_io[:, :], 0.0, None, ALU.is_equal).then_inc(ve, 1)
            sch.inc("ve")
            if eng_name == "pe":
                eng.wait_ge(ve, sch.val("ve"))
                eng.wait_ge(ld, sch.val("ld"))
                eng.matmul(psb[:, :], ones1[:, :], lutrow[:, :], start=True, stop=True).then_inc(mm, 1)
            sch.inc("mm")
            if eng_name == "dve":
                eng.wait_ge(mm, sch.val("mm"))
                eng.tensor_copy(lutB[:, :], psb[:, :]).then_inc(ve, 1)
            sch.inc("ve")

            for name, rows, cols, is_ghost in PREP_MATS:
                prow = min(rows, 128)
                ns = ceil_div(rows, 128)
                ic = cols // 128
                for o in range(ns):
                    src = ins[name][o * 128:o * 128 + prow, :]
                    if is_ghost:
                        if eng_name == "pool":
                            eng.wait_ge(ve, sch.val("ve_idx_free"))
                            eng.dma_start(out=i32t[0:prow, 0:cols], in_=src).then_inc(ld, 16)
                        sch.inc("ld", 16)
                        if eng_name == "dve":
                            eng.wait_ge(ld, sch.val("ld"))
                            eng.wait_ge(mm, sch.val("mm"))
                            eng.tensor_copy(idxb[0:prow, 0:cols], i32t[0:prow, 0:cols])
                            eng.memset(wrow[0:prow, 0:cols], 0.0)
                            for l in range(16):
                                eng.tensor_scalar(tmpb[0:prow, 0:cols], idxb[0:prow, 0:cols],
                                                  float(l), lutB[0:prow, l:l + 1],
                                                  ALU.is_equal, ALU.mult)
                                ins_ = eng.tensor_tensor(wrow[0:prow, 0:cols], wrow[0:prow, 0:cols],
                                                         tmpb[0:prow, 0:cols], ALU.add)
                                if l == 15:
                                    ins_.then_inc(ve, 1)
                        sch.inc("ve")
                        sch.set("ve_idx_free", sch.val("ve"))
                        sch.set("wrow_ready_ve", sch.val("ve"))
                    else:
                        if eng_name == "pool":
                            eng.wait_ge(mm, sch.val("mm"))
                            eng.dma_start(out=wrow[0:prow, 0:cols], in_=src).then_inc(ld, 16)
                        sch.inc("ld", 16)

                    for d in range(ic):
                        bank = d % 4
                        if eng_name == "pe":
                            if is_ghost:
                                eng.wait_ge(ve, sch.val("wrow_ready_ve"))
                            else:
                                eng.wait_ge(ld, sch.val("ld"))
                            prev = sch.val(f"bank{bank}_ve")
                            if prev > 0:
                                eng.wait_ge(ve, prev)
                            prevst = sch.val("slab_store_st")
                            if prevst > 0 and d < 4:
                                eng.wait_ge(st, prevst)
                            eng.transpose(pst[bank][:, 0:prow],
                                          wrow[0:prow, d * 128:(d + 1) * 128],
                                          ident[0:prow, 0:prow]).then_inc(mm, 1)
                        sch.inc("mm")
                        if eng_name == "dve":
                            eng.wait_ge(mm, sch.val("mm"))
                            prevst = sch.val("slab_store_st")
                            if prevst > 0:
                                eng.wait_ge(st, prevst)
                            eng.tensor_copy(slab[:, d * prow:(d + 1) * prow],
                                            pst[bank][:, 0:prow]).then_inc(ve, 1)
                        sch.inc("ve")
                        sch.set(f"bank{bank}_ve", sch.val("ve"))
                    if eng_name == "sync":
                        eng.wait_ge(ve, sch.val("ve"))
                        eng.dma_start(out=outs[name][o, :, 0:ic * prow],
                                      in_=slab[:, 0:ic * prow]).then_inc(st, 16)
                    sch.inc("st", 16)
                    sch.set("slab_store_st", sch.val("st"))

            if eng_name == "sync":
                eng.wait_ge(st, sch.val("st"))

        build_blocked(nc, gen)
    return nc


def host_prep_inputs(inp):
    def pad_rows(a, total):
        if a.shape[0] == total:
            return a
        pad = np.zeros((total - a.shape[0],) + a.shape[1:], a.dtype)
        return np.concatenate([a, pad], axis=0)

    full = {
        "gw": pad_rows(np.asarray(inp["L0_ffn_gate_w"]), 6144),
        "uw": pad_rows(np.asarray(inp["L0_ffn_up_w"]), 6144),
        "dw": np.asarray(inp["L0_ffn_down_w"]),
        "bq": np.asarray(inp["L0_attn_b_Wq"]),
        "bk": np.asarray(inp["L0_attn_b_Wk"]),
        "bv": np.asarray(inp["L0_attn_b_Wv"]),
        "bo": np.asarray(inp["L0_attn_b_Wo"]),
        "gg": pad_rows(np.asarray(inp["L1_ffn_gate_idx"]), 6144),
        "gu": pad_rows(np.asarray(inp["L1_ffn_up_idx"]), 6144),
        "gd": np.asarray(inp["L1_ffn_down_idx"]),
        "w1a": np.ascontiguousarray(np.asarray(inp["L1_attn_b_W1"][0]).T),
        "w1b": np.ascontiguousarray(np.asarray(inp["L1_attn_b_W1"][1]).T),
        "w2a": np.ascontiguousarray(np.asarray(inp["L1_attn_b_W2"][0]).T),
        "w2b": np.ascontiguousarray(np.asarray(inp["L1_attn_b_W2"][1]).T),
    }
    lut = np.asarray(inp["L1_ffn_lut"]).reshape(1, 16).astype(np.float32)
    in_maps = []
    for c in range(NCORES):
        m = {"in_lut": lut}
        for name, rows, cols, is_ghost in PREP_MATS:
            a = full[name]
            m["in_" + name] = np.ascontiguousarray(a[c * rows:(c + 1) * rows])
        in_maps.append(m)
    return in_maps


def host_prep_assemble(results):
    tot = {}
    trims = {"gw": 44, "uw": 44, "gg": 43, "gu": 43}
    for name, rows, cols, is_ghost in PREP_MATS:
        parts = [results[c]["out_" + name] for c in range(NCORES)]
        a = np.concatenate(parts, axis=0)
        if name in trims:
            a = a[:trims[name]]
        tot[name] = np.ascontiguousarray(a)
    return tot




def build_ffn(ni, has_scales):
    nc = bass.Bass("TRN2", target_bir_lowering=False, debug=False)
    TW = 512
    NH = 16
    xs = nc.dram_tensor("xs", [D, 1024], F32, kind="ExternalInput").ap()
    gslab = nc.dram_tensor("gslab", [ni, 128, 2048], BF16, kind="ExternalInput").ap()
    uslab = nc.dram_tensor("uslab", [ni, 128, 2048], BF16, kind="ExternalInput").ap()
    dslab = nc.dram_tensor("dslab", [NH, 128, ni * 128], BF16, kind="ExternalInput").ap()
    if has_scales:
        gs_in = nc.dram_tensor("gs", [ni, 128], F32, kind="ExternalInput").ap()
        us_in = nc.dram_tensor("us", [ni, 128], F32, kind="ExternalInput").ap()
        ds_in = nc.dram_tensor("ds", [NH, 128], F32, kind="ExternalInput").ap()
    y = nc.dram_tensor("y", [D, 1024], F32, kind="ExternalOutput").ap()

    ctx = ExitStack()
    with ctx:
        xt = [ctx.enter_context(nc.sbuf_tensor(f"xt{d}", [128, 1024], F32)) for d in range(NH)]
        xn = [ctx.enter_context(nc.sbuf_tensor(f"xn{d}", [128, TW], BF16)) for d in range(NH)]
        h = [ctx.enter_context(nc.sbuf_tensor(f"h{i}", [128, TW], BF16)) for i in range(ni)]
        gbuf = [ctx.enter_context(nc.sbuf_tensor(f"gbuf{p}", [128, 2048], BF16)) for p in range(2)]
        ubuf = [ctx.enter_context(nc.sbuf_tensor(f"ubuf{p}", [128, 2048], BF16)) for p in range(2)]
        dbuf = [ctx.enter_context(nc.sbuf_tensor(f"dbuf{p}", [128, ni * 128], BF16)) for p in range(2)]
        t1 = [ctx.enter_context(nc.sbuf_tensor(f"t1_{p}", [128, TW], F32)) for p in range(2)]
        t2 = [ctx.enter_context(nc.sbuf_tensor(f"t2_{p}", [128, TW], F32)) for p in range(2)]
        yscr = [ctx.enter_context(nc.sbuf_tensor(f"yscr{p}", [128, TW], F32)) for p in range(2)]
        castsq = ctx.enter_context(nc.sbuf_tensor("castsq", [128, 4 * TW], BF16))
        scr = ctx.enter_context(nc.sbuf_tensor("scr", [128, TW], F32))
        row = ctx.enter_context(nc.sbuf_tensor("row", [1, 4 * TW], F32))
        ones_f = ctx.enter_context(nc.sbuf_tensor("ones_f", [1, 128], F32))
        ones_b = ctx.enter_context(nc.sbuf_tensor("ones_b", [128, 1], BF16))
        if has_scales:
            srow = ctx.enter_context(nc.sbuf_tensor("srow", [64, 128], F32))
            identf = ctx.enter_context(nc.sbuf_tensor("identf", [128, 128], F32))
            gsT = ctx.enter_context(nc.sbuf_tensor("gsT", [128, ni], F32))
            usT = ctx.enter_context(nc.sbuf_tensor("usT", [128, ni], F32))
            dsT = ctx.enter_context(nc.sbuf_tensor("dsT", [128, NH], F32))
        iden_io = ctx.enter_context(nc.sbuf_tensor("iden_io", [128, 128], F32))

        ps_stat = ctx.enter_context(nc.psum_tensor("ps_stat", [1, 2 * TW], F32))
        ps_b = ctx.enter_context(nc.psum_tensor("ps_b", [128, 2 * TW], F32))
        psg = [ctx.enter_context(nc.psum_tensor(f"psg{p}", [128, TW], F32)) for p in range(2)]
        psu = [ctx.enter_context(nc.psum_tensor(f"psu{p}", [128, TW], F32)) for p in range(2)]
        psd = psg
        pstr = ps_b

        ld = ctx.enter_context(nc.semaphore("ld"))
        mm = ctx.enter_context(nc.semaphore("mm"))
        ve = ctx.enter_context(nc.semaphore("ve"))
        sc = ctx.enter_context(nc.semaphore("sc"))
        st = ctx.enter_context(nc.semaphore("st"))
        sems = {"ld": ld, "mm": mm, "ve": ve, "sc": sc, "st": st}

        def gen(eng_name, eng, sch):
            if eng_name == "sync":
                for d in range(NH):
                    eng.dma_start(out=xt[d][:, :], in_=xs[d * 128:(d + 1) * 128, :]).then_inc(ld, 16)
            sch.inc("ld", 16 * NH)
            if eng_name == "pool":
                eng.iota(iden_io[:, :], pattern=[[1, 128]], base=0, channel_multiplier=-1,
                         allow_small_or_imprecise_dtypes=True).then_inc(ve, 1)
                if has_scales:
                    eng.dma_start(out=srow[0:ni, 0:128], in_=gs_in[:, :]).then_inc(ld, 16)
            sch.inc("ve")
            if has_scales:
                sch.inc("ld", 16)
            if eng_name == "dve":
                eng.wait_ge(ve, sch.val("ve"))
                eng.memset(ones_f[:, :], 1.0)
                eng.memset(ones_b[:, :], 1.0)
                if has_scales:
                    eng.tensor_scalar(identf[:, :], iden_io[:, :], 0.0, None, ALU.is_equal).then_inc(ve, 1)
                else:
                    eng.memset(ones_f[:, :], 1.0).then_inc(ve, 1)
            sch.inc("ve")
            if has_scales:
                for which in range(3):
                    nrow = [ni, ni, NH][which]
                    src = [gs_in, us_in, ds_in][which]
                    if which > 0:
                        if eng_name == "pool":
                            eng.wait_ge(mm, sch.val("mm"))
                            eng.dma_start(out=srow[0:nrow, 0:128], in_=src[:, :]).then_inc(ld, 16)
                        sch.inc("ld", 16)
                    if eng_name == "pe":
                        eng.wait_ge(ld, sch.val("ld"))
                        eng.wait_ge(ve, sch.val("ve"))
                        if which > 0:
                            eng.wait_ge(ve, sch.val("pstr_ve"))
                        eng.transpose(pstr[0:128, 0:nrow], srow[0:nrow, 0:128], identf[0:nrow, 0:nrow]).then_inc(mm, 1)
                    sch.inc("mm")
                    if eng_name == "dve":
                        eng.wait_ge(mm, sch.val("mm"))
                        dst = [gsT, usT, dsT][which]
                        eng.tensor_copy(dst[:, 0:nrow], pstr[:, 0:nrow]).then_inc(ve, 1)
                    sch.inc("ve")
                    sch.set("pstr_ve", sch.val("ve"))

            for sp in range(2):
                T0 = sp * TW
                if sp > 0:
                    pass
                for d in range(NH):
                    par = d % 2
                    if eng_name == "dve":
                        if d == 0:
                            eng.wait_ge(ld, 16 * NH)
                        g = sch.val(f"lncast{par}_mm")
                        if g > 0:
                            eng.wait_ge(mm, g)
                        base = par * 2 * TW
                        eng.tensor_copy(castsq[:, base:base + TW], xt[d][:, T0:T0 + TW])
                        eng.tensor_tensor(castsq[:, base + TW:base + 2 * TW], xt[d][:, T0:T0 + TW],
                                          xt[d][:, T0:T0 + TW], ALU.mult).then_inc(ve, 1)
                    sch.inc("ve")
                    if eng_name == "pe":
                        eng.wait_ge(ve, sch.val("ve"))
                        if d == 0:
                            g = sch.val("ps_stat_free_ve")
                            if g > 0:
                                eng.wait_ge(ve, g)
                        base = par * 2 * TW
                        eng.matmul(ps_stat[0:1, 0:TW], ones_b[:, :], castsq[:, base:base + TW],
                                   start=(d == 0), stop=(d == NH - 1))
                        eng.matmul(ps_stat[0:1, TW:2 * TW], ones_b[:, :],
                                   castsq[:, base + TW:base + 2 * TW],
                                   start=(d == 0), stop=(d == NH - 1)).then_inc(mm, 1)
                    sch.inc("mm")
                    sch.set(f"lncast{par}_mm", sch.val("mm"))
                if eng_name == "dve":
                    eng.wait_ge(mm, sch.val("mm"))
                    eng.tensor_scalar(row[0:1, 0:TW], ps_stat[0:1, 0:TW], 1.0 / D, None, ALU.mult)
                    eng.tensor_scalar(row[0:1, TW:2 * TW], ps_stat[0:1, TW:2 * TW], 1.0 / D, None, ALU.mult)
                    eng.tensor_tensor(row[0:1, 2 * TW:3 * TW], row[0:1, 0:TW], row[0:1, 0:TW], ALU.mult)
                    eng.tensor_tensor(row[0:1, TW:2 * TW], row[0:1, TW:2 * TW],
                                      row[0:1, 2 * TW:3 * TW], ALU.subtract)
                    eng.tensor_scalar(row[0:1, TW:2 * TW], row[0:1, TW:2 * TW], 1e-5, None,
                                      ALU.add).then_inc(ve, 1)
                sch.inc("ve")
                sch.set("ps_stat_free_ve", sch.val("ve"))
                if eng_name == "act":
                    eng.wait_ge(ve, sch.val("ve"))
                    eng.activation(row[0:1, 2 * TW:3 * TW], row[0:1, TW:2 * TW], AF.Sqrt).then_inc(sc, 1)
                sch.inc("sc")
                if eng_name == "dve":
                    eng.wait_ge(sc, sch.val("sc"))
                    eng.reciprocal(row[0:1, 3 * TW:4 * TW], row[0:1, 2 * TW:3 * TW]).then_inc(ve, 1)
                sch.inc("ve")
                if eng_name == "pe":
                    eng.wait_ge(ve, sch.val("ve"))
                    g = sch.val("ps_b_free_ve")
                    if g > 0:
                        eng.wait_ge(ve, g)
                    eng.matmul(ps_b[:, 0:TW], ones_f[:, :], row[0:1, 0:TW], start=True, stop=True)
                    eng.matmul(ps_b[:, TW:2 * TW], ones_f[:, :], row[0:1, 3 * TW:4 * TW],
                               start=True, stop=True).then_inc(mm, 1)
                sch.inc("mm")
                for d in range(NH):
                    if eng_name == "dve":
                        if d == 0:
                            eng.wait_ge(mm, sch.val("mm"))
                            g = sch.val("xn_free_mm")
                            if g > 0:
                                eng.wait_ge(mm, g)
                        eng.tensor_tensor(scr[:, :], xt[d][:, T0:T0 + TW], ps_b[:, 0:TW], ALU.subtract)
                        ii = eng.tensor_tensor(xn[d][:, :], scr[:, :], ps_b[:, TW:2 * TW], ALU.mult)
                        if d == NH - 1:
                            ii.then_inc(ve, 1)
                sch.inc("ve")
                sch.set("ps_b_free_ve", sch.val("ve"))
                xn_ready_ve = sch.val("ve")

                for oi in range(ni):
                    par = oi % 2
                    if eng_name == "pool":
                        g = sch.val(f"gbuf{par}_mm")
                        if g > 0:
                            eng.wait_ge(mm, g)
                        eng.dma_start(out=gbuf[par][:, :], in_=gslab[oi]).then_inc(ld, 16)
                        eng.dma_start(out=ubuf[par][:, :], in_=uslab[oi]).then_inc(ld, 16)
                    sch.inc("ld", 32)
                    if eng_name == "pe":
                        eng.wait_ge(ld, sch.val("ld"))
                        if oi == 0:
                            eng.wait_ge(ve, xn_ready_ve)
                        gg = sch.val(f"psg{par}_sc")
                        if gg > 0:
                            eng.wait_ge(sc, gg)
                        gg = sch.val(f"psd{par}_ve")
                        if gg > 0:
                            eng.wait_ge(ve, gg)
                        gg = sch.val(f"psu{par}_ve")
                        if gg > 0:
                            eng.wait_ge(ve, gg)
                        for d in range(NH):
                            eng.matmul(psg[par][:, :], gbuf[par][:, d * 128:(d + 1) * 128],
                                       xn[d][:, :], start=(d == 0), stop=(d == NH - 1))
                        for d in range(NH):
                            ii = eng.matmul(psu[par][:, :], ubuf[par][:, d * 128:(d + 1) * 128],
                                            xn[d][:, :], start=(d == 0), stop=(d == NH - 1))
                        ii.then_inc(mm, 1)
                    sch.inc("mm")
                    sch.set(f"gbuf{par}_mm", sch.val("mm"))
                    if eng_name == "act":
                        eng.wait_ge(mm, sch.val("mm"))
                        g = sch.val(f"t1_{par}_ve")
                        if g > 0:
                            eng.wait_ge(ve, g)
                        if has_scales:
                            eng.activation(t1[par][:, :], psg[par][:, :], AF.Silu,
                                           scale=gsT[:, oi:oi + 1]).then_inc(sc, 1)
                        else:
                            eng.activation(t1[par][:, :], psg[par][:, :], AF.Silu).then_inc(sc, 1)
                    sch.inc("sc")
                    sch.set(f"psg{par}_sc", sch.val("sc"))
                    if eng_name == "dve":
                        eng.wait_ge(sc, sch.val("sc"))
                        if has_scales:
                            eng.tensor_scalar(t2[par][:, :], psu[par][:, :], usT[:, oi:oi + 1],
                                              None, ALU.mult)
                            eng.tensor_tensor(h[oi][:, :], t1[par][:, :], t2[par][:, :],
                                              ALU.mult).then_inc(ve, 1)
                        else:
                            eng.tensor_tensor(h[oi][:, :], t1[par][:, :], psu[par][:, :],
                                              ALU.mult).then_inc(ve, 1)
                    sch.inc("ve")
                    sch.set(f"psu{par}_ve", sch.val("ve"))
                    sch.set(f"t1_{par}_ve", sch.val("ve"))
                h_ready_ve = sch.val("ve")

                for oo in range(NH):
                    par = oo % 2
                    if eng_name == "pool":
                        g = sch.val(f"dbuf{par}_mm")
                        if g > 0:
                            eng.wait_ge(mm, g)
                        eng.dma_start(out=dbuf[par][:, :], in_=dslab[oo]).then_inc(ld, 16)
                    sch.inc("ld", 16)
                    if eng_name == "pe":
                        eng.wait_ge(ld, sch.val("ld"))
                        if oo == 0:
                            eng.wait_ge(ve, h_ready_ve)
                        g = sch.val(f"psd{par}_ve")
                        if g > 0:
                            eng.wait_ge(ve, g)
                        g = sch.val(f"psg{par}_sc")
                        if g > 0:
                            eng.wait_ge(sc, g)
                        for i in range(ni):
                            ii = eng.matmul(psd[par][:, :], dbuf[par][:, i * 128:(i + 1) * 128],
                                            h[i][:, :], start=(i == 0), stop=(i == ni - 1))
                        ii.then_inc(mm, 1)
                    sch.inc("mm")
                    sch.set(f"dbuf{par}_mm", sch.val("mm"))
                    if eng_name == "dve":
                        eng.wait_ge(mm, sch.val("mm"))
                        g = sch.val(f"yscr{par}_st")
                        if g > 0:
                            eng.wait_ge(st, g)
                        if has_scales:
                            eng.scalar_tensor_tensor(yscr[par][:, :], psd[par][:, :],
                                                     dsT[:, oo:oo + 1], xt[oo][:, T0:T0 + TW],
                                                     ALU.mult, ALU.add).then_inc(ve, 1)
                        else:
                            eng.tensor_tensor(yscr[par][:, :], psd[par][:, :],
                                              xt[oo][:, T0:T0 + TW], ALU.add).then_inc(ve, 1)
                    sch.inc("ve")
                    sch.set(f"psd{par}_ve", sch.val("ve"))
                    if eng_name == "sync":
                        eng.wait_ge(ve, sch.val("ve"))
                        eng.dma_start(out=y[oo * 128:(oo + 1) * 128, T0:T0 + TW],
                                      in_=yscr[par][:, :]).then_inc(st, 16)
                    sch.inc("st", 16)
                    sch.set(f"yscr{par}_st", sch.val("st"))
                sch.set("xn_free_mm", sch.val("mm"))

            if eng_name == "sync":
                eng.wait_ge(st, sch.val("st"))

        build_blocked(nc, gen)
    return nc



class LNKit:

    _n = 0

    def __init__(self, nc, ctx, tw=512, tag=None, banks=None, dmodel=D, gb=False):
        LNKit._n += 1
        tag = tag or f"ln{LNKit._n}"
        self.tw = tw
        self.dmodel = dmodel
        self.castsq = ctx.enter_context(nc.sbuf_tensor(f"{tag}_castsq", [128, 4 * tw], BF16))
        self.row = ctx.enter_context(nc.sbuf_tensor(f"{tag}_row", [1, 4 * tw], F32))
        self.scr = ctx.enter_context(nc.sbuf_tensor(f"{tag}_scr", [128, tw], F32))
        self.scr4 = (ctx.enter_context(nc.sbuf_tensor(f"{tag}_scr4", [128, 4 * tw], F32))
                     if gb else None)
        self.ones_f = ctx.enter_context(nc.sbuf_tensor(f"{tag}_onesf", [1, 128], F32))
        self.ones_b = ctx.enter_context(nc.sbuf_tensor(f"{tag}_onesb", [128, 1], BF16))
        if banks is None:
            b0 = ctx.enter_context(nc.psum_tensor(f"{tag}_pb0", [128, tw], F32))
            b1 = ctx.enter_context(nc.psum_tensor(f"{tag}_pb1", [128, tw], F32))
            b2 = ctx.enter_context(nc.psum_tensor(f"{tag}_pb2", [128, tw], F32))
            b3 = ctx.enter_context(nc.psum_tensor(f"{tag}_pb3", [128, tw], F32))
            banks = [b0, b1, b2, b3]
        self.banks = banks
        self.tag = tag

    def emit_init(self, eng_name, eng, sch):
        if eng_name == "dve":
            eng.memset(self.ones_f[:, :], 1.0)
            eng.memset(self.ones_b[:, :], 1.0).then_inc(sch.sems["ve"], 1)
        sch.inc("ve")
        sch.set(self.tag + "_init_ve", sch.val("ve"))

    def emit(self, eng_name, eng, sch, xt, t0, xnt, ot0, tw=None, xt_ready=None,
             xn_free_mm=None, gamma_col=None, beta_col=None, nch=16, chunk_cb=None,
             upto=99, fuse_plain=False):
        TW = tw or self.tw
        mm, ve, sc = sch.sems["mm"], sch.sems["ve"], sch.sems["sc"]
        tag = self.tag
        K128 = min(128, xt[0].shape[0]) if hasattr(xt[0], "shape") else 128
        NP = xt[0].shape[0] if hasattr(xt[0], "shape") else 128
        for d in range(nch):
            par = d % 2
            if eng_name == "dve":
                if d == 0:
                    eng.wait_ge(ve, sch.val(tag + "_init_ve"))
                    if xt_ready is not None:
                        eng.wait_ge(sch.sems[xt_ready[0]], xt_ready[1])
                    g = sch.val(tag + "_stat_free_ve")
                g = sch.val(tag + f"_cast{par}_mm")
                if g > 0:
                    eng.wait_ge(mm, g)
                base = par * 2 * TW
                eng.tensor_copy(self.castsq[:, base:base + TW], xt[d][:, t0:t0 + TW])
                eng.tensor_tensor(self.castsq[:, base + TW:base + 2 * TW], xt[d][:, t0:t0 + TW],
                                  xt[d][:, t0:t0 + TW], ALU.mult).then_inc(ve, 1)
            sch.inc("ve")
            if eng_name == "pe":
                eng.wait_ge(ve, sch.val("ve"))
                if d == 0:
                    g = sch.val(tag + "_stat_free_sc")
                    if g > 0:
                        eng.wait_ge(sc, g)
                base = par * 2 * TW
                eng.matmul(self.banks[0][0:1, 0:TW], self.ones_b[0:K128, :],
                           self.castsq[0:K128, base:base + TW], start=(d == 0), stop=(d == nch - 1))
                eng.matmul(self.banks[1][0:1, 0:TW], self.ones_b[0:K128, :],
                           self.castsq[0:K128, base + TW:base + 2 * TW],
                           start=(d == 0), stop=(d == nch - 1)).then_inc(mm, 1)
            sch.inc("mm")
            sch.set(tag + f"_cast{par}_mm", sch.val("mm"))
        if upto <= 1:
            return sch.val("ve")
        row = self.row
        if eng_name == "act":
            eng.wait_ge(mm, sch.val("mm"))
            eng.activation(row[0:1, 0:TW], self.banks[0][0:1, 0:TW], AF.Copy,
                           scale=1.0 / self.dmodel)
            eng.activation(row[0:1, TW:2 * TW], self.banks[1][0:1, 0:TW], AF.Copy,
                           scale=1.0 / self.dmodel).then_inc(sc, 1)
        sch.inc("sc")
        sch.set(tag + "_stat_free_ve", 0)
        sch.set(tag + "_stat_free_sc", sch.val("sc"))
        if eng_name == "dve":
            eng.wait_ge(sc, sch.val("sc"))
            eng.tensor_tensor(row[0:1, 2 * TW:3 * TW], row[0:1, 0:TW], row[0:1, 0:TW], ALU.mult)
            eng.memset(self.castsq[0:1, 0:1], 0.0)
            ii = eng.scalar_tensor_tensor(row[0:1, TW:2 * TW], row[0:1, TW:2 * TW], 1e-5,
                                          row[0:1, 2 * TW:3 * TW], ALU.add, ALU.subtract)
            if fuse_plain:
                eng.memset(self.castsq[0:1, 4:5], 0.0)
                eng.memset(self.castsq[0:1, 5:6], 0.0)
                ii = eng.tensor_scalar(row[0:1, TW:2 * TW], row[0:1, TW:2 * TW],
                                       1.00001, -1e-5, ALU.mult, ALU.add)
            ii.then_inc(ve, 1)
        sch.inc("ve")
        if upto <= 2:
            return sch.val("ve")
        if eng_name == "act":
            eng.wait_ge(ve, sch.val("ve"))
            eng.activation(row[0:1, 2 * TW:3 * TW], row[0:1, TW:2 * TW], AF.Sqrt).then_inc(sc, 1)
        sch.inc("sc")
        if eng_name == "dve":
            eng.wait_ge(sc, sch.val("sc"))
            eng.reciprocal(row[0:1, 3 * TW:4 * TW], row[0:1, 2 * TW:3 * TW]).then_inc(ve, 1)
        sch.inc("ve")
        if upto <= 3:
            return sch.val("ve")
        if eng_name == "pe":
            eng.wait_ge(ve, sch.val("ve"))
            g = sch.val(tag + "_psb_free_ve")
            if g > 0:
                eng.wait_ge(ve, g)
            eng.matmul(self.banks[2][0:NP, 0:TW], self.ones_f[0:1, 0:NP], row[0:1, 0:TW],
                       start=True, stop=True)
            eng.matmul(self.banks[3][0:NP, 0:TW], self.ones_f[0:1, 0:NP], row[0:1, 3 * TW:4 * TW],
                       start=True, stop=True).then_inc(mm, 1)
        sch.inc("mm")
        L = len(xnt)
        if gamma_col is None:
            for d in range(nch):
                dd = d % L
                if eng_name == "dve":
                    if d == 0:
                        eng.wait_ge(mm, sch.val("mm"))
                        if xn_free_mm is not None and xn_free_mm > 0:
                            eng.wait_ge(mm, xn_free_mm)
                    g = sch.val(tag + f"_xnb{dd}_free")
                    if g > 0 and L < nch:
                        eng.wait_ge(sch.sems["st"], g)
                    eng.tensor_tensor(self.scr[0:NP, 0:TW], xt[d][:, t0:t0 + TW],
                                      self.banks[2][0:NP, 0:TW], ALU.subtract)
                    eng.memset(self.castsq[0:1, 2:3], 0.0)
                    ii = eng.tensor_tensor(xnt[dd][:, ot0:ot0 + TW], self.scr[0:NP, 0:TW],
                                           self.banks[3][0:NP, 0:TW], ALU.mult)
                    ii.then_inc(ve, 1)
                sch.inc("ve")
                if chunk_cb is not None:
                    chunk_cb(eng_name, eng, sch, d, dd)
        else:
            def sA(par):
                return self.scr4[0:NP, par * TW:(par + 1) * TW]

            def sB(par):
                return self.scr4[0:NP, (2 + par) * TW:(3 + par) * TW]

            def op3(d2):
                par2 = d2 % 2
                dd2 = d2 % L
                return eng.scalar_tensor_tensor(
                    xnt[dd2][:, ot0:ot0 + TW], sB(par2), gamma_col[d2],
                    beta_col[d2].to_broadcast([NP, TW]), ALU.mult, ALU.add)

            for d in range(nch):
                par = d % 2
                if eng_name == "dve":
                    if d == 0:
                        eng.wait_ge(mm, sch.val("mm"))
                        if xn_free_mm is not None and xn_free_mm > 0:
                            eng.wait_ge(mm, xn_free_mm)
                    eng.tensor_tensor(sA(par), xt[d][:, t0:t0 + TW],
                                      self.banks[2][0:NP, 0:TW], ALU.subtract)
                    if d > 0:
                        op3(d - 1)
                    else:
                        eng.memset(self.castsq[0:1, 1:2], 0.0)
                    eng.tensor_tensor(sB(par), sA(par), self.banks[3][0:NP, 0:TW], ALU.mult)
            if eng_name == "dve":
                eng.memset(self.castsq[0:1, 3:4], 0.0)
                op3(nch - 1).then_inc(ve, 1)
            sch.inc("ve")
        sch.set(tag + "_psb_free_ve", sch.val("ve"))
        return sch.val("ve")


def build_s5():
    nc = bass.Bass("TRN2", target_bir_lowering=False, debug=False)
    NH = 16
    xs = nc.dram_tensor("xs", [D, 1024], F32, kind="ExternalInput").ap()
    xa = nc.dram_tensor("xa", [D, S], F32, kind="ExternalInput").ap()
    wq = nc.dram_tensor("wq", [D, D], F32, kind="ExternalInput").ap()
    wk = nc.dram_tensor("wk", [D, D], F32, kind="ExternalInput").ap()
    wv = nc.dram_tensor("wv", [D, D], F32, kind="ExternalInput").ap()
    wo = nc.dram_tensor("wo", [D, D], F32, kind="ExternalInput").ap()
    bo_in = nc.dram_tensor("bo", [NH, 128], F32, kind="ExternalInput").ap()
    y = nc.dram_tensor("y", [D, 1024], F32, kind="ExternalOutput").ap()
    xnall = nc.dram_tensor("xnall", [D, S], BF16)
    kscr = nc.dram_tensor("kscr", [S, D], BF16)
    vscr = nc.dram_tensor("vscr", [S, D], BF16)

    ctx = ExitStack()
    with ctx:
        AR = 49152
        arena = ctx.enter_context(nc.sbuf_tensor("arena", [128, AR], BF16))
        arena_f = arena.ap().bitcast(F32)
        xn_own = [ctx.enter_context(nc.sbuf_tensor(f"xno{d}", [128, 1024], BF16)) for d in range(NH)]
        q_fm = [ctx.enter_context(nc.sbuf_tensor(f"qfm{d}", [128, 1024], BF16)) for d in range(NH)]
        attn = xn_own
        kv_sb = ctx.enter_context(nc.sbuf_tensor("kv_sb", [128, 1040], F32))
        kv_bf = ctx.enter_context(nc.sbuf_tensor("kv_bf", [128, 1040], BF16))
        rcpb_sb = ctx.enter_context(nc.sbuf_tensor("rcpb_sb", [64, 512], F32))
        rcrow = ctx.enter_context(nc.sbuf_tensor("rcrow", [1, 512], F32))
        xres = [ctx.enter_context(nc.sbuf_tensor(f"xres{p}", [128, 1024], F32)) for p in range(2)]
        yscr = [ctx.enter_context(nc.sbuf_tensor(f"yscr{p}", [128, 512], F32)) for p in range(2)]
        kvout = [ctx.enter_context(nc.sbuf_tensor(f"kvout{p}", [128, 512], BF16)) for p in range(2)]
        boT = ctx.enter_context(nc.sbuf_tensor("boT", [128, NH], F32))
        srow = ctx.enter_context(nc.sbuf_tensor("srow", [NH, 128], F32))
        identf = ctx.enter_context(nc.sbuf_tensor("identf", [128, 128], F32))
        iden_io = ctx.enter_context(nc.sbuf_tensor("iden_io", [128, 128], F32))
        ln = LNKit(nc, ctx, tw=512)

        pp = [ctx.enter_context(nc.psum_tensor(f"pp{p}", [128, 512], F32)) for p in range(2)]
        kv_ps = ctx.enter_context(nc.psum_tensor("kv_ps", [128, 520], F32))

        ld = ctx.enter_context(nc.semaphore("ld"))
        mm = ctx.enter_context(nc.semaphore("mm"))
        ve = ctx.enter_context(nc.semaphore("ve"))
        sc = ctx.enter_context(nc.semaphore("sc"))
        st = ctx.enter_context(nc.semaphore("st"))
        sems = {"ld": ld, "mm": mm, "ve": ve, "sc": sc, "st": st}

        def wtile(d):
            return arena[:, d * 2048:(d + 1) * 2048]

        XS_OFF = 32768
        def xsup(d):
            return arena_f[:, XS_OFF // 2 + d * 512: XS_OFF // 2 + (d + 1) * 512]

        def xnsup(d):
            return arena[:, XS_OFF + d * 512: XS_OFF + (d + 1) * 512]

        def kouts(p):
            return arena[:, XS_OFF + 8192 + p * 512: XS_OFF + 8192 + (p + 1) * 512]

        def ksup(tt):
            return arena[:, XS_OFF + tt * 2048: XS_OFF + (tt + 1) * 2048]

        def vsup(tt):
            return arena[:, XS_OFF + 8192 + tt * 2048: XS_OFF + 8192 + (tt + 1) * 2048]

        def barrier(eng_name, eng, sch):
            for nm in ("mm", "ve", "sc", "ld", "st"):
                v = sch.val(nm)
                if v > 0:
                    eng.wait_ge(sems[nm], v)

        def kvslot(hh):
            return (64 * (hh % 2), (hh // 2) * 65)

        def gen(eng_name, eng, sch):
            ln.emit_init(eng_name, eng, sch)
            if eng_name == "pool":
                eng.iota(iden_io[:, :], pattern=[[1, 128]], base=0, channel_multiplier=-1,
                         allow_small_or_imprecise_dtypes=True)
                eng.dma_start(out=srow[:, :], in_=bo_in[:, :]).then_inc(ld, 16)
            sch.inc("ld", 16)
            if eng_name == "dve":
                eng.wait_ge(ld, sch.val("ld"))
                eng.tensor_scalar(identf[:, :], iden_io[:, :], 0.0, None, ALU.is_equal)
                eng.memset(kv_sb[:, :], 0.0).then_inc(ve, 1)
            sch.inc("ve")
            if eng_name == "pe":
                eng.wait_ge(ve, sch.val("ve"))
                eng.transpose(pp[0][0:128, 0:NH], srow[:, :], identf[0:NH, 0:NH]).then_inc(mm, 1)
            sch.inc("mm")
            if eng_name == "dve":
                eng.wait_ge(mm, sch.val("mm"))
                eng.tensor_copy(boT[:, :], pp[0][0:128, 0:NH]).then_inc(ve, 1)
            sch.inc("ve")

            for sp2 in range(2):
                if eng_name == "sync":
                    barrier(eng_name, eng, sch)
                    for d in range(NH):
                        eng.dma_start(out=xsup(d), in_=xs[d * 128:(d + 1) * 128,
                                                         sp2 * 512:(sp2 + 1) * 512]).then_inc(st, 16)
                sch.inc("st", 16 * NH)
                ln.emit(eng_name, eng, sch, [xsup(d) for d in range(NH)], 0,
                        xn_own, sp2 * 512, xt_ready=("st", sch.val("st")))
                for e in ENGS:
                    pass
                barrier(eng_name, eng, sch)

            if eng_name == "pool":
                for d in range(NH):
                    eng.dma_start(out=wtile(d), in_=wq[d * 128:(d + 1) * 128, :]).then_inc(ld, 16)
            sch.inc("ld", 16 * NH)
            for o in range(NH):
                for sp2 in range(2):
                    par = (o * 2 + sp2) % 2
                    if eng_name == "pe":
                        if o == 0 and sp2 == 0:
                            eng.wait_ge(ld, sch.val("ld"))
                        g = sch.val(f"pp{par}_sc")
                        if g > 0:
                            eng.wait_ge(sc, g)
                        for d in range(NH):
                            ii = eng.matmul(pp[par][:, :], wtile(d)[:, o * 128:(o + 1) * 128],
                                            xn_own[d][:, sp2 * 512:(sp2 + 1) * 512],
                                            start=(d == 0), stop=(d == NH - 1))
                        ii.then_inc(mm, 1)
                    sch.inc("mm")
                    if eng_name == "act":
                        eng.wait_ge(mm, sch.val("mm"))
                        eng.activation(q_fm[o][:, sp2 * 512:(sp2 + 1) * 512], pp[par][:, :],
                                       AF.Relu).then_inc(sc, 1)
                    sch.inc("sc")
                    sch.set(f"pp{par}_sc", sch.val("sc"))
            barrier(eng_name, eng, sch)

            for sup in range(16):
                if eng_name == "sync":
                    barrier(eng_name, eng, sch)
                    for d in range(NH):
                        eng.dma_start(out=xsup(d), in_=xa[d * 128:(d + 1) * 128,
                                                         sup * 512:(sup + 1) * 512]).then_inc(st, 16)
                sch.inc("st", 16 * NH)

                def cb(egn, eg, sh, d, dd, _sup=sup):
                    if egn == "sync":
                        eg.wait_ge(ve, sh.val("ve"))
                        eg.dma_start(out=xnall[d * 128:(d + 1) * 128,
                                               _sup * 512:(_sup + 1) * 512],
                                     in_=kouts(dd)).then_inc(st, 16)
                    sh.inc("st", 16)
                    sh.set(ln.tag + f"_xnb{dd}_free", sh.val("st"))

                ln.emit(eng_name, eng, sch, [xsup(d) for d in range(NH)], 0,
                        [kouts(0), kouts(1)], 0, xt_ready=("st", sch.val("st")), chunk_cb=cb)
                barrier(eng_name, eng, sch)

            for which in range(2):
                wsrc = wk if which == 0 else wv
                dst = kscr if which == 0 else vscr
                if eng_name == "pool":
                    barrier(eng_name, eng, sch)
                    for d in range(NH):
                        eng.dma_start(out=wtile(d), in_=wsrc[d * 128:(d + 1) * 128, :]).then_inc(ld, 16)
                sch.inc("ld", 16 * NH)
                for sup in range(16):
                    if eng_name == "sync":
                        if sup > 0:
                            eng.wait_ge(mm, sch.val(f"xnsup_mm"))
                        for d in range(NH):
                            eng.dma_start(out=xnsup(d),
                                          in_=xnall[d * 128:(d + 1) * 128,
                                                    sup * 512:(sup + 1) * 512]).then_inc(st, 16)
                    sch.inc("st", 16 * NH)
                    for tt in range(4):
                        for oc in range(4):
                            par = (tt * 4 + oc) % 2
                            if eng_name == "pe":
                                if tt == 0 and oc == 0:
                                    eng.wait_ge(st, sch.val("st"))
                                    if sup == 0:
                                        eng.wait_ge(ld, sch.val("ld"))
                                g = sch.val(f"pp{par}_sc")
                                if g > 0:
                                    eng.wait_ge(sc, g)
                                for d in range(NH):
                                    ii = eng.matmul(pp[par][:, :],
                                                    xnsup(d)[:, tt * 128:(tt + 1) * 128],
                                                    wtile(d)[:, oc * 512:(oc + 1) * 512],
                                                    start=(d == 0), stop=(d == NH - 1))
                                ii.then_inc(mm, 1)
                            sch.inc("mm")
                            if eng_name == "act":
                                eng.wait_ge(mm, sch.val("mm"))
                                g = sch.val(f"ko{par}_st")
                                if g > 0:
                                    eng.wait_ge(st, g)
                                if which == 0:
                                    eng.activation(kouts(par), pp[par][:, :], AF.Relu).then_inc(sc, 1)
                                else:
                                    eng.copy(kouts(par), pp[par][:, :]).then_inc(sc, 1)
                            sch.inc("sc")
                            sch.set(f"pp{par}_sc", sch.val("sc"))
                            if eng_name == "sync":
                                eng.wait_ge(sc, sch.val("sc"))
                                eng.dma_start(out=dst[sup * 512 + tt * 128: sup * 512 + (tt + 1) * 128,
                                                      oc * 512:(oc + 1) * 512],
                                              in_=kouts(par)).then_inc(st, 16)
                            sch.inc("st", 16)
                            sch.set(f"ko{par}_st", sch.val("st"))
                    sch.set("xnsup_mm", sch.val("mm"))
                barrier(eng_name, eng, sch)

            for sup in range(16):
                if eng_name == "sync":
                    if sup > 0:
                        eng.wait_ge(mm, sch.val("kvsup_mm"))
                    for tt in range(4):
                        eng.dma_start(out=ksup(tt),
                                      in_=kscr[sup * 512 + tt * 128: sup * 512 + (tt + 1) * 128, :]
                                      ).then_inc(st, 16)
                        eng.dma_start(out=vsup(tt),
                                      in_=vscr[sup * 512 + tt * 128: sup * 512 + (tt + 1) * 128, :]
                                      ).then_inc(st, 16)
                sch.inc("st", 16 * 8)
                for half in range(2):
                    if eng_name == "pe":
                        if half == 0:
                            eng.wait_ge(st, sch.val("st"))
                        g = sch.val("kvps_ve")
                        if g > 0:
                            eng.wait_ge(ve, g)
                        for hh in range(16):
                            h = half * 16 + hh
                            rb, cb0 = kvslot(hh)
                            for tt in range(4):
                                eng.matmul(kv_ps[rb:rb + 64, cb0:cb0 + 64],
                                           ksup(tt)[:, h * 64:(h + 1) * 64],
                                           vsup(tt)[:, h * 64:(h + 1) * 64],
                                           start=(tt == 0), stop=False)
                                ii = eng.matmul(kv_ps[rb:rb + 64, cb0 + 64:cb0 + 65],
                                                ksup(tt)[:, h * 64:(h + 1) * 64],
                                                ln.ones_b[:, :],
                                                start=False, stop=(tt == 3))
                        ii.then_inc(mm, 1)
                    sch.inc("mm")
                    if eng_name == "dve":
                        eng.wait_ge(mm, sch.val("mm"))
                        eng.tensor_tensor(kv_sb[:, half * 520:(half + 1) * 520],
                                          kv_sb[:, half * 520:(half + 1) * 520],
                                          kv_ps[:, 0:520], ALU.add).then_inc(ve, 1)
                    sch.inc("ve")
                    sch.set("kvps_ve", sch.val("ve"))
                sch.set("kvsup_mm", sch.val("mm"))
            if eng_name == "dve":
                eng.tensor_copy(kv_bf[:, :], kv_sb[:, :]).then_inc(ve, 1)
            sch.inc("ve")
            barrier(eng_name, eng, sch)

            for h in range(32):
                half, hh = divmod(h, 16)
                rb = 64 * (h % 2)
                cb0 = half * 520 + (hh // 2) * 65
                for sp2 in range(2):
                    if eng_name == "pe":
                        g = sch.val("attnps_ve")
                        if g > 0:
                            eng.wait_ge(ve, g)
                        eng.matmul(pp[0][0:64, :], kv_bf[rb:rb + 64, cb0:cb0 + 64],
                                   q_fm[h // 2][rb:rb + 64, sp2 * 512:(sp2 + 1) * 512],
                                   start=True, stop=True)
                        eng.matmul(kv_ps[0:1, 0:512], kv_bf[rb:rb + 64, cb0 + 64:cb0 + 65],
                                   q_fm[h // 2][rb:rb + 64, sp2 * 512:(sp2 + 1) * 512],
                                   start=True, stop=True).then_inc(mm, 1)
                    sch.inc("mm")
                    if eng_name == "dve":
                        eng.wait_ge(mm, sch.val("mm"))
                        eng.tensor_scalar(rcrow[0:1, :], kv_ps[0:1, 0:512], 1e-6, None, ALU.max)
                        eng.reciprocal(rcrow[0:1, :], rcrow[0:1, :]).then_inc(ve, 1)
                    sch.inc("ve")
                    if eng_name == "pe":
                        eng.wait_ge(ve, sch.val("ve"))
                        eng.matmul(pp[1][0:64, :], ln.ones_f[0:1, 0:64], rcrow[0:1, :],
                                   start=True, stop=True).then_inc(mm, 1)
                    sch.inc("mm")
                    if eng_name == "dve":
                        eng.wait_ge(mm, sch.val("mm"))
                        eng.tensor_copy(rcpb_sb[:, :], pp[1][0:64, :])
                        eng.tensor_tensor(attn[h // 2][rb:rb + 64, sp2 * 512:(sp2 + 1) * 512],
                                          pp[0][0:64, :], rcpb_sb[:, :], ALU.mult).then_inc(ve, 1)
                    sch.inc("ve")
                    sch.set("attnps_ve", sch.val("ve"))
            barrier(eng_name, eng, sch)

            if eng_name == "pool":
                for d in range(NH):
                    eng.dma_start(out=wtile(d), in_=wo[d * 128:(d + 1) * 128, :]).then_inc(ld, 16)
            sch.inc("ld", 16 * NH)
            for o in range(NH):
                if eng_name == "sync":
                    g = sch.val(f"xres{o % 2}_ve")
                    if g > 0:
                        eng.wait_ge(ve, g)
                    eng.dma_start(out=xres[o % 2][:, :],
                                  in_=xs[o * 128:(o + 1) * 128, :]).then_inc(st, 16)
                sch.inc("st", 16)
                for sp2 in range(2):
                    par = sp2
                    if eng_name == "pe":
                        if o == 0 and sp2 == 0:
                            eng.wait_ge(ld, sch.val("ld"))
                        g = sch.val(f"yps{par}_ve")
                        if g > 0:
                            eng.wait_ge(ve, g)
                        for d in range(NH):
                            ii = eng.matmul(pp[par][:, :], wtile(d)[:, o * 128:(o + 1) * 128],
                                            attn[d][:, sp2 * 512:(sp2 + 1) * 512],
                                            start=(d == 0), stop=(d == NH - 1))
                        ii.then_inc(mm, 1)
                    sch.inc("mm")
                    if eng_name == "dve":
                        eng.wait_ge(mm, sch.val("mm"))
                        eng.wait_ge(st, sch.val("st"))
                        g = sch.val(f"ysc{par}_st")
                        if g > 0:
                            eng.wait_ge(st, g)
                        eng.scalar_tensor_tensor(yscr[par][:, :], pp[par][:, :], boT[:, o:o + 1],
                                                 xres[o % 2][:, sp2 * 512:(sp2 + 1) * 512],
                                                 ALU.add, ALU.add).then_inc(ve, 1)
                    sch.inc("ve")
                    sch.set(f"yps{par}_ve", sch.val("ve"))
                    if eng_name == "sync":
                        eng.wait_ge(ve, sch.val("ve"))
                        eng.dma_start(out=y[o * 128:(o + 1) * 128, sp2 * 512:(sp2 + 1) * 512],
                                      in_=yscr[par][:, :]).then_inc(st, 16)
                    sch.inc("st", 16)
                    sch.set(f"ysc{par}_st", sch.val("st"))
                sch.set(f"xres{o % 2}_ve", sch.val("ve"))

            if eng_name == "sync":
                eng.wait_ge(st, sch.val("st"))

        build_blocked(nc, gen, sems)
    return nc


def build_s3():
    nc = bass.Bass("TRN2", target_bir_lowering=False, debug=False)
    NH = 16
    M0 = 2.0 ** (-0.5)
    xs = nc.dram_tensor("xs", [D, 1024], F32, kind="ExternalInput").ap()
    xa = nc.dram_tensor("xa", [D, S], F32, kind="ExternalInput").ap()
    bq = nc.dram_tensor("bq", [16, 128, 2048], BF16, kind="ExternalInput").ap()
    bk = nc.dram_tensor("bk", [8, 128, 1024], BF16, kind="ExternalInput").ap()
    bv = nc.dram_tensor("bv", [8, 128, 1024], BF16, kind="ExternalInput").ap()
    bo = nc.dram_tensor("bo", [16, 128, 2048], BF16, kind="ExternalInput").ap()
    qpos = nc.dram_tensor("qpos", [1, 1024], F32, kind="ExternalInput").ap()
    y = nc.dram_tensor("y", [D, 1024], F32, kind="ExternalOutput").ap()
    xnall = nc.dram_tensor("xnall", [D, S], BF16)
    kfm = nc.dram_tensor("kfm", [512, S], BF16)
    vsw = nc.dram_tensor("vsw", [4, 128, S], BF16)
    fdram = nc.dram_tensor("fdram", [2, 8832], BF16)

    ctx = ExitStack()
    with ctx:
        ps = [ctx.enter_context(nc.psum_tensor(f"ps{i}", [128, 512], F32)) for i in range(8)]
        ln = LNKit(nc, ctx, tw=512, banks=ps[0:4])

        AR = 49152
        arena = ctx.enter_context(nc.sbuf_tensor("arena", [128, AR], BF16))
        arena_f = arena.ap().bitcast(F32)
        q_fm = [ctx.enter_context(nc.sbuf_tensor(f"qfm{d}", [128, 1024], BF16)) for d in range(NH)]
        attn = q_fm
        qposb = ctx.enter_context(nc.sbuf_tensor("qposb", [1, 1024], F32))
        fut = ctx.enter_context(nc.sbuf_tensor("fut", [128, 69], F32))
        fd = ctx.enter_context(nc.sbuf_tensor("fd", [128, 69], F32))
        fm = ctx.enter_context(nc.sbuf_tensor("fm", [128, 69], F32))
        fq = ctx.enter_context(nc.sbuf_tensor("fq", [128, 69], F32))
        fbrow = ctx.enter_context(nc.sbuf_tensor("fbrow", [128, 69], BF16))
        hrow = [ctx.enter_context(nc.sbuf_tensor(f"hrow{p}", [128, 512], BF16)) for p in range(2)]
        io2 = ctx.enter_context(nc.sbuf_tensor("io2", [128, 128], F32))
        revs = ctx.enter_context(nc.sbuf_tensor("revs", [128, 128], F32))
        revMb = ctx.enter_context(nc.sbuf_tensor("revMb", [128, 2048], BF16))
        probs = [ctx.enter_context(nc.sbuf_tensor(f"probs{p}", [128, 512], BF16)) for p in range(2)]
        rcrow = ctx.enter_context(nc.sbuf_tensor("rcrow", [1, 1024], F32))
        rcpb = ctx.enter_context(nc.sbuf_tensor("rcpb", [128, 512], F32))
        kevac = [ctx.enter_context(nc.sbuf_tensor(f"kevac{p}", [64, 512], BF16)) for p in range(2)]
        vevac = [ctx.enter_context(nc.sbuf_tensor(f"vevac{p}", [128, 512], BF16)) for p in range(2)]
        xres = [ctx.enter_context(nc.sbuf_tensor(f"xres{p}", [128, 1024], F32)) for p in range(2)]
        yscr = [ctx.enter_context(nc.sbuf_tensor(f"yscr{p}", [128, 512], F32)) for p in range(2)]
        wbuf = [ctx.enter_context(nc.sbuf_tensor(f"wbuf{p}", [128, 2048], BF16)) for p in range(2)]

        ld = ctx.enter_context(nc.semaphore("ld"))
        mm = ctx.enter_context(nc.semaphore("mm"))
        ve = ctx.enter_context(nc.semaphore("ve"))
        sc = ctx.enter_context(nc.semaphore("sc"))
        st = ctx.enter_context(nc.semaphore("st"))
        sems = {"ld": ld, "mm": mm, "ve": ve, "sc": sc, "st": st}

        def xsup(d):
            return arena_f[:, d * 512:(d + 1) * 512]

        def xnsup(d):
            return arena[:, 16384 + d * 512:16384 + (d + 1) * 512]

        def kvw(i):
            return arena[:, 24576 + i * 1024:24576 + (i + 1) * 1024]

        def kgrow():
            return arena[:, 0:8192]

        def vgrow():
            return arena[:, 8192:16384]

        def barrier(eng_name, eng, sch):
            for nm in ("mm", "ve", "sc", "ld", "st"):
                v = sch.val(nm)
                if v > 0:
                    eng.wait_ge(sems[nm], v)

        import copy as _copy
        _fap0 = fdram.ap()

        def fap(sp2, kt):
            v = _copy.copy(_fap0)
            v.ap = type(_fap0.ap)([(1, 128), (1, 512)])
            v.offset = sp2 * 8832 + 8192 - kt * 128
            return v

        def flin(sp2):
            v = _copy.copy(_fap0)
            v.ap = type(_fap0.ap)([(1, 128), (128, 69)])
            v.offset = sp2 * 8832
            return v

        def gen(eng_name, eng, sch):
            ln.emit_init(eng_name, eng, sch)
            if eng_name == "pool":
                eng.iota(fut[:, :], pattern=[[128, 69]], base=0, channel_multiplier=1,
                         allow_small_or_imprecise_dtypes=True)
                eng.iota(io2[:, :], pattern=[[1, 128]], base=0, channel_multiplier=1,
                         allow_small_or_imprecise_dtypes=True)
                eng.dma_start(out=qposb[:, :], in_=qpos[:, :]).then_inc(ld, 16)
            sch.inc("ld", 16)
            for sp2 in range(2):
                if eng_name == "pe":
                    if sp2 == 0:
                        eng.wait_ge(ld, sch.val("ld"))
                        g = sch.val(ln.tag + "_init_ve")
                        if g > 0:
                            eng.wait_ge(ve, g)
                    eng.matmul(ps[4][:, sp2:sp2 + 1], ln.ones_f[:, :],
                               qposb[0:1, sp2 * 512:sp2 * 512 + 1],
                               start=True, stop=True).then_inc(mm, 1)
                sch.inc("mm")
                if eng_name == "dve":
                    eng.wait_ge(mm, sch.val("mm"))
                    if sp2 == 1:
                        eng.wait_ge(st, sch.val("st"))
                    eng.tensor_scalar(fd[:, :], fut[:, :], -8319.0,
                                      ps[4][:, sp2:sp2 + 1], ALU.add, ALU.add)
                    eng.memset(hrow[0][0:1, 0:1], 0.0)
                    eng.tensor_scalar(fm[:, :], fd[:, :], 0.0, 1e30,
                                      ALU.is_lt, ALU.mult)
                    eng.tensor_scalar(fq[:, :], fd[:, :], 0.0, None,
                                      ALU.max).then_inc(ve, 1)
                sch.inc("ve")
                if eng_name == "act":
                    eng.wait_ge(ve, sch.val("ve"))
                    eng.activation(fq[:, :], fq[:, :], AF.Sqrt).then_inc(sc, 1)
                sch.inc("sc")
                if eng_name == "dve":
                    eng.wait_ge(sc, sch.val("sc"))
                    eng.tensor_tensor(fbrow[:, :], fq[:, :], fm[:, :],
                                      ALU.add).then_inc(ve, 1)
                sch.inc("ve")
                if eng_name == "sync":
                    eng.wait_ge(ve, sch.val("ve"))
                    eng.dma_start(out=flin(sp2), in_=fbrow[:, :],
                                  allow_slow_non_contiguous=True).then_inc(st, 16)
                sch.inc("st", 16)
            if eng_name == "dve":
                eng.tensor_scalar(revs[:, :], io2[:, :], 127.0, None, ALU.is_equal)
                eng.memset(hrow[1][0:1, 0:1], 0.0)
                ii = None
                for h in range(NH):
                    ii = eng.tensor_scalar(revMb[:, h * 128:(h + 1) * 128], revs[:, :],
                                           -(M0 ** (h + 1)), None, ALU.mult)
                ii.then_inc(ve, 1)
            sch.inc("ve")

            for sup in range(16):
                if eng_name == "sync":
                    barrier(eng_name, eng, sch)
                    for d in range(NH):
                        eng.dma_start(out=xsup(d), in_=xa[d * 128:(d + 1) * 128,
                                                         sup * 512:(sup + 1) * 512]).then_inc(st, 16)
                sch.inc("st", 16 * NH)

                def cb(egn, eg, sh, d, dd, _sup=sup):
                    if egn == "sync":
                        eg.wait_ge(ve, sh.val("ve"))
                        eg.dma_start(out=xnall[d * 128:(d + 1) * 128,
                                               _sup * 512:(_sup + 1) * 512],
                                     in_=xnsup(dd)).then_inc(st, 16)
                    sh.inc("st", 16)
                    sh.set(ln.tag + f"_xnb{dd}_free", sh.val("st"))

                ln.emit(eng_name, eng, sch, [xsup(d) for d in range(NH)], 0,
                        [xnsup(0), xnsup(1)], 0, xt_ready=("st", sch.val("st")), chunk_cb=cb)
                barrier(eng_name, eng, sch)

            if eng_name == "pool":
                for i in range(8):
                    eng.dma_start(out=kvw(i), in_=bk[i]).then_inc(ld, 16)
                for i in range(8):
                    eng.dma_start(out=kvw(8 + i), in_=bv[i]).then_inc(ld, 16)
            sch.inc("ld", 16 * 16)
            for sup in range(16):
                if eng_name == "sync":
                    if sup > 0:
                        eng.wait_ge(mm, sch.val("kv_xn_mm"))
                    for d in range(NH):
                        eng.dma_start(out=xnsup(d), in_=xnall[d * 128:(d + 1) * 128,
                                                            sup * 512:(sup + 1) * 512]).then_inc(st, 16)
                sch.inc("st", 16 * NH)
                for oc in range(8):
                    par = oc % 2
                    if eng_name == "pe":
                        if oc == 0:
                            eng.wait_ge(st, sch.val("st"))
                            if sup == 0:
                                eng.wait_ge(ld, sch.val("ld"))
                        g = sch.val(f"kps{par}_sc")
                        if g > 0:
                            eng.wait_ge(sc, g)
                        for d in range(NH):
                            ii = eng.matmul(ps[4 + par][0:64, :], kvw(oc)[:, d * 64:(d + 1) * 64],
                                            xnsup(d), start=(d == 0), stop=(d == NH - 1))
                        ii.then_inc(mm, 1)
                    sch.inc("mm")
                    if eng_name == "act":
                        eng.wait_ge(mm, sch.val("mm"))
                        g = sch.val(f"kev{par}_st")
                        if g > 0:
                            eng.wait_ge(st, g)
                        eng.copy(kevac[par][:, :], ps[4 + par][0:64, :]).then_inc(sc, 1)
                    sch.inc("sc")
                    sch.set(f"kps{par}_sc", sch.val("sc"))
                    if eng_name == "sync":
                        eng.wait_ge(sc, sch.val("sc"))
                        eng.dma_start(out=kfm[oc * 64:(oc + 1) * 64, sup * 512:(sup + 1) * 512],
                                      in_=kevac[par][:, :]).then_inc(st, 16)
                    sch.inc("st", 16)
                    sch.set(f"kev{par}_st", sch.val("st"))
                for tt in range(4):
                    par = tt % 2
                    if eng_name == "pe":
                        g = sch.val(f"vps{par}_sc")
                        if g > 0:
                            eng.wait_ge(sc, g)
                        for oc in range(8):
                            for d in range(NH):
                                ii = eng.matmul(ps[6 + par][:, oc * 64:(oc + 1) * 64],
                                                xnsup(d)[:, tt * 128:(tt + 1) * 128],
                                                kvw(8 + oc)[:, d * 64:(d + 1) * 64],
                                                start=(d == 0), stop=(d == NH - 1))
                        ii.then_inc(mm, 1)
                    sch.inc("mm")
                    if eng_name == "act":
                        eng.wait_ge(mm, sch.val("mm"))
                        g = sch.val(f"vev{par}_st")
                        if g > 0:
                            eng.wait_ge(st, g)
                        eng.copy(vevac[par][:, :], ps[6 + par][:, :]).then_inc(sc, 1)
                    sch.inc("sc")
                    sch.set(f"vps{par}_sc", sch.val("sc"))
                    if eng_name == "sync":
                        eng.wait_ge(sc, sch.val("sc"))
                        for g4 in range(4):
                            eng.dma_start(out=vsw[g4, :, (sup * 4 + tt) * 128:(sup * 4 + tt + 1) * 128],
                                          in_=vevac[par][:, g4 * 128:(g4 + 1) * 128]).then_inc(st, 16)
                    sch.inc("st", 64)
                    sch.set(f"vev{par}_st", sch.val("st"))
                sch.set("kv_xn_mm", sch.val("mm"))
            barrier(eng_name, eng, sch)

            for sp2 in range(2):
                if eng_name == "sync":
                    barrier(eng_name, eng, sch)
                    for d in range(NH):
                        eng.dma_start(out=xsup(d), in_=xs[d * 128:(d + 1) * 128,
                                                         sp2 * 512:(sp2 + 1) * 512]).then_inc(st, 16)
                sch.inc("st", 16 * NH)
                ln.emit(eng_name, eng, sch, [xsup(d) for d in range(NH)], 0,
                        [xnsup(d) for d in range(NH)], sp2 * 0, xt_ready=("st", sch.val("st")))
                for o in range(NH):
                    par = o % 2
                    if eng_name == "pool":
                        g = sch.val(f"wb{par}_mm")
                        if g > 0:
                            eng.wait_ge(mm, g)
                        eng.dma_start(out=wbuf[par][:, :], in_=bq[o]).then_inc(ld, 16)
                    sch.inc("ld", 16)
                    if eng_name == "pe":
                        eng.wait_ge(ld, sch.val("ld"))
                        eng.wait_ge(ve, sch.val("ve"))
                        g = sch.val(f"qps{par}_ve")
                        if g > 0:
                            eng.wait_ge(ve, g)
                        for d in range(NH):
                            ii = eng.matmul(ps[4 + par][:, :], wbuf[par][:, d * 128:(d + 1) * 128],
                                            xnsup(d), start=(d == 0), stop=(d == NH - 1))
                        ii.then_inc(mm, 1)
                    sch.inc("mm")
                    sch.set(f"wb{par}_mm", sch.val("mm"))
                    if eng_name == "dve":
                        eng.wait_ge(mm, sch.val("mm"))
                        eng.tensor_scalar(q_fm[o][:, sp2 * 512:(sp2 + 1) * 512], ps[4 + par][:, :],
                                          0.08838834764831845, None, ALU.mult).then_inc(ve, 1)
                    sch.inc("ve")
                    sch.set(f"qps{par}_ve", sch.val("ve"))
                barrier(eng_name, eng, sch)

            for g4 in range(4):
                if eng_name == "sync":
                    barrier(eng_name, eng, sch)
                    eng.dma_start(out=kgrow(), in_=kfm[g4 * 128:(g4 + 1) * 128, :]).then_inc(st, 16)
                    eng.dma_start(out=vgrow(), in_=vsw[g4]).then_inc(st, 16)
                sch.inc("st", 32)
                for rp in range(2):
                    for sp2 in range(2):
                        for kt in range(64):
                            par = kt % 2
                            if kt == 0 and eng_name == "dve":
                                barrier(eng_name, eng, sch)
                            if eng_name == "sync":
                                g = sch.val(f"hq{par}_mm")
                                if g > 0:
                                    eng.wait_ge(mm, g)
                                eng.dma_start(out=hrow[par][:, :],
                                              in_=fap(sp2, kt)).then_inc(st, 16)
                            sch.inc("st", 16)
                            hst = sch.val("st")
                            for rr in range(2):
                                r = rp * 2 + rr
                                h = g4 * 4 + r
                                if eng_name == "pe":
                                    if kt == 0 and rr == 0:
                                        eng.wait_ge(st, sch.val("st"))
                                    elif rr == 0:
                                        eng.wait_ge(st, hst)
                                    g = sch.val(f"sps{par}_sc")
                                    if g > 0:
                                        eng.wait_ge(sc, g)
                                    eng.matmul(ps[par][:, :], kgrow()[:, kt * 128:(kt + 1) * 128],
                                               q_fm[h][:, sp2 * 512:(sp2 + 1) * 512],
                                               start=True, stop=False)
                                    eng.matmul(ps[par][:, :], revMb[:, h * 128:(h + 1) * 128],
                                               hrow[par][:, :],
                                               start=False, stop=True).then_inc(mm, 1)
                                sch.inc("mm")
                                sch.set(f"hq{par}_mm", sch.val("mm"))
                                if eng_name == "act":
                                    eng.wait_ge(mm, sch.val("mm"))
                                    eng.activation(probs[par][:, :], ps[par][:, :],
                                                   AF.Exp).then_inc(sc, 1)
                                sch.inc("sc")
                                sch.set(f"sps{par}_sc", sch.val("sc"))
                                if eng_name == "pe":
                                    eng.wait_ge(sc, sch.val("sc"))
                                    if kt == 0:
                                        g = sch.val(f"avout{rr}_ve")
                                        if g > 0:
                                            eng.wait_ge(ve, g)
                                    eng.matmul(ps[2 + rr][:, :], vgrow()[:, kt * 128:(kt + 1) * 128],
                                               probs[par][:, :], start=(kt == 0), stop=(kt == 63))
                                    ii = eng.matmul(ps[6 + rr][0:1, :], ln.ones_b[:, :],
                                                    probs[par][:, :], start=(kt == 0), stop=(kt == 63))
                                    if kt == 63:
                                        ii.then_inc(mm, 1)
                                if kt == 63:
                                    sch.inc("mm")
                        for rr in range(2):
                            r = rp * 2 + rr
                            h = g4 * 4 + r
                            if eng_name == "dve":
                                eng.wait_ge(mm, sch.val("mm"))
                                eng.reciprocal(rcrow[0:1, 0:512], ps[6 + rr][0:1, :]).then_inc(ve, 1)
                            sch.inc("ve")
                            if eng_name == "pe":
                                eng.wait_ge(ve, sch.val("ve"))
                                g = sch.val("bc4_ve")
                                if g > 0:
                                    eng.wait_ge(ve, g)
                                eng.matmul(ps[4][:, :], ln.ones_f[:, :], rcrow[0:1, 0:512],
                                           start=True, stop=True).then_inc(mm, 1)
                            sch.inc("mm")
                            if eng_name == "dve":
                                eng.wait_ge(mm, sch.val("mm"))
                                eng.tensor_copy(rcpb[:, :], ps[4][:, :])
                                eng.tensor_tensor(attn[h][:, sp2 * 512:(sp2 + 1) * 512],
                                                  ps[2 + rr][:, :], rcpb[:, :],
                                                  ALU.mult).then_inc(ve, 1)
                            sch.inc("ve")
                            sch.set(f"avout{rr}_ve", sch.val("ve"))
                            sch.set("bc4_ve", sch.val("ve"))
            barrier(eng_name, eng, sch)

            for o in range(NH):
                if eng_name == "sync":
                    g = sch.val(f"xr{o % 2}_ve")
                    if g > 0:
                        eng.wait_ge(ve, g)
                    eng.dma_start(out=xres[o % 2][:, :], in_=xs[o * 128:(o + 1) * 128, :]).then_inc(st, 16)
                sch.inc("st", 16)
                if eng_name == "pool":
                    g = sch.val(f"wb{o % 2}_mm")
                    if g > 0:
                        eng.wait_ge(mm, g)
                    eng.dma_start(out=wbuf[o % 2][:, :], in_=bo[o]).then_inc(ld, 16)
                sch.inc("ld", 16)
                for sp2 in range(2):
                    par = sp2
                    if eng_name == "pe":
                        eng.wait_ge(ld, sch.val("ld"))
                        g = sch.val(f"yps{par}_ve")
                        if g > 0:
                            eng.wait_ge(ve, g)
                        for d in range(NH):
                            ii = eng.matmul(ps[4 + par][:, :], wbuf[o % 2][:, d * 128:(d + 1) * 128],
                                            attn[d][:, sp2 * 512:(sp2 + 1) * 512],
                                            start=(d == 0), stop=(d == NH - 1))
                        ii.then_inc(mm, 1)
                    sch.inc("mm")
                    sch.set(f"wb{o % 2}_mm", sch.val("mm"))
                    if eng_name == "dve":
                        eng.wait_ge(mm, sch.val("mm"))
                        eng.wait_ge(st, sch.val("st"))
                        g = sch.val(f"ysc{par}_st")
                        if g > 0:
                            eng.wait_ge(st, g)
                        eng.tensor_tensor(yscr[par][:, :], ps[4 + par][:, :],
                                          xres[o % 2][:, sp2 * 512:(sp2 + 1) * 512],
                                          ALU.add).then_inc(ve, 1)
                    sch.inc("ve")
                    sch.set(f"yps{par}_ve", sch.val("ve"))
                    if eng_name == "sync":
                        eng.wait_ge(ve, sch.val("ve"))
                        eng.dma_start(out=y[o * 128:(o + 1) * 128, sp2 * 512:(sp2 + 1) * 512],
                                      in_=yscr[par][:, :]).then_inc(st, 16)
                    sch.inc("st", 16)
                    sch.set(f"ysc{par}_st", sch.val("st"))
                sch.set(f"xr{o % 2}_ve", sch.val("ve"))

            if eng_name == "sync":
                eng.wait_ge(st, sch.val("st"))

        build_blocked(nc, gen, sems)
    return nc



PI = 3.141592653589793
TWO_PI = 6.283185307179586


def emit_rrsin(eng_name, eng, sch, sems, in_ap, out_ap, i32t, f32t, scrA, phase):
    ve, sc = sems["ve"], sems["sc"]
    if eng_name == "dve":
        eng.wait_ge(sems["mm"], sch.val("mm"))
        g = sch.val("rrs_sc")
        if g > 0:
            eng.wait_ge(sc, g)
        eng.tensor_scalar(scrA, in_ap, 1.0 / TWO_PI, phase / TWO_PI, ALU.mult, ALU.add)
        eng.tensor_copy(i32t, scrA)
        eng.tensor_copy(f32t, i32t)
        eng.tensor_scalar(f32t, f32t, -TWO_PI, phase, ALU.mult, ALU.add)
        eng.tensor_tensor(scrA, in_ap, f32t, ALU.add)
        eng.tensor_scalar(f32t, scrA, PI, None, ALU.is_gt)
        eng.scalar_tensor_tensor(scrA, f32t, -TWO_PI, scrA, ALU.mult, ALU.add)
        eng.tensor_scalar(f32t, scrA, -PI, None, ALU.is_lt)
        eng.scalar_tensor_tensor(scrA, f32t, TWO_PI, scrA, ALU.mult, ALU.add).then_inc(ve, 1)
    sch.inc("ve")
    if eng_name == "act":
        eng.wait_ge(ve, sch.val("ve"))
        eng.activation(out_ap, scrA, AF.Sin).then_inc(sc, 1)
    sch.inc("sc")
    sch.set("rrs_sc", sch.val("sc"))


def build_s1(dbg=False):
    nc = bass.Bass("TRN2", target_bir_lowering=False, debug=False)
    NH = 16
    M0 = 2.0 ** (-0.5)
    LN1E4 = 9.210340371976184
    xs = nc.dram_tensor("xs", [D, 1024], F32, kind="ExternalInput").ap()
    xa = nc.dram_tensor("xa", [D, S], F32, kind="ExternalInput").ap()
    wq = nc.dram_tensor("wq", [D, D], F32, kind="ExternalInput").ap()
    wkv = nc.dram_tensor("wkv", [D, D], F32, kind="ExternalInput").ap()
    wo = nc.dram_tensor("wo", [D, D], F32, kind="ExternalInput").ap()
    bo_in = nc.dram_tensor("bo", [NH, 128], F32, kind="ExternalInput").ap()
    qpos = nc.dram_tensor("qpos", [1, 1024], F32, kind="ExternalInput").ap()
    y = nc.dram_tensor("y", [D, 1024], F32, kind="ExternalOutput").ap()
    kind2 = "ExternalOutput" if dbg else "Internal"
    xnall = nc.dram_tensor("xnall", [D, S], BF16, kind=kind2)
    kfm = nc.dram_tensor("kfm", [1024, S], BF16, kind=kind2)
    vsw = nc.dram_tensor("vsw", [8, 128, S], BF16, kind=kind2)
    fdram = nc.dram_tensor("fdram", [2, 8832], BF16, kind=kind2)
    if dbg:
        qdbg = nc.dram_tensor("qdbg", [D, 1024], BF16, kind="ExternalOutput")
        adbg = nc.dram_tensor("adbg", [D, 1024], BF16, kind="ExternalOutput")
        sdbg = nc.dram_tensor("sdbg", [128, 256], BF16, kind="ExternalOutput")
        cdbg = nc.dram_tensor("cdbg", [128, 256], BF16, kind="ExternalOutput")
        ktdbg = nc.dram_tensor("ktdbg", [128, 1024], BF16, kind="ExternalOutput")

    ctx = ExitStack()
    with ctx:
        ps = [ctx.enter_context(nc.psum_tensor(f"ps{i}", [128, 512], F32)) for i in range(8)]
        ln = LNKit(nc, ctx, tw=512, banks=ps[0:4])

        AR = 49152
        arena = ctx.enter_context(nc.sbuf_tensor("arena", [128, AR], BF16))
        arena_f = arena.ap().bitcast(F32)
        xn_own = [arena[:, 32768 + d * 1024:32768 + (d + 1) * 1024] for d in range(NH)]
        q_fm = [ctx.enter_context(nc.sbuf_tensor(f"qfm{d}", [128, 1024], BF16)) for d in range(NH)]
        attn = q_fm
        qposb = ctx.enter_context(nc.sbuf_tensor("qposb", [1, 1024], F32))
        fut = ctx.enter_context(nc.sbuf_tensor("fut", [128, 69], F32))
        fd = ctx.enter_context(nc.sbuf_tensor("fd", [128, 69], F32))
        fng = ctx.enter_context(nc.sbuf_tensor("fng", [128, 69], F32))
        fab = ctx.enter_context(nc.sbuf_tensor("fab", [128, 69], F32))
        fmk = ctx.enter_context(nc.sbuf_tensor("fmk", [128, 69], F32))
        fq = ctx.enter_context(nc.sbuf_tensor("fq", [128, 69], F32))
        fbrow = ctx.enter_context(nc.sbuf_tensor("fbrow", [128, 69], BF16))
        hrow = [ctx.enter_context(nc.sbuf_tensor(f"hrow{p}", [128, 512], BF16)) for p in range(2)]
        revs = ctx.enter_context(nc.sbuf_tensor("revs", [128, 128], F32))
        io2 = ctx.enter_context(nc.sbuf_tensor("io2", [128, 128], F32))
        revMb = ctx.enter_context(nc.sbuf_tensor("revMb", [128, 2048], BF16))
        qbc = ctx.enter_context(nc.sbuf_tensor("qbc", [128, 2], F32))
        r64row = ctx.enter_context(nc.sbuf_tensor("r64row", [1, 128], F32))
        onesr = ctx.enter_context(nc.sbuf_tensor("onesr", [1, 512], F32))
        jnk = ctx.enter_context(nc.sbuf_tensor("jnk", [1, 16], F32))
        permM = ctx.enter_context(nc.sbuf_tensor("permM", [128, 128], BF16))
        freqR = ctx.enter_context(nc.sbuf_tensor("freqR", [1, 128], F32))
        freqB = ctx.enter_context(nc.sbuf_tensor("freqB", [128, 64], F32))
        cosq = ctx.enter_context(nc.sbuf_tensor("cosq", [128, 1024], BF16))
        sinsq = ctx.enter_context(nc.sbuf_tensor("sinsq", [128, 1024], BF16))
        coskt = ctx.enter_context(nc.sbuf_tensor("coskt", [128, 256], BF16))
        sinkt = ctx.enter_context(nc.sbuf_tensor("sinkt", [128, 256], BF16))
        rrF = ctx.enter_context(nc.sbuf_tensor("rrF", [128, 1024], F32))
        rrI = ctx.enter_context(nc.sbuf_tensor("rrI", [128, 512], I32))
        bscr = ctx.enter_context(nc.sbuf_tensor("bscr", [128, 1024], BF16))
        cB = ctx.enter_context(nc.sbuf_tensor("cB", [128, 1024], BF16))
        crow = ctx.enter_context(nc.sbuf_tensor("crow", [1, 1024], F32))
        mrow = ctx.enter_context(nc.sbuf_tensor("mrow", [1, 1024], F32))
        mqB = ctx.enter_context(nc.sbuf_tensor("mqB", [128, 1024], BF16))
        xnsum = ctx.enter_context(nc.sbuf_tensor("xnsum", [128, 17], F32))
        xnmb = ctx.enter_context(nc.sbuf_tensor("xnmb", [128, 16], BF16))
        vmean = ctx.enter_context(nc.sbuf_tensor("vmean", [128, 8], F32))
        probs = [ctx.enter_context(nc.sbuf_tensor(f"probs{p}", [128, 512], BF16)) for p in range(2)]
        rcrow = ctx.enter_context(nc.sbuf_tensor("rcrow", [1, 512], F32))
        rcpb = ctx.enter_context(nc.sbuf_tensor("rcpb", [128, 512], F32))
        ktt = ctx.enter_context(nc.sbuf_tensor("ktt", [128, 1024], BF16))
        kt2 = ctx.enter_context(nc.sbuf_tensor("kt2", [128, 1024], BF16))
        kevac = [ctx.enter_context(nc.sbuf_tensor(f"kevac{p}", [128, 128], BF16)) for p in range(2)]
        vevac = [ctx.enter_context(nc.sbuf_tensor(f"vevac{p}", [128, 512], BF16)) for p in range(2)]
        xres = [ctx.enter_context(nc.sbuf_tensor(f"xres{p}", [128, 512], F32)) for p in range(2)]
        yscr = [ctx.enter_context(nc.sbuf_tensor(f"yscr{p}", [128, 512], F32)) for p in range(2)]
        boT = ctx.enter_context(nc.sbuf_tensor("boT", [128, NH], F32))
        srow = ctx.enter_context(nc.sbuf_tensor("srow", [NH, 128], F32))
        identf = ctx.enter_context(nc.sbuf_tensor("identf", [128, 128], F32))
        identb = ctx.enter_context(nc.sbuf_tensor("identb", [128, 128], BF16))
        iden_io = ctx.enter_context(nc.sbuf_tensor("iden_io", [128, 128], F32))

        ld = ctx.enter_context(nc.semaphore("ld"))
        mm = ctx.enter_context(nc.semaphore("mm"))
        ve = ctx.enter_context(nc.semaphore("ve"))
        sc = ctx.enter_context(nc.semaphore("sc"))
        st = ctx.enter_context(nc.semaphore("st"))
        gp = ctx.enter_context(nc.semaphore("gp"))
        sems = {"ld": ld, "mm": mm, "ve": ve, "sc": sc, "st": st, "gp": gp}

        def wtile(d):
            return arena[:, d * 2048:(d + 1) * 2048]

        def xsup(d):
            return arena_f[:, d * 512:(d + 1) * 512]

        def xnsup(d):
            return arena[:, 32768 + d * 512:32768 + (d + 1) * 512]

        def kgrow():
            return arena[:, 0:8192]

        def vgrow():
            return arena[:, 8192:16384]

        def barrier(eng_name, eng, sch):
            for nm in ("mm", "ve", "sc", "ld", "st", "gp"):
                v = sch.val(nm)
                if v > 0:
                    eng.wait_ge(sems[nm], v)

        import copy as _copy
        _fap0 = fdram.ap()

        def fap(sp2, kt):
            v = _copy.copy(_fap0)
            v.ap = type(_fap0.ap)([(1, 128), (1, 512)])
            v.offset = sp2 * 8832 + 8192 - kt * 128
            return v

        def flin(sp2):
            v = _copy.copy(_fap0)
            v.ap = type(_fap0.ap)([(1, 128), (128, 69)])
            v.offset = sp2 * 8832
            return v

        def gen(eng_name, eng, sch):
            ln.emit_init(eng_name, eng, sch)
            if eng_name == "pool":
                eng.iota(fut[:, :], pattern=[[128, 69]], base=0, channel_multiplier=1,
                         allow_small_or_imprecise_dtypes=True)
                eng.iota(io2[:, :], pattern=[[1, 128]], base=0, channel_multiplier=1,
                         allow_small_or_imprecise_dtypes=True)
                eng.iota(iden_io[:, :], pattern=[[1, 128]], base=0, channel_multiplier=-1,
                         allow_small_or_imprecise_dtypes=True)
                eng.iota(freqR[:, :], pattern=[[1, 128]], base=0, channel_multiplier=0,
                         allow_small_or_imprecise_dtypes=True).then_inc(gp, 1)
                eng.dma_start(out=qposb[:, :], in_=qpos[:, :]).then_inc(ld, 16)
                eng.dma_start(out=srow[:, :], in_=bo_in[:, :]).then_inc(ld, 16)
            sch.inc("gp")
            sch.inc("ld", 32)
            if eng_name == "dve":
                eng.wait_ge(ld, sch.val("ld"))
                eng.wait_ge(gp, sch.val("gp"))
                eng.tensor_scalar(identf[:, :], iden_io[:, :], 0.0, None, ALU.is_equal)
                eng.tensor_copy(identb[:, :], identf[:, :])
                eng.tensor_scalar(rrF[:, 0:128], iden_io[:, :], 64.0, None, ALU.is_equal)
                eng.tensor_scalar(rrF[:, 128:256], iden_io[:, :], -64.0, None, ALU.is_equal)
                eng.tensor_tensor(rrF[:, 0:128], rrF[:, 0:128], rrF[:, 128:256], ALU.add)
                eng.tensor_copy(permM[:, :], rrF[:, 0:128])
                eng.tensor_scalar(r64row[0:1, :], iden_io[0:1, :], 63.5, -1e30,
                                  ALU.is_lt, ALU.mult)
                eng.memset(onesr[0:1, :], 1.0)
                eng.tensor_scalar(revs[:, :], io2[:, :], 127.0, None, ALU.is_equal)
                eng.tensor_scalar(freqR[0:1, 0:64], freqR[0:1, 0:64], -LN1E4 / 64.0, None,
                                  ALU.mult).then_inc(ve, 1)
            sch.inc("ve")
            if eng_name == "act":
                eng.wait_ge(ve, sch.val("ve"))
                eng.activation(freqR[0:1, 0:64], freqR[0:1, 0:64], AF.Exp).then_inc(sc, 1)
            sch.inc("sc")
            if eng_name == "dve":
                eng.wait_ge(sc, sch.val("sc"))
                eng.tensor_copy(freqR[0:1, 64:128], freqR[0:1, 0:64])
                ii = None
                for h in range(NH):
                    ii = eng.tensor_scalar(revMb[:, h * 128:(h + 1) * 128], revs[:, :],
                                           -(M0 ** (h + 1)), None, ALU.mult)
                ii.then_inc(ve, 1)
            sch.inc("ve")
            if eng_name == "dve":
                eng.wait_ge(sc, sch.val("sc"))
                eng.tensor_scalar(mrow[0:1, :], qposb[0:1, :], 63.5, None, ALU.is_lt).then_inc(ve, 1)
            sch.inc("ve")
            if eng_name == "pe":
                eng.wait_ge(ve, sch.val("ve"))
                eng.matmul(ps[4][:, 0:1], ln.ones_f[:, :], qposb[0:1, 0:1],
                           start=True, stop=True)
                eng.matmul(ps[4][:, 1:2], ln.ones_f[:, :], qposb[0:1, 512:513],
                           start=True, stop=True)
                eng.matmul(ps[5][:, :], ln.ones_f[:, :], mrow[0:1, 0:512], start=True, stop=True)
                eng.matmul(ps[6][:, :], ln.ones_f[:, :], mrow[0:1, 512:1024], start=True, stop=True)
                eng.matmul(ps[7][:, 0:64], ln.ones_f[:, :], freqR[0:1, 0:64], start=True, stop=True)
                eng.transpose(ps[3][0:128, 0:NH], srow[:, :], identf[0:NH, 0:NH]).then_inc(mm, 1)
            sch.inc("mm")
            if eng_name == "dve":
                eng.wait_ge(mm, sch.val("mm"))
                eng.tensor_copy(qbc[:, 0:2], ps[4][:, 0:2])
                eng.tensor_copy(mqB[:, 0:512], ps[5][:, :])
                eng.tensor_copy(mqB[:, 512:1024], ps[6][:, :])
                eng.tensor_copy(freqB[:, :], ps[7][:, 0:64])
                eng.tensor_copy(boT[:, :], ps[3][0:128, 0:NH])
                eng.memset(xnsum[:, :], 0.0).then_inc(ve, 1)
            sch.inc("ve")
            for sp2 in range(2):
                if eng_name == "dve":
                    if sp2 == 1:
                        eng.wait_ge(st, sch.val("st"))
                    eng.tensor_scalar(fd[:, :], fut[:, :], -8319.0,
                                      qbc[:, sp2:sp2 + 1], ALU.add, ALU.add)
                    eng.memset(jnk[0:1, 0:1], 0.0)
                    eng.memset(jnk[0:1, 1:2], 0.0)
                    eng.tensor_scalar(fng[:, :], fd[:, :], -1.0, None, ALU.mult)
                    eng.memset(jnk[0:1, 2:3], 0.0)
                    eng.memset(jnk[0:1, 3:4], 0.0)
                    eng.tensor_tensor(fab[:, :], fd[:, :], fng[:, :], ALU.max)
                    eng.memset(jnk[0:1, 4:5], 0.0)
                    eng.memset(jnk[0:1, 5:6], 0.0)
                    eng.tensor_scalar(fmk[:, :], fab[:, :], 256.5, 1e30, ALU.is_le, ALU.mult)
                    eng.tensor_scalar(fq[:, :], fab[:, :], -256.0, None, ALU.add)
                    eng.memset(jnk[0:1, 6:7], 0.0)
                    eng.memset(jnk[0:1, 7:8], 0.0)
                    eng.tensor_tensor(fbrow[:, :], fq[:, :], fmk[:, :],
                                      ALU.add).then_inc(ve, 1)
                sch.inc("ve")
                if eng_name == "sync":
                    eng.wait_ge(ve, sch.val("ve"))
                    eng.dma_start(out=flin(sp2), in_=fbrow[:, :],
                                  allow_slow_non_contiguous=True).then_inc(st, 16)
                sch.inc("st", 16)

            for half in range(2):
                if eng_name == "pe":
                    eng.wait_ge(ve, sch.val("ve"))
                    g = sch.val("rr_ve")
                    if g > 0:
                        eng.wait_ge(ve, g)
                    eng.matmul(ps[4 + half][:, :], freqR[0:1, 0:128],
                               qposb[0:1, half * 512:(half + 1) * 512],
                               start=True, stop=True).then_inc(mm, 1)
                sch.inc("mm")
                emit_rrsin(eng_name, eng, sch, sems, ps[4 + half][:, :],
                           sinsq[:, half * 512:(half + 1) * 512],
                           rrI[:, :], rrF[:, 0:512], rrF[:, 512:1024], 0.0)
                emit_rrsin(eng_name, eng, sch, sems, ps[4 + half][:, :],
                           cosq[:, half * 512:(half + 1) * 512],
                           rrI[:, :], rrF[:, 0:512], rrF[:, 512:1024], PI / 2)
                sch.set("rr_ve", sch.val("ve"))
            if eng_name == "dve":
                eng.wait_ge(sc, sch.val("sc"))
                eng.tensor_scalar(sinsq[0:64, :], sinsq[0:64, :], -1.0, None, ALU.mult).then_inc(ve, 1)
            sch.inc("ve")

            for sp2 in range(2):
                if eng_name == "sync":
                    barrier(eng_name, eng, sch)
                    for d in range(NH):
                        eng.dma_start(out=xsup(d), in_=xs[d * 128:(d + 1) * 128,
                                                         sp2 * 512:(sp2 + 1) * 512]).then_inc(st, 16)
                sch.inc("st", 16 * NH)
                ln.emit(eng_name, eng, sch, [xsup(d) for d in range(NH)], 0,
                        xn_own, sp2 * 512, xt_ready=("st", sch.val("st")))
                for d in range(NH):
                    if eng_name == "act":
                        eng.wait_ge(ve, sch.val("ve"))
                        g = sch.val("abs_mm")
                        if g > 0:
                            eng.wait_ge(mm, g)
                        eng.activation(probs[0][:, :], xn_own[d][:, sp2 * 512:(sp2 + 1) * 512],
                                       AF.Abs).then_inc(sc, 1)
                    sch.inc("sc")
                    if eng_name == "pe":
                        eng.wait_ge(sc, sch.val("sc"))
                        ii = eng.matmul(ps[4][0:1, :], ln.ones_b[:, :], probs[0][:, :],
                                        start=(d == 0), stop=(d == NH - 1))
                        if d == NH - 1:
                            ii.then_inc(mm, 1)
                        else:
                            ii.then_inc(mm, 1)
                    sch.inc("mm")
                    sch.set("abs_mm", sch.val("mm"))
                if eng_name == "dve":
                    eng.wait_ge(mm, sch.val("mm"))
                    eng.tensor_copy(crow[0:1, sp2 * 512:(sp2 + 1) * 512], ps[4][0:1, :]).then_inc(ve, 1)
                sch.inc("ve")
                barrier(eng_name, eng, sch)
            if eng_name == "act":
                eng.activation(crow[0:1, :], crow[0:1, :], AF.Sigmoid).then_inc(sc, 1)
            sch.inc("sc")
            if eng_name == "dve":
                eng.wait_ge(sc, sch.val("sc"))
                eng.tensor_scalar(crow[0:1, :], crow[0:1, :], 0.08838834764831845, None,
                                  ALU.mult).then_inc(ve, 1)
            sch.inc("ve")
            for half in range(2):
                if eng_name == "pe":
                    eng.wait_ge(ve, sch.val("ve"))
                    eng.matmul(ps[4 + half][:, :], ln.ones_f[:, :],
                               crow[0:1, half * 512:(half + 1) * 512],
                               start=True, stop=True).then_inc(mm, 1)
                sch.inc("mm")
            if eng_name == "dve":
                eng.wait_ge(mm, sch.val("mm"))
                eng.tensor_copy(cB[:, 0:512], ps[4][:, :])
                eng.tensor_copy(cB[:, 512:1024], ps[5][:, :]).then_inc(ve, 1)
            sch.inc("ve")

            if eng_name == "pool":
                barrier(eng_name, eng, sch)
                for d in range(NH):
                    eng.dma_start(out=wtile(d), in_=wq[d * 128:(d + 1) * 128, :]).then_inc(ld, 16)
            sch.inc("ld", 16 * NH)
            for o in range(NH):
                for sp2 in range(2):
                    par = sp2
                    if eng_name == "pe":
                        if o == 0 and sp2 == 0:
                            eng.wait_ge(ld, sch.val("ld"))
                        g = sch.val(f"qps{par}_ve")
                        if g > 0:
                            eng.wait_ge(ve, g)
                        for d in range(NH):
                            ii = eng.matmul(ps[4 + par][:, :], wtile(d)[:, o * 128:(o + 1) * 128],
                                            xn_own[d][:, sp2 * 512:(sp2 + 1) * 512],
                                            start=(d == 0), stop=(d == NH - 1))
                        ii.then_inc(mm, 1)
                    sch.inc("mm")
                    if eng_name == "dve":
                        eng.wait_ge(mm, sch.val("mm"))
                        eng.tensor_copy(q_fm[o][:, sp2 * 512:(sp2 + 1) * 512],
                                        ps[4 + par][:, :]).then_inc(ve, 1)
                    sch.inc("ve")
                    sch.set(f"qps{par}_ve", sch.val("ve"))
                if eng_name == "pe":
                    eng.wait_ge(ve, sch.val("ve"))
                    g = sch.val("ropeps_ve")
                    if g > 0:
                        eng.wait_ge(ve, g)
                    eng.matmul(ps[6][:, :], permM[:, :], q_fm[o][:, 0:512], start=True, stop=True)
                    eng.matmul(ps[7][:, :], permM[:, :], q_fm[o][:, 512:1024],
                               start=True, stop=True).then_inc(mm, 1)
                sch.inc("mm")
                if eng_name == "dve":
                    eng.wait_ge(mm, sch.val("mm"))
                    for half in range(2):
                        sl = slice(half * 512, (half + 1) * 512)
                        eng.tensor_copy(bscr[:, 0:512], ps[6 + half][:, :])
                        eng.tensor_tensor(bscr[:, 512:1024], q_fm[o][:, sl], cosq[:, sl], ALU.mult)
                        eng.tensor_tensor(bscr[:, 0:512], bscr[:, 0:512], sinsq[:, sl], ALU.mult)
                        eng.tensor_tensor(bscr[:, 0:512], bscr[:, 0:512],
                                          bscr[:, 512:1024], ALU.add)
                        ii = eng.tensor_tensor(q_fm[o][:, sl], bscr[:, 0:512], cB[:, sl], ALU.mult)
                    ii.then_inc(ve, 1)
                sch.inc("ve")
                sch.set("ropeps_ve", sch.val("ve"))
            barrier(eng_name, eng, sch)
            if dbg:
                if eng_name == "sync":
                    for d in range(NH):
                        eng.dma_start(out=qdbg[d * 128:(d + 1) * 128, :],
                                      in_=q_fm[d][:, :]).then_inc(st, 16)
                sch.inc("st", 16 * NH)
                barrier(eng_name, eng, sch)

            for sup in range(16):
                if eng_name == "sync":
                    barrier(eng_name, eng, sch)
                    for d in range(NH):
                        eng.dma_start(out=xsup(d), in_=xa[d * 128:(d + 1) * 128,
                                                         sup * 512:(sup + 1) * 512]).then_inc(st, 16)
                sch.inc("st", 16 * NH)

                def cb(egn, eg, sh, d, dd, _sup=sup):
                    if egn == "sync":
                        eg.wait_ge(ve, sh.val("ve"))
                        eg.dma_start(out=xnall[d * 128:(d + 1) * 128,
                                               _sup * 512:(_sup + 1) * 512],
                                     in_=xnsup(dd)).then_inc(st, 16)
                    sh.inc("st", 16)
                    sh.set(ln.tag + f"_xnb{dd}_free", sh.val("st"))
                    if egn == "dve":
                        eg.tensor_reduce(xnsum[:, 16:17], xnsup(dd), mybir.AxisListType.X, ALU.add)
                        eg.tensor_tensor(xnsum[:, d:d + 1], xnsum[:, d:d + 1],
                                         xnsum[:, 16:17], ALU.add)

                ln.emit(eng_name, eng, sch, [xsup(d) for d in range(NH)], 0,
                        [xnsup(0), xnsup(1)], 0, xt_ready=("st", sch.val("st")), chunk_cb=cb)
                barrier(eng_name, eng, sch)

            if eng_name == "pool":
                for d in range(NH):
                    eng.dma_start(out=wtile(d), in_=wkv[d * 128:(d + 1) * 128, :]).then_inc(ld, 16)
            sch.inc("ld", 16 * NH)
            for sup in range(16):
                if eng_name == "sync":
                    if sup > 0:
                        eng.wait_ge(mm, sch.val("kv_xn_mm"))
                    for d in range(NH):
                        eng.dma_start(out=xnsup(d), in_=xnall[d * 128:(d + 1) * 128,
                                                            sup * 512:(sup + 1) * 512]).then_inc(st, 16)
                sch.inc("st", 16 * NH)
                for tt in range(4):
                    if eng_name == "pool":
                        g = sch.val("krr_ve")
                        if g > 0:
                            eng.wait_ge(ve, g)
                        eng.iota(rrF[:, 0:1], pattern=[[0, 1]], base=sup * 512 + tt * 128,
                                 channel_multiplier=1,
                                 allow_small_or_imprecise_dtypes=True).then_inc(gp, 1)
                    sch.inc("gp")
                    if eng_name == "dve":
                        eng.wait_ge(gp, sch.val("gp"))
                        g = sch.val("rrs_sc")
                        if g > 0:
                            eng.wait_ge(sc, g)
                        eng.tensor_scalar(rrF[:, 64:128], freqB[:, :], rrF[:, 0:1],
                                          None, ALU.mult).then_inc(ve, 1)
                    sch.inc("ve")
                    emit_rrsin(eng_name, eng, sch, sems, rrF[:, 64:128],
                               sinkt[:, tt * 64:(tt + 1) * 64],
                               rrI[:, 0:64], rrF[:, 128:192], rrF[:, 192:256], 0.0)
                    emit_rrsin(eng_name, eng, sch, sems, rrF[:, 64:128],
                               coskt[:, tt * 64:(tt + 1) * 64],
                               rrI[:, 0:64], rrF[:, 128:192], rrF[:, 192:256], PI / 2)
                    sch.set("krr_ve", sch.val("ve"))
                for tt in range(4):
                    for oc2 in range(2):
                        if eng_name == "pe":
                            if tt == 0 and oc2 == 0:
                                eng.wait_ge(st, sch.val("st"))
                                eng.wait_ge(sc, sch.val("sc"))
                                if sup == 0:
                                    eng.wait_ge(ld, sch.val("ld"))
                            g = sch.val(f"kps{oc2}_sc")
                            if g > 0:
                                eng.wait_ge(sc, g)
                            for d in range(NH):
                                ii = eng.matmul(ps[4 + oc2][:, :],
                                                xnsup(d)[:, tt * 128:(tt + 1) * 128],
                                                wtile(d)[:, oc2 * 512:(oc2 + 1) * 512],
                                                start=(d == 0), stop=(d == NH - 1))
                            ii.then_inc(mm, 1)
                        sch.inc("mm")
                        if eng_name == "act":
                            eng.wait_ge(mm, sch.val("mm"))
                            g = sch.val("ktt_ve")
                            if g > 0 and oc2 == 0:
                                eng.wait_ge(ve, g)
                            g = sch.val("ktr_mm")
                            if g > 0 and oc2 == 0:
                                eng.wait_ge(mm, g)
                            eng.copy(ktt[:, oc2 * 512:(oc2 + 1) * 512],
                                     ps[4 + oc2][:, :]).then_inc(sc, 1)
                        sch.inc("sc")
                        sch.set(f"kps{oc2}_sc", sch.val("sc"))
                    if eng_name == "dve":
                        eng.wait_ge(sc, sch.val("sc"))
                        g = sch.val("ktr_mm")
                        if g > 0:
                            eng.wait_ge(mm, g)
                        for kvh in range(8):
                            c0 = kvh * 128
                            cs = slice(tt * 64, (tt + 1) * 64)
                            eng.tensor_tensor(kt2[:, c0:c0 + 64], ktt[:, c0:c0 + 64],
                                              coskt[:, cs], ALU.mult)
                            eng.tensor_tensor(bscr[:, 0:64], ktt[:, c0 + 64:c0 + 128],
                                              sinkt[:, cs], ALU.mult)
                            eng.tensor_tensor(kt2[:, c0:c0 + 64], kt2[:, c0:c0 + 64],
                                              bscr[:, 0:64], ALU.subtract)
                            eng.tensor_tensor(kt2[:, c0 + 64:c0 + 128], ktt[:, c0 + 64:c0 + 128],
                                              coskt[:, cs], ALU.mult)
                            eng.tensor_tensor(bscr[:, 64:128], ktt[:, c0:c0 + 64],
                                              sinkt[:, cs], ALU.mult)
                            ii = eng.tensor_tensor(kt2[:, c0 + 64:c0 + 128],
                                                   kt2[:, c0 + 64:c0 + 128],
                                                   bscr[:, 64:128], ALU.add)
                        ii.then_inc(ve, 1)
                    sch.inc("ve")
                    sch.set("ktt_ve", sch.val("ve"))
                    for kvh in range(8):
                        par = kvh % 2
                        if eng_name == "pe":
                            eng.wait_ge(ve, sch.val("ve"))
                            g = sch.val(f"ktr{par}_sc")
                            if g > 0:
                                eng.wait_ge(sc, g)
                            eng.transpose(ps[6 + par].ap().bitcast(BF16)[:, 0:128],
                                          kt2[:, kvh * 128:(kvh + 1) * 128],
                                          identb[:, :]).then_inc(mm, 1)
                        sch.inc("mm")
                        if eng_name == "act":
                            eng.wait_ge(mm, sch.val("mm"))
                            g = sch.val(f"kev{par}_st")
                            if g > 0:
                                eng.wait_ge(st, g)
                            eng.copy(kevac[par][:, :],
                                     ps[6 + par].ap().bitcast(BF16)[:, 0:128]).then_inc(sc, 1)
                        sch.inc("sc")
                        sch.set(f"ktr{par}_sc", sch.val("sc"))
                        if eng_name == "sync":
                            eng.wait_ge(sc, sch.val("sc"))
                            eng.dma_start(out=kfm[kvh * 128:(kvh + 1) * 128,
                                                  sup * 512 + tt * 128:sup * 512 + (tt + 1) * 128],
                                          in_=kevac[par][:, :]).then_inc(st, 16)
                        sch.inc("st", 16)
                        sch.set(f"kev{par}_st", sch.val("st"))
                    sch.set("ktr_mm", sch.val("mm"))
                    for oc2 in range(2):
                        par = oc2
                        if eng_name == "pe":
                            g = sch.val(f"vps{par}_sc")
                            if g > 0:
                                eng.wait_ge(sc, g)
                            for d in range(NH):
                                ii = eng.matmul(ps[4 + par][:, :],
                                                xnsup(d)[:, tt * 128:(tt + 1) * 128],
                                                wtile(d)[:, 1024 + oc2 * 512:1024 + (oc2 + 1) * 512],
                                                start=(d == 0), stop=(d == NH - 1))
                            ii.then_inc(mm, 1)
                        sch.inc("mm")
                        if eng_name == "act":
                            eng.wait_ge(mm, sch.val("mm"))
                            g = sch.val(f"vev{par}_st")
                            if g > 0:
                                eng.wait_ge(st, g)
                            eng.copy(vevac[par][:, :], ps[4 + par][:, :]).then_inc(sc, 1)
                        sch.inc("sc")
                        sch.set(f"vps{par}_sc", sch.val("sc"))
                        if eng_name == "sync":
                            eng.wait_ge(sc, sch.val("sc"))
                            for g4 in range(4):
                                kvh = oc2 * 4 + g4
                                eng.dma_start(out=vsw[kvh, :, (sup * 4 + tt) * 128:(sup * 4 + tt + 1) * 128],
                                              in_=vevac[par][:, g4 * 128:(g4 + 1) * 128]).then_inc(st, 16)
                        sch.inc("st", 64)
                        sch.set(f"vev{par}_st", sch.val("st"))
                sch.set("kv_xn_mm", sch.val("mm"))
            if dbg:
                if eng_name == "sync":
                    barrier(eng_name, eng, sch)
                    eng.dma_start(out=sdbg.ap(), in_=sinkt[:, :]).then_inc(st, 16)
                    eng.dma_start(out=cdbg.ap(), in_=coskt[:, :]).then_inc(st, 16)
                    eng.dma_start(out=ktdbg.ap(), in_=kt2[:, :]).then_inc(st, 16)
                sch.inc("st", 48)
            if eng_name == "dve":
                eng.wait_ge(ve, sch.val("ve"))
                eng.tensor_scalar(xnmb[:, :], xnsum[:, 0:16], 1.0 / 8192.0, None,
                                  ALU.mult).then_inc(ve, 1)
            sch.inc("ve")
            for o in range(8):
                if eng_name == "pe":
                    eng.wait_ge(ve, sch.val("ve"))
                    for d in range(NH):
                        ii = eng.matmul(ps[4][:, o:o + 1],
                                        wtile(d)[:, 1024 + o * 128:1024 + (o + 1) * 128],
                                        xnmb[:, d:d + 1], start=(d == 0), stop=(d == NH - 1))
                    ii.then_inc(mm, 1)
                sch.inc("mm")
            if eng_name == "dve":
                eng.wait_ge(mm, sch.val("mm"))
                eng.tensor_copy(vmean[:, :], ps[4][:, 0:8]).then_inc(ve, 1)
            sch.inc("ve")
            barrier(eng_name, eng, sch)

            for kvh in range(8):
                if eng_name == "sync":
                    barrier(eng_name, eng, sch)
                    eng.dma_start(out=kgrow(), in_=kfm[kvh * 128:(kvh + 1) * 128, :]).then_inc(st, 16)
                    eng.dma_start(out=vgrow(), in_=vsw[kvh]).then_inc(st, 16)
                sch.inc("st", 32)
                for sp2 in range(2):
                    for kt in range(64):
                        par = kt % 2
                        if eng_name == "sync":
                            g = sch.val(f"hq{par}_mm")
                            if g > 0:
                                eng.wait_ge(mm, g)
                            eng.dma_start(out=hrow[par][:, :],
                                          in_=fap(sp2, kt)).then_inc(st, 16)
                        sch.inc("st", 16)
                        hst = sch.val("st")
                        for rep in range(2):
                            h = kvh * 2 + rep
                            if eng_name == "pe":
                                if kt == 0 and rep == 0:
                                    eng.wait_ge(st, sch.val("st"))
                                elif rep == 0:
                                    eng.wait_ge(st, hst)
                                g = sch.val(f"sps{par}_sc")
                                if g > 0:
                                    eng.wait_ge(sc, g)
                                eng.matmul(ps[par][:, :], kgrow()[:, kt * 128:(kt + 1) * 128],
                                           q_fm[h][:, sp2 * 512:(sp2 + 1) * 512],
                                           start=True, stop=False)
                                ii = eng.matmul(ps[par][:, :], revMb[:, h * 128:(h + 1) * 128],
                                                hrow[par][:, :],
                                                start=False, stop=(kt != 0))
                                if kt == 0:
                                    ii = eng.matmul(ps[par][:, :], r64row[0:1, :],
                                                    onesr[0:1, :], start=False, stop=True)
                                ii.then_inc(mm, 1)
                            sch.inc("mm")
                            sch.set(f"hq{par}_mm", sch.val("mm"))
                            if eng_name == "act":
                                eng.wait_ge(mm, sch.val("mm"))
                                eng.activation(probs[par][:, :], ps[par][:, :],
                                               AF.Exp).then_inc(sc, 1)
                            sch.inc("sc")
                            sch.set(f"sps{par}_sc", sch.val("sc"))
                            if eng_name == "pe":
                                eng.wait_ge(sc, sch.val("sc"))
                                if kt == 0:
                                    g = sch.val(f"avout{rep}_ve")
                                    if g > 0:
                                        eng.wait_ge(ve, g)
                                eng.matmul(ps[2 + rep][:, :], vgrow()[:, kt * 128:(kt + 1) * 128],
                                           probs[par][:, :], start=(kt == 0), stop=(kt == 63))
                                ii = eng.matmul(ps[6 + rep][0:1, :], ln.ones_b[:, :],
                                                probs[par][:, :], start=(kt == 0), stop=(kt == 63))
                                if kt == 63:
                                    ii.then_inc(mm, 1)
                            if kt == 63:
                                sch.inc("mm")
                    for rep in range(2):
                        h = kvh * 2 + rep
                        if eng_name == "dve":
                            eng.wait_ge(mm, sch.val("mm"))
                            eng.reciprocal(rcrow[0:1, :], ps[6 + rep][0:1, :]).then_inc(ve, 1)
                        sch.inc("ve")
                        if eng_name == "pe":
                            eng.wait_ge(ve, sch.val("ve"))
                            g = sch.val("bc4_ve")
                            if g > 0:
                                eng.wait_ge(ve, g)
                            eng.matmul(ps[4][:, :], ln.ones_f[:, :], rcrow[0:1, :],
                                       start=True, stop=True).then_inc(mm, 1)
                        sch.inc("mm")
                        if eng_name == "dve":
                            eng.wait_ge(mm, sch.val("mm"))
                            eng.tensor_copy(rcpb[:, :], ps[4][:, :])
                            eng.tensor_tensor(attn[h][:, sp2 * 512:(sp2 + 1) * 512],
                                              ps[2 + rep][:, :], rcpb[:, :], ALU.mult).then_inc(ve, 1)
                        sch.inc("ve")
                        sch.set(f"avout{rep}_ve", sch.val("ve"))
                        sch.set("bc4_ve", sch.val("ve"))
            barrier(eng_name, eng, sch)
            for h in range(NH):
                if eng_name == "dve":
                    eng.tensor_tensor(bscr[:, 0:1024], attn[h][:, :], mqB[:, :], ALU.mult)
                    eng.tensor_tensor(attn[h][:, :], attn[h][:, :], bscr[:, 0:1024], ALU.subtract)
                    eng.tensor_scalar(bscr[:, 0:1024], mqB[:, :],
                                      vmean[:, h // 2:h // 2 + 1], None, ALU.mult)
                    ii = eng.tensor_tensor(attn[h][:, :], attn[h][:, :],
                                           bscr[:, 0:1024], ALU.add)
                    if h == NH - 1:
                        ii.then_inc(ve, 1)
            sch.inc("ve")
            barrier(eng_name, eng, sch)
            if dbg:
                if eng_name == "sync":
                    for d in range(NH):
                        eng.dma_start(out=adbg[d * 128:(d + 1) * 128, :],
                                      in_=attn[d][:, :]).then_inc(st, 16)
                sch.inc("st", 16 * NH)
                barrier(eng_name, eng, sch)

            if eng_name == "pool":
                for d in range(NH):
                    eng.dma_start(out=wtile(d), in_=wo[d * 128:(d + 1) * 128, :]).then_inc(ld, 16)
            sch.inc("ld", 16 * NH)
            for o in range(NH):
                for sp2 in range(2):
                    par = sp2
                    if eng_name == "sync":
                        g = sch.val(f"xr{par}_ve")
                        if g > 0:
                            eng.wait_ge(ve, g)
                        eng.dma_start(out=xres[par][:, :],
                                      in_=xs[o * 128:(o + 1) * 128,
                                             sp2 * 512:(sp2 + 1) * 512]).then_inc(st, 16)
                    sch.inc("st", 16)
                    if eng_name == "pe":
                        if o == 0 and sp2 == 0:
                            eng.wait_ge(ld, sch.val("ld"))
                        g = sch.val(f"yps{par}_ve")
                        if g > 0:
                            eng.wait_ge(ve, g)
                        for d in range(NH):
                            ii = eng.matmul(ps[4 + par][:, :], wtile(d)[:, o * 128:(o + 1) * 128],
                                            attn[d][:, sp2 * 512:(sp2 + 1) * 512],
                                            start=(d == 0), stop=(d == NH - 1))
                        ii.then_inc(mm, 1)
                    sch.inc("mm")
                    if eng_name == "dve":
                        eng.wait_ge(mm, sch.val("mm"))
                        eng.wait_ge(st, sch.val("st"))
                        g = sch.val(f"ysc{par}_st")
                        if g > 0:
                            eng.wait_ge(st, g)
                        eng.scalar_tensor_tensor(yscr[par][:, :], ps[4 + par][:, :], boT[:, o:o + 1],
                                                 xres[par][:, :],
                                                 ALU.add, ALU.add).then_inc(ve, 1)
                    sch.inc("ve")
                    sch.set(f"yps{par}_ve", sch.val("ve"))
                    sch.set(f"xr{par}_ve", sch.val("ve"))
                    if eng_name == "sync":
                        eng.wait_ge(ve, sch.val("ve"))
                        eng.dma_start(out=y[o * 128:(o + 1) * 128, sp2 * 512:(sp2 + 1) * 512],
                                      in_=yscr[par][:, :]).then_inc(st, 16)
                    sch.inc("st", 16)
                    sch.set(f"ysc{par}_st", sch.val("st"))

            if eng_name == "sync":
                eng.wait_ge(st, sch.val("st"))

        build_blocked(nc, gen, sems)
    return nc


S7A_SUPS = 32
S7A_CHUNKS = 64


def build_s7a(plain_first):
    nc = bass.Bass("TRN2", target_bir_lowering=False, debug=False)
    NH = 16
    xfull = nc.dram_tensor("xfull", [D, S], F32, kind="ExternalInput").ap()
    wqh = nc.dram_tensor("wqh", [D, 256], F32, kind="ExternalInput").ap()
    wkh = nc.dram_tensor("wkh", [D, 256], F32, kind="ExternalInput").ap()
    wvh = nc.dram_tensor("wvh", [D, 256], F32, kind="ExternalInput").ap()
    gng = nc.dram_tensor("gng", [2, 128], F32, kind="ExternalInput").ap()
    gnb = nc.dram_tensor("gnb", [2, 128], F32, kind="ExternalInput").ap()
    lng = nc.dram_tensor("lng", [NH, 128], F32, kind="ExternalInput").ap()
    lnb = nc.dram_tensor("lnb", [NH, 128], F32, kind="ExternalInput").ap()
    gam = nc.dram_tensor("gam", [1, 1], F32, kind="ExternalInput").ap()
    gny = nc.dram_tensor("gny", [256, S], BF16, kind="ExternalOutput").ap()

    ctx = ExitStack()
    with ctx:
        ps = [ctx.enter_context(nc.psum_tensor(f"ps{i}", [128, 512], F32)) for i in range(8)]
        ln = LNKit(nc, ctx, tw=512, banks=ps[0:4], gb=True)
        gn = LNKit(nc, ctx, tw=128, banks=ps[0:4], dmodel=256, tag="gn", gb=True)

        arena = ctx.enter_context(nc.sbuf_tensor("arena", [128, 24576], BF16))
        arena_f = arena.ap().bitcast(F32)
        q_fm = [ctx.enter_context(nc.sbuf_tensor(f"qfm{h2}", [128, S], BF16)) for h2 in range(2)]
        k_fm = [ctx.enter_context(nc.sbuf_tensor(f"kfm{h2}", [128, S], BF16)) for h2 in range(2)]
        v_tok = ctx.enter_context(nc.sbuf_tensor("v_tok", [128, 64 * 256], BF16))
        wq_sb = ctx.enter_context(nc.sbuf_tensor("wq_sb", [128, NH * 256], BF16))
        wk_sb = ctx.enter_context(nc.sbuf_tensor("wk_sb", [128, NH * 256], BF16))
        wv_sb = ctx.enter_context(nc.sbuf_tensor("wv_sb", [128, NH * 256], BF16))
        lngT = ctx.enter_context(nc.sbuf_tensor("lngT", [128, NH], F32))
        lnbT = ctx.enter_context(nc.sbuf_tensor("lnbT", [128, NH], F32))
        gngT = ctx.enter_context(nc.sbuf_tensor("gngT", [128, 2], F32))
        gnbT = ctx.enter_context(nc.sbuf_tensor("gnbT", [128, 2], F32))
        srow = ctx.enter_context(nc.sbuf_tensor("srow", [NH, 128], F32))
        identf = ctx.enter_context(nc.sbuf_tensor("identf", [128, 128], F32))
        identb = ctx.enter_context(nc.sbuf_tensor("identb", [128, 128], BF16))
        iden_io = ctx.enter_context(nc.sbuf_tensor("iden_io", [128, 128], F32))
        gamrow = ctx.enter_context(nc.sbuf_tensor("gamrow", [1, 8], F32))
        grow = ctx.enter_context(nc.sbuf_tensor("grow", [1, 256], F32))
        GQ = ctx.enter_context(nc.sbuf_tensor("GQ", [128, 128], BF16))
        KG = ctx.enter_context(nc.sbuf_tensor("KG", [128, 128], BF16))
        DTf = ctx.enter_context(nc.sbuf_tensor("DTf", [128, 128], F32))
        DTb = ctx.enter_context(nc.sbuf_tensor("DTb", [128, 128], BF16))
        R0 = ctx.enter_context(nc.sbuf_tensor("R0", [128, 128], F32))
        g128c = ctx.enter_context(nc.sbuf_tensor("g128c", [128, 1], F32))
        lngc = ctx.enter_context(nc.sbuf_tensor("lngc", [128, 1], F32))
        Ssb = ctx.enter_context(nc.sbuf_tensor("Ssb", [128, 512], F32))
        Sbf = ctx.enter_context(nc.sbuf_tensor("Sbf", [128, 512], BF16))
        qtb = ctx.enter_context(nc.sbuf_tensor("qtb", [128, 256], BF16))
        ktb = ctx.enter_context(nc.sbuf_tensor("ktb", [128, 256], BF16))
        kttk = ctx.enter_context(nc.sbuf_tensor("kttk", [128, 256], BF16))
        pbT = ctx.enter_context(nc.sbuf_tensor("pbT", [128, 128], BF16))
        ysb = ctx.enter_context(nc.sbuf_tensor("ysb", [128, 256], F32))
        gnt = ctx.enter_context(nc.sbuf_tensor("gnt", [128, 256], BF16))

        ld = ctx.enter_context(nc.semaphore("ld"))
        mm = ctx.enter_context(nc.semaphore("mm"))
        ve = ctx.enter_context(nc.semaphore("ve"))
        sc = ctx.enter_context(nc.semaphore("sc"))
        st = ctx.enter_context(nc.semaphore("st"))
        gp = ctx.enter_context(nc.semaphore("gp"))
        sems = {"ld": ld, "mm": mm, "ve": ve, "sc": sc, "st": st, "gp": gp}

        SW = 256
        def xsup(d):
            return arena_f[:, d * SW:(d + 1) * SW]

        def xnsup(d):
            return arena[:, 16384 + d * SW:16384 + (d + 1) * SW]

        def barrier(eng_name, eng, sch):
            for nm in ("mm", "ve", "sc", "ld", "st", "gp"):
                v = sch.val(nm)
                if v > 0:
                    eng.wait_ge(sems[nm], v)

        def gen(eng_name, eng, sch):
            ln.emit_init(eng_name, eng, sch)
            gn.emit_init(eng_name, eng, sch)
            if eng_name == "pool":
                eng.iota(R0[:, :], pattern=[[1, 128]], base=0, channel_multiplier=-1,
                         allow_small_or_imprecise_dtypes=True)
                eng.iota(iden_io[:, :], pattern=[[1, 128]], base=0, channel_multiplier=-1,
                         allow_small_or_imprecise_dtypes=True)
                eng.iota(grow[0:1, 0:128], pattern=[[1, 128]], base=1, channel_multiplier=0,
                         allow_small_or_imprecise_dtypes=True)
                eng.iota(grow[0:1, 128:256], pattern=[[-1, 128]], base=127, channel_multiplier=0,
                         allow_small_or_imprecise_dtypes=True).then_inc(gp, 1)
                eng.dma_start(out=gamrow[0:1, 0:1], in_=gam[:, :]).then_inc(ld, 16)
                eng.dma_start(out=srow[:, :], in_=lng[:, :]).then_inc(ld, 16)
                for d in range(NH):
                    eng.dma_start(out=wq_sb[:, d * 256:(d + 1) * 256],
                                  in_=wqh[d * 128:(d + 1) * 128, :]).then_inc(ld, 16)
                    eng.dma_start(out=wk_sb[:, d * 256:(d + 1) * 256],
                                  in_=wkh[d * 128:(d + 1) * 128, :]).then_inc(ld, 16)
                    eng.dma_start(out=wv_sb[:, d * 256:(d + 1) * 256],
                                  in_=wvh[d * 128:(d + 1) * 128, :]).then_inc(ld, 16)
            sch.inc("gp"); sch.inc("ld", 16 * (2 + 3 * NH))
            if eng_name == "act":
                eng.wait_ge(ld, sch.val("ld"))
                eng.wait_ge(gp, sch.val("gp"))
                eng.activation(gamrow[0:1, 1:2], gamrow[0:1, 0:1], AF.Ln).then_inc(sc, 1)
            sch.inc("sc")
            if eng_name == "dve":
                eng.wait_ge(sc, sch.val("sc"))
                eng.tensor_scalar(identf[:, :], iden_io[:, :], 0.0, None, ALU.is_equal)
                eng.tensor_copy(identb[:, :], identf[:, :])
                eng.tensor_scalar(grow[0:1, 0:128], grow[0:1, 0:128], gamrow[0:1, 1:2], None, ALU.mult)
                eng.tensor_scalar(grow[0:1, 128:256], grow[0:1, 128:256], gamrow[0:1, 1:2], None, ALU.mult)
                eng.tensor_scalar(gamrow[0:1, 2:3], gamrow[0:1, 1:2], 128.0, None,
                                  ALU.mult).then_inc(ve, 1)
            sch.inc("ve")
            if eng_name == "act":
                eng.wait_ge(ve, sch.val("ve"))
                eng.activation(grow[0:1, 0:128], grow[0:1, 0:128], AF.Exp)
                eng.activation(grow[0:1, 128:256], grow[0:1, 128:256], AF.Exp)
                eng.activation(gamrow[0:1, 3:4], gamrow[0:1, 2:3], AF.Exp).then_inc(sc, 1)
            sch.inc("sc")
            if eng_name == "pe":
                eng.wait_ge(sc, sch.val("sc"))
                eng.matmul(ps[4][:, 0:128], ln.ones_f[:, :], grow[0:1, 0:128], start=True, stop=True)
                eng.matmul(ps[5][:, 0:128], ln.ones_f[:, :], grow[0:1, 128:256], start=True, stop=True)
                eng.matmul(ps[6][:, 0:1], ln.ones_f[:, :], gamrow[0:1, 3:4], start=True, stop=True)
                eng.matmul(ps[6][:, 1:2], ln.ones_f[:, :], gamrow[0:1, 1:2], start=True, stop=True)
                eng.transpose(ps[7][0:128, 0:NH], srow[:, :], identf[0:NH, 0:NH]).then_inc(mm, 1)
            sch.inc("mm")
            if eng_name == "dve":
                eng.wait_ge(mm, sch.val("mm"))
                eng.tensor_copy(GQ[:, :], ps[4][:, 0:128])
                eng.tensor_copy(KG[:, :], ps[5][:, 0:128])
                eng.tensor_copy(g128c[:, :], ps[6][:, 0:1])
                eng.tensor_copy(lngc[:, :], ps[6][:, 1:2])
                eng.tensor_copy(lngT[:, :], ps[7][0:128, 0:NH])
                eng.memset(Ssb[:, :], 0.0)
                eng.tensor_scalar(DTf[:, :], R0[:, :], lngc[:, 0:1], None, ALU.mult).then_inc(ve, 1)
            sch.inc("ve")
            if eng_name == "act":
                eng.wait_ge(ve, sch.val("ve"))
                eng.activation(DTf[:, :], DTf[:, :], AF.Exp).then_inc(sc, 1)
            sch.inc("sc")
            if eng_name == "pool":
                eng.wait_ge(sc, sch.val("sc"))
                eng.affine_select(DTf[:, :], DTf[:, :], pattern=[[1, 128]], base=0,
                                  channel_multiplier=-1, compare_op=ALU.is_ge,
                                  fill=0.0).then_inc(gp, 1)
            sch.inc("gp")
            if eng_name == "dve":
                eng.wait_ge(gp, sch.val("gp"))
                eng.tensor_copy(DTb[:, :], DTf[:, :]).then_inc(ve, 1)
            sch.inc("ve")
            for which, (srcp, dstp, nr) in enumerate([(lnb, lnbT, NH), (gng, gngT, 2), (gnb, gnbT, 2)]):
                if eng_name == "pool":
                    eng.wait_ge(mm, sch.val("mm"))
                    eng.dma_start(out=srow[0:nr, :], in_=srcp[:, :]).then_inc(ld, 16)
                sch.inc("ld", 16)
                if eng_name == "pe":
                    eng.wait_ge(ld, sch.val("ld"))
                    eng.wait_ge(ve, sch.val("ve"))
                    eng.transpose(ps[7][0:128, 0:nr], srow[0:nr, :], identf[0:nr, 0:nr]).then_inc(mm, 1)
                sch.inc("mm")
                if eng_name == "dve":
                    eng.wait_ge(mm, sch.val("mm"))
                    eng.tensor_copy(dstp[:, 0:nr], ps[7][0:128, 0:nr]).then_inc(ve, 1)
                sch.inc("ve")

            lg_cols = [lngT[:, d:d + 1] for d in range(NH)]
            lb_cols = [lnbT[:, d:d + 1] for d in range(NH)]
            for sup in range(S7A_SUPS):
                barrier(eng_name, eng, sch)
                if eng_name == "sync":
                    for d in range(NH):
                        eng.dma_start(out=xsup(d), in_=xfull[d * 128:(d + 1) * 128,
                                                           sup * SW:(sup + 1) * SW]).then_inc(st, 16)
                sch.inc("st", 16 * NH)
                ln.emit(eng_name, eng, sch, [xsup(d) for d in range(NH)], 0,
                        [xnsup(d) for d in range(NH)], 0, tw=SW, xt_ready=("st", sch.val("st")),
                        gamma_col=lg_cols, beta_col=lb_cols, fuse_plain=plain_first)
                barrier(eng_name, eng, sch)
                for which in range(2):
                    wsb = wq_sb if which == 0 else wk_sb
                    dst = q_fm if which == 0 else k_fm
                    for h2 in range(2):
                        par = (which * 2 + h2) % 2
                        if eng_name == "pe":
                            eng.wait_ge(ve, sch.val("ve"))
                            g = sch.val(f"qk{par}_sc")
                            if g > 0:
                                eng.wait_ge(sc, g)
                            for d in range(NH):
                                ii = eng.matmul(ps[4 + par][:, 0:SW],
                                                wsb[:, d * 256 + h2 * 128:d * 256 + (h2 + 1) * 128],
                                                xnsup(d), start=(d == 0), stop=(d == NH - 1))
                            ii.then_inc(mm, 1)
                        sch.inc("mm")
                        if eng_name == "act":
                            eng.wait_ge(mm, sch.val("mm"))
                            eng.copy(dst[h2][:, sup * SW:(sup + 1) * SW],
                                     ps[4 + par][:, 0:SW]).then_inc(sc, 1)
                        sch.inc("sc")
                        sch.set(f"qk{par}_sc", sch.val("sc"))
                for tt in range(2):
                    par = tt % 2
                    if eng_name == "pe":
                        g = sch.val(f"vv{par}_sc")
                        if g > 0:
                            eng.wait_ge(sc, g)
                        for d in range(NH):
                            ii = eng.matmul(ps[6 + par][:, 0:256],
                                            xnsup(d)[:, tt * 128:(tt + 1) * 128],
                                            wv_sb[:, d * 256:(d + 1) * 256],
                                            start=(d == 0), stop=(d == NH - 1))
                        ii.then_inc(mm, 1)
                    sch.inc("mm")
                    if eng_name == "act":
                        eng.wait_ge(mm, sch.val("mm"))
                        eng.copy(v_tok[:, (sup * 2 + tt) * 256:(sup * 2 + tt + 1) * 256],
                                 ps[6 + par][:, 0:256]).then_inc(sc, 1)
                    sch.inc("sc")
                    sch.set(f"vv{par}_sc", sch.val("sc"))
            barrier(eng_name, eng, sch)

            gg_cols = [gngT[:, 0:1], gngT[:, 1:2]]
            gb_cols = [gnbT[:, 0:1], gnbT[:, 1:2]]
            for n in range(S7A_CHUNKS):
                barrier(eng_name, eng, sch)
                cn = slice(n * 128, (n + 1) * 128)
                if eng_name == "dve":
                    if n > 0:
                        eng.wait_ge(mm, sch.val("ret_mm"))
                    eng.tensor_tensor(qtb[:, 0:128], q_fm[0][:, cn], GQ[:, :], ALU.mult)
                    eng.tensor_tensor(ktb[:, 0:128], k_fm[0][:, cn], KG[:, :], ALU.mult)
                    eng.tensor_tensor(qtb[:, 128:256], q_fm[1][:, cn], GQ[:, :], ALU.mult)
                    eng.tensor_tensor(ktb[:, 128:256], k_fm[1][:, cn], KG[:, :], ALU.mult)
                    eng.tensor_copy(Sbf[:, :], Ssb[:, :]).then_inc(ve, 1)
                sch.inc("ve")
                if eng_name == "pe":
                    eng.wait_ge(ve, sch.val("ve"))
                    g = sch.val("ps4_ve")
                    if g > 0:
                        eng.wait_ge(ve, g)
                    eng.matmul(ps[4][:, 0:128], k_fm[0][:, cn], q_fm[0][:, cn], start=True, stop=False)
                    eng.matmul(ps[4][:, 0:128], k_fm[1][:, cn], q_fm[1][:, cn],
                               start=False, stop=True).then_inc(mm, 1)
                sch.inc("mm")
                if eng_name == "dve":
                    eng.wait_ge(mm, sch.val("mm"))
                    eng.tensor_tensor(pbT[:, :], ps[4][:, 0:128], DTb[:, :], ALU.mult).then_inc(ve, 1)
                sch.inc("ve")
                if eng_name == "pe":
                    eng.wait_ge(ve, sch.val("ve"))
                    for dvh in range(2):
                        r0 = dvh * 128
                        eng.matmul(ps[4][:, r0:r0 + 128],
                                   v_tok[:, n * 256 + dvh * 128:n * 256 + (dvh + 1) * 128],
                                   pbT[:, :], start=True, stop=False)
                        eng.matmul(ps[4][:, r0:r0 + 128],
                                   Sbf[:, 0 * 256 + dvh * 128:0 * 256 + (dvh + 1) * 128],
                                   qtb[:, 0:128], start=False, stop=False)
                        ii = eng.matmul(ps[4][:, r0:r0 + 128],
                                        Sbf[:, 1 * 256 + dvh * 128:1 * 256 + (dvh + 1) * 128],
                                        qtb[:, 128:256], start=False, stop=True)
                    ii.then_inc(mm, 1)
                sch.inc("mm")
                if eng_name == "act":
                    eng.wait_ge(mm, sch.val("mm"))
                    eng.copy(ysb[:, 0:256], ps[4][:, 0:256]).then_inc(sc, 1)
                sch.inc("sc")
                if eng_name == "dve":
                    g = sch.val("gnt_st")
                    if g > 0:
                        eng.wait_ge(st, g)
                gn.emit(eng_name, eng, sch, [ysb[:, 0:128], ysb[:, 128:256]], 0,
                        [gnt[:, 0:128], gnt[:, 128:256]], 0, tw=128, nch=2,
                        gamma_col=gg_cols, beta_col=gb_cols,
                        xt_ready=("sc", sch.val("sc")))
                if eng_name == "sync":
                    eng.wait_ge(ve, sch.val("ve"))
                    eng.dma_start(out=gny[0:128, cn], in_=gnt[:, 0:128]).then_inc(st, 16)
                    eng.dma_start(out=gny[128:256, cn], in_=gnt[:, 128:256]).then_inc(st, 16)
                sch.inc("st", 32)
                sch.set("gnt_st", sch.val("st"))
                for h2 in range(2):
                    if eng_name == "pe":
                        eng.wait_ge(ve, sch.val("ve"))
                        g = sch.val("ktr7_sc")
                        if g > 0:
                            eng.wait_ge(sc, g)
                        eng.transpose(ps[7].ap().bitcast(BF16)[:, 0:128],
                                      ktb[:, h2 * 128:(h2 + 1) * 128], identb[:, :]).then_inc(mm, 1)
                    sch.inc("mm")
                    if eng_name == "act":
                        eng.wait_ge(mm, sch.val("mm"))
                        eng.copy(kttk[:, h2 * 128:(h2 + 1) * 128],
                                 ps[7].ap().bitcast(BF16)[:, 0:128]).then_inc(sc, 1)
                    sch.inc("sc")
                    sch.set("ktr7_sc", sch.val("sc"))
                if eng_name == "pe":
                    eng.wait_ge(sc, sch.val("sc"))
                    for dkh in range(2):
                        ii = eng.matmul(ps[4][:, dkh * 256:(dkh + 1) * 256],
                                        kttk[:, dkh * 128:(dkh + 1) * 128],
                                        v_tok[:, n * 256:(n + 1) * 256],
                                        start=True, stop=True)
                    ii.then_inc(mm, 1)
                sch.inc("mm")
                if eng_name == "dve":
                    eng.wait_ge(mm, sch.val("mm"))
                    eng.scalar_tensor_tensor(Ssb[:, :], Ssb[:, :], g128c[:, 0:1], ps[4][:, 0:512],
                                             ALU.mult, ALU.add).then_inc(ve, 1)
                sch.inc("ve")
                sch.set("ps4_ve", sch.val("ve"))
                sch.set("ret_mm", sch.val("mm"))

            if eng_name == "sync":
                eng.wait_ge(st, sch.val("st"))

        build_blocked(nc, gen, sems)
    return nc


def build_s7b(plain_first, add_x):
    nc = bass.Bass("TRN2", target_bir_lowering=False, debug=False)
    NH = 16
    xin = nc.dram_tensor("xin", [D, 1024], F32, kind="ExternalInput").ap()
    gnyin = nc.dram_tensor("gnyin", [D, 1024], BF16, kind="ExternalInput").ap()
    wgslab = nc.dram_tensor("wgslab", [NH, 128, 2048], F32, kind="ExternalInput").ap()
    woslab = nc.dram_tensor("woslab", [NH, 128, 2048], F32, kind="ExternalInput").ap()
    w1slab = nc.dram_tensor("w1slab", [64, 128, 2048], BF16, kind="ExternalInput").ap()
    w2slab = nc.dram_tensor("w2slab", [NH, 128, 8192], BF16, kind="ExternalInput").ap()
    b1in = nc.dram_tensor("b1in", [64, 128], F32, kind="ExternalInput").ap()
    b2in = nc.dram_tensor("b2in", [NH, 128], F32, kind="ExternalInput").ap()
    lng = nc.dram_tensor("lng", [NH, 128], F32, kind="ExternalInput").ap()
    lnb = nc.dram_tensor("lnb", [NH, 128], F32, kind="ExternalInput").ap()
    if add_x:
        xres_in = nc.dram_tensor("xres", [D, 1024], F32, kind="ExternalInput").ap()
    y = nc.dram_tensor("y", [D, 1024], F32, kind="ExternalOutput").ap()

    ctx = ExitStack()
    with ctx:
        ps = [ctx.enter_context(nc.psum_tensor(f"ps{i}", [128, 512], F32)) for i in range(8)]
        ln = LNKit(nc, ctx, tw=512, banks=ps[0:4], gb=True)

        curs = [ctx.enter_context(nc.sbuf_tensor(f"curs{d}", [128, 1024], F32)) for d in range(NH)]
        xn2 = [ctx.enter_context(nc.sbuf_tensor(f"xn2_{d}", [128, 1024], BF16)) for d in range(NH)]
        prod = [ctx.enter_context(nc.sbuf_tensor(f"prod{d}", [128, 1024], BF16)) for d in range(NH)]
        warena = ctx.enter_context(nc.sbuf_tensor("warena", [128, 20480], BF16))
        gnyb = [ctx.enter_context(nc.sbuf_tensor(f"gnyb{p}", [128, 1024], BF16)) for p in range(2)]
        xrb = [ctx.enter_context(nc.sbuf_tensor(f"xrb{p}", [128, 256], F32)) for p in range(2)]
        gsil = ctx.enter_context(nc.sbuf_tensor("gsil", [128, 1024], BF16))
        yscr = [ctx.enter_context(nc.sbuf_tensor(f"yscr{p}", [128, 256], F32)) for p in range(2)]
        b1T = ctx.enter_context(nc.sbuf_tensor("b1T", [128, 64], F32))
        b2T = ctx.enter_context(nc.sbuf_tensor("b2T", [128, NH], F32))
        lngT = ctx.enter_context(nc.sbuf_tensor("lngT", [128, NH], F32))
        lnbT = ctx.enter_context(nc.sbuf_tensor("lnbT", [128, NH], F32))
        srow = ctx.enter_context(nc.sbuf_tensor("srow", [64, 128], F32))
        iden_io = ctx.enter_context(nc.sbuf_tensor("iden_io", [128, 128], F32))
        identf = iden_io

        ld = ctx.enter_context(nc.semaphore("ld"))
        mm = ctx.enter_context(nc.semaphore("mm"))
        ve = ctx.enter_context(nc.semaphore("ve"))
        sc = ctx.enter_context(nc.semaphore("sc"))
        st = ctx.enter_context(nc.semaphore("st"))
        gp = ctx.enter_context(nc.semaphore("gp"))
        sems = {"ld": ld, "mm": mm, "ve": ve, "sc": sc, "st": st, "gp": gp}

        def wg_buf(p):
            return warena[:, p * 2048:(p + 1) * 2048]

        def w1buf(p):
            return warena[:, p * 2048:(p + 1) * 2048]

        def w2buf(p):
            return warena[:, 4096 + p * 8192:4096 + (p + 1) * 8192]

        def h1t(o):
            return prod[o // 4][:, (o % 4) * 256:(o % 4 + 1) * 256]

        def barrier(eng_name, eng, sch):
            for nm in ("mm", "ve", "sc", "ld", "st", "gp"):
                v = sch.val(nm)
                if v > 0:
                    eng.wait_ge(sems[nm], v)

        def gen(eng_name, eng, sch):
            ln.emit_init(eng_name, eng, sch)
            if eng_name == "pool":
                eng.iota(iden_io[:, :], pattern=[[1, 128]], base=0, channel_multiplier=-1,
                         allow_small_or_imprecise_dtypes=True).then_inc(gp, 1)
            sch.inc("gp")
            if eng_name == "dve":
                eng.wait_ge(gp, sch.val("gp"))
                eng.tensor_scalar(iden_io[:, :], iden_io[:, :], 0.0, None, ALU.is_equal).then_inc(ve, 1)
            sch.inc("ve")
            for which, (srcp, dstp, nr) in enumerate([(b1in, b1T, 64), (b2in, b2T, NH),
                                                      (lng, lngT, NH), (lnb, lnbT, NH)]):
                if eng_name == "pool":
                    if which > 0:
                        eng.wait_ge(mm, sch.val("mm"))
                    eng.dma_start(out=srow[0:nr, :], in_=srcp[:, :]).then_inc(ld, 16)
                sch.inc("ld", 16)
                if eng_name == "pe":
                    eng.wait_ge(ld, sch.val("ld"))
                    eng.wait_ge(ve, sch.val("ve"))
                    g = sch.val("tr_ve")
                    if g > 0:
                        eng.wait_ge(ve, g)
                    eng.transpose(ps[4][0:128, 0:nr], srow[0:nr, :], identf[0:nr, 0:nr]).then_inc(mm, 1)
                sch.inc("mm")
                if eng_name == "dve":
                    eng.wait_ge(mm, sch.val("mm"))
                    eng.tensor_copy(dstp[:, 0:nr], ps[4][0:128, 0:nr]).then_inc(ve, 1)
                sch.inc("ve")
                sch.set("tr_ve", sch.val("ve"))
            if eng_name == "sync":
                for d in range(NH):
                    eng.dma_start(out=curs[d][:, :], in_=xin[d * 128:(d + 1) * 128, :]).then_inc(st, 16)
            sch.inc("st", 16 * NH)

            lg_cols = [lngT[:, d:d + 1] for d in range(NH)]
            lb_cols = [lnbT[:, d:d + 1] for d in range(NH)]
            for sp2 in range(2):
                if plain_first:
                    ln.emit(eng_name, eng, sch, curs, sp2 * 512, curs, sp2 * 512,
                            xt_ready=("st", sch.val("st")))
                ln.emit(eng_name, eng, sch, curs, sp2 * 512, xn2, sp2 * 512,
                        xt_ready=("st", sch.val("st")), gamma_col=lg_cols, beta_col=lb_cols)
            barrier(eng_name, eng, sch)

            for o in range(NH):
                par = o % 2
                if eng_name == "pool":
                    g = sch.val(f"wgb{par}_mm")
                    if g > 0:
                        eng.wait_ge(mm, g)
                    eng.dma_start(out=wg_buf(par), in_=wgslab[o]).then_inc(ld, 16)
                sch.inc("ld", 16)
                if eng_name == "sync":
                    g = sch.val(f"gnyb{par}_ve")
                    if g > 0:
                        eng.wait_ge(ve, g)
                    eng.dma_start(out=gnyb[par][:, :], in_=gnyin[o * 128:(o + 1) * 128, :]).then_inc(st, 16)
                sch.inc("st", 16)
                for sp2 in range(2):
                    pp = 4 + sp2
                    if eng_name == "pe":
                        eng.wait_ge(ld, sch.val("ld"))
                        g = sch.val(f"gps{sp2}_sc")
                        if g > 0:
                            eng.wait_ge(sc, g)
                        for d in range(NH):
                            ii = eng.matmul(ps[pp][:, :], wg_buf(par)[:, d * 128:(d + 1) * 128],
                                            xn2[d][:, sp2 * 512:(sp2 + 1) * 512],
                                            start=(d == 0), stop=(d == NH - 1))
                        ii.then_inc(mm, 1)
                    sch.inc("mm")
                    sch.set(f"wgb{par}_mm", sch.val("mm"))
                    if eng_name == "act":
                        eng.wait_ge(mm, sch.val("mm"))
                        g = sch.val(f"gsil{sp2}_ve")
                        if g > 0:
                            eng.wait_ge(ve, g)
                        eng.activation(gsil[:, sp2 * 512:(sp2 + 1) * 512], ps[pp][:, :],
                                       AF.Silu).then_inc(sc, 1)
                    sch.inc("sc")
                    sch.set(f"gps{sp2}_sc", sch.val("sc"))
                    if eng_name == "dve":
                        eng.wait_ge(sc, sch.val("sc"))
                        eng.wait_ge(st, sch.val("st"))
                        eng.tensor_tensor(prod[o][:, sp2 * 512:(sp2 + 1) * 512],
                                          gsil[:, sp2 * 512:(sp2 + 1) * 512],
                                          gnyb[par][:, sp2 * 512:(sp2 + 1) * 512],
                                          ALU.mult).then_inc(ve, 1)
                    sch.inc("ve")
                    sch.set(f"gsil{sp2}_ve", sch.val("ve"))
                sch.set(f"gnyb{par}_ve", sch.val("ve"))
            barrier(eng_name, eng, sch)

            for o in range(NH):
                par = o % 2
                if eng_name == "pool":
                    g = sch.val(f"wgb{par}_mm")
                    if g > 0:
                        eng.wait_ge(mm, g)
                    eng.dma_start(out=wg_buf(par), in_=woslab[o]).then_inc(ld, 16)
                sch.inc("ld", 16)
                for sp2 in range(2):
                    pp = 4 + sp2
                    if eng_name == "pe":
                        eng.wait_ge(ld, sch.val("ld"))
                        g = sch.val(f"gps{sp2}_sc")
                        if g > 0:
                            eng.wait_ge(ve, g)
                        for d in range(NH):
                            ii = eng.matmul(ps[pp][:, :], wg_buf(par)[:, d * 128:(d + 1) * 128],
                                            prod[d][:, sp2 * 512:(sp2 + 1) * 512],
                                            start=(d == 0), stop=(d == NH - 1))
                        ii.then_inc(mm, 1)
                    sch.inc("mm")
                    sch.set(f"wgb{par}_mm", sch.val("mm"))
                    if eng_name == "dve":
                        eng.wait_ge(mm, sch.val("mm"))
                        eng.tensor_tensor(curs[o][:, sp2 * 512:(sp2 + 1) * 512],
                                          curs[o][:, sp2 * 512:(sp2 + 1) * 512],
                                          ps[pp][:, :], ALU.add).then_inc(ve, 1)
                    sch.inc("ve")
                    sch.set(f"gps{sp2}_sc", sch.val("ve"))
            barrier(eng_name, eng, sch)
            sch.set("gps0_sc", 0); sch.set("gps1_sc", 0)

            for sp2 in range(2):
                ln.emit(eng_name, eng, sch, curs, sp2 * 512, xn2, sp2 * 512,
                        xn_free_mm=sch.val("mm"), gamma_col=lg_cols, beta_col=lb_cols)
            barrier(eng_name, eng, sch)

            for sp in range(4):
                c0, c1 = sp * 256, (sp + 1) * 256
                for o in range(64):
                    par = o % 2
                    if eng_name == "pool":
                        g = sch.val(f"w1b{par}_mm")
                        if g > 0:
                            eng.wait_ge(mm, g)
                        eng.dma_start(out=w1buf(par), in_=w1slab[o]).then_inc(ld, 16)
                    sch.inc("ld", 16)
                    if eng_name == "pe":
                        eng.wait_ge(ld, sch.val("ld"))
                        g = sch.val(f"h1ps{par}_sc")
                        if g > 0:
                            eng.wait_ge(sc, g)
                        if sp == 0 and o == 0:
                            eng.wait_ge(ve, sch.val("ve"))
                        for d in range(NH):
                            ii = eng.matmul(ps[4 + par][:, 0:256], w1buf(par)[:, d * 128:(d + 1) * 128],
                                            xn2[d][:, c0:c1], start=(d == 0), stop=(d == NH - 1))
                        ii.then_inc(mm, 1)
                    sch.inc("mm")
                    sch.set(f"w1b{par}_mm", sch.val("mm"))
                    if eng_name == "act":
                        eng.wait_ge(mm, sch.val("mm"))
                        g = sch.val(f"h1c_mm")
                        if g > 0 and o < 2:
                            eng.wait_ge(mm, g)
                        eng.activation(h1t(o), ps[4 + par][:, 0:256], AF.Gelu_apprx_tanh,
                                       bias=b1T[:, o:o + 1]).then_inc(sc, 1)
                    sch.inc("sc")
                    sch.set(f"h1ps{par}_sc", sch.val("sc"))
                for o2 in range(NH):
                    par = o2 % 2
                    if eng_name == "pool":
                        g = sch.val(f"w2b{par}_mm")
                        if g > 0:
                            eng.wait_ge(mm, g)
                        eng.dma_start(out=w2buf(par), in_=w2slab[o2]).then_inc(ld, 16)
                    sch.inc("ld", 16)
                    if add_x:
                        if eng_name == "sync":
                            g = sch.val(f"xrb{par}_ve")
                            if g > 0:
                                eng.wait_ge(ve, g)
                            eng.dma_start(out=xrb[par][:, 0:256],
                                          in_=xres_in[o2 * 128:(o2 + 1) * 128, c0:c1]).then_inc(st, 16)
                        sch.inc("st", 16)
                    if eng_name == "pe":
                        eng.wait_ge(ld, sch.val("ld"))
                        if o2 == 0:
                            eng.wait_ge(sc, sch.val("sc"))
                        g = sch.val(f"y2ps{par}_ve")
                        if g > 0:
                            eng.wait_ge(ve, g)
                        for dd in range(64):
                            ii = eng.matmul(ps[6 + par][:, 0:256], w2buf(par)[:, dd * 128:(dd + 1) * 128],
                                            h1t(dd), start=(dd == 0), stop=(dd == 63))
                        ii.then_inc(mm, 1)
                    sch.inc("mm")
                    sch.set(f"w2b{par}_mm", sch.val("mm"))
                    if eng_name == "dve":
                        eng.wait_ge(mm, sch.val("mm"))
                        if add_x:
                            eng.wait_ge(st, sch.val("st"))
                        g = sch.val(f"ys{par}_st")
                        if g > 0:
                            eng.wait_ge(st, g)
                        eng.scalar_tensor_tensor(yscr[par][:, :], ps[6 + par][:, 0:256],
                                                 b2T[:, o2:o2 + 1], curs[o2][:, c0:c1],
                                                 ALU.add, ALU.add)
                        if add_x:
                            ii = eng.tensor_tensor(yscr[par][:, :], yscr[par][:, :],
                                                   xrb[par][:, 0:256], ALU.add)
                            ii.then_inc(ve, 1)
                        else:
                            eng.tensor_copy(yscr[par][:, :], yscr[par][:, :]).then_inc(ve, 1)
                    sch.inc("ve")
                    sch.set(f"y2ps{par}_ve", sch.val("ve"))
                    if add_x:
                        sch.set(f"xrb{par}_ve", sch.val("ve"))
                    if eng_name == "sync":
                        eng.wait_ge(ve, sch.val("ve"))
                        eng.dma_start(out=y[o2 * 128:(o2 + 1) * 128, c0:c1],
                                      in_=yscr[par][:, :]).then_inc(st, 16)
                    sch.inc("st", 16)
                    sch.set(f"ys{par}_st", sch.val("st"))
                sch.set("h1c_mm", sch.val("mm"))

            if eng_name == "sync":
                eng.wait_ge(st, sch.val("st"))

        build_blocked(nc, gen, sems)
    return nc

GAMMAS_RET = np.asarray(1.0 - np.exp(np.linspace(np.log(1.0 / 32), np.log(1.0 / 512), 8)),
                        np.float32)


def core_cols(c):
    a, b = BLOCKS_OF_CORE[c]
    return [(a * BLK, (a + 1) * BLK), (b * BLK, (b + 1) * BLK)]


def scatter_full(per_core):
    out = np.zeros((D, S), np.float32)
    for c in range(NCORES):
        (a0, a1), (b0, b1) = core_cols(c)
        out[:, a0:a1] = per_core[c][:, 0:BLK]
        out[:, b0:b1] = per_core[c][:, BLK:2 * BLK]
    return out


def slice_core(xfm, c):
    (a0, a1), (b0, b1) = core_cols(c)
    return np.ascontiguousarray(np.concatenate([xfm[:, a0:a1], xfm[:, b0:b1]], axis=1))


def host_slab16(W):
    return np.ascontiguousarray(
        np.asarray(W).reshape(16, 128, 16, 128).transpose(2, 1, 0, 3).reshape(16, 128, 2048))


_NC_CACHE = {}


def _get_nc(name, builder, *args):
    if name not in _NC_CACHE:
        _NC_CACHE[name] = builder(*args)
    return _NC_CACHE[name]


def _qpos(c):
    (a0, a1), (b0, b1) = core_cols(c)
    return np.concatenate([np.arange(a0, a1), np.arange(b0, b1)]).astype(np.float32).reshape(1, 1024)


def _launch(nc, in_maps):
    res = run_bass_kernel_spmd(nc, in_maps, core_ids=list(range(NCORES)))
    return res.results


def _run_s7_layer(i, cur_fm, plain_first, add_x, inp, slabs, x_res_fm, lng, lnb):
    nca = _get_nc(f"s7a_{plain_first}", build_s7a, plain_first)
    Wq = np.asarray(inp["L1_attn_b_Wq"][i])
    Wk = np.asarray(inp["L1_attn_b_Wk"][i])
    Wv = np.asarray(inp["L1_attn_b_Wv"][i])
    gg = np.asarray(inp["L1_attn_b_gn_gamma"][i]).reshape(8, 2, 128).astype(np.float32)
    gb = np.asarray(inp["L1_attn_b_gn_beta"][i]).reshape(8, 2, 128).astype(np.float32)
    in_maps = []
    for c in range(NCORES):
        in_maps.append({
            "xfull": cur_fm,
            "wqh": np.ascontiguousarray(Wq[:, c * 256:(c + 1) * 256]),
            "wkh": np.ascontiguousarray(Wk[:, c * 256:(c + 1) * 256]),
            "wvh": np.ascontiguousarray(Wv[:, c * 256:(c + 1) * 256]),
            "gng": np.ascontiguousarray(gg[c]), "gnb": np.ascontiguousarray(gb[c]),
            "lng": lng, "lnb": lnb,
            "gam": np.array([[GAMMAS_RET[c]]], np.float32),
        })
    res = _launch(nca, in_maps)
    gnY = np.concatenate([res[c]["gny"] for c in range(NCORES)], axis=0)

    ncb = _get_nc(f"s7b_{plain_first}_{add_x}", build_s7b, plain_first, add_x)
    wgs = host_slab16(inp["L1_attn_b_Wg"][i])
    wos = host_slab16(inp["L1_attn_b_Wo"][i])
    w1s = slabs["w1a" if i == 0 else "w1b"]
    w2s = slabs["w2a" if i == 0 else "w2b"]
    b1 = np.asarray(inp["L1_attn_b_b1"][i]).reshape(64, 128).astype(np.float32)
    b2 = np.asarray(inp["L1_attn_b_b2"][i]).reshape(16, 128).astype(np.float32)
    in_maps2 = []
    for c in range(NCORES):
        (ca0, ca1), (cb0, cb1) = core_cols(c)
        gslice = np.ascontiguousarray(np.concatenate([gnY[:, ca0:ca1], gnY[:, cb0:cb1]], axis=1))
        m = {
            "xin": slice_core(cur_fm, c), "gnyin": gslice,
            "wgslab": wgs, "woslab": wos, "w1slab": w1s, "w2slab": w2s,
            "b1in": b1, "b2in": b2, "lng": lng, "lnb": lnb,
        }
        if add_x:
            m["xres"] = slice_core(x_res_fm, c)
        in_maps2.append(m)
    res2 = _launch(ncb, in_maps2)
    return scatter_full([res2[c]["y"] for c in range(NCORES)])


def kernel(**inputs):
    inp = {k: np.asarray(v) for k, v in inputs.items()}

    ncp = _get_nc("prep", build_prep)
    slabs = host_prep_assemble(_launch(ncp, host_prep_inputs(inp)))

    xfm = np.ascontiguousarray(inp["L0_attn_a_x"][0].astype(np.float32).T)

    nc1 = _get_nc("s1", build_s1)
    bo1 = np.asarray(inp["L0_attn_a_bo"]).reshape(16, 128).astype(np.float32)
    in_maps = [{"xs": slice_core(xfm, c), "xa": xfm, "qpos": _qpos(c),
                "wq": inp["L0_attn_a_Wq"], "wkv": inp["L0_attn_a_Wkv"],
                "wo": inp["L0_attn_a_Wo"], "bo": bo1} for c in range(NCORES)]
    xfm = scatter_full([r["y"] for r in _launch(nc1, in_maps)])

    ncf0 = _get_nc("ffn44", build_ffn, 44, True)
    gs = np.asarray(inp["L0_ffn_gate_s"]).reshape(44, 128).astype(np.float32)
    us = np.asarray(inp["L0_ffn_up_s"]).reshape(44, 128).astype(np.float32)
    ds = np.asarray(inp["L0_ffn_down_s"]).reshape(16, 128).astype(np.float32)
    in_maps = [{"xs": slice_core(xfm, c), "gslab": slabs["gw"], "uslab": slabs["uw"],
                "dslab": slabs["dw"], "gs": gs, "us": us, "ds": ds} for c in range(NCORES)]
    xfm = scatter_full([r["y"] for r in _launch(ncf0, in_maps)])

    nc3 = _get_nc("s3", build_s3)
    in_maps = [{"xs": slice_core(xfm, c), "xa": xfm, "qpos": _qpos(c),
                "bq": slabs["bq"], "bk": slabs["bk"], "bv": slabs["bv"],
                "bo": slabs["bo"]} for c in range(NCORES)]
    xfm = scatter_full([r["y"] for r in _launch(nc3, in_maps)])

    in_maps = [{"xs": slice_core(xfm, c), "gslab": slabs["gw"], "uslab": slabs["uw"],
                "dslab": slabs["dw"], "gs": gs, "us": us, "ds": ds} for c in range(NCORES)]
    xfm = scatter_full([r["y"] for r in _launch(ncf0, in_maps)])

    nc5 = _get_nc("s5", build_s5)
    bo5 = np.asarray(inp["L1_attn_a_bo"]).reshape(16, 128).astype(np.float32)
    in_maps = [{"xs": slice_core(xfm, c), "xa": xfm,
                "wq": inp["L1_attn_a_Wq"], "wk": inp["L1_attn_a_Wk"],
                "wv": inp["L1_attn_a_Wv"], "wo": inp["L1_attn_a_Wo"],
                "bo": bo5} for c in range(NCORES)]
    xfm = scatter_full([r["y"] for r in _launch(nc5, in_maps)])

    ncf1 = _get_nc("ffn43", build_ffn, 43, False)
    in_maps = [{"xs": slice_core(xfm, c), "gslab": slabs["gg"], "uslab": slabs["gu"],
                "dslab": slabs["gd"]} for c in range(NCORES)]
    xfm = scatter_full([r["y"] for r in _launch(ncf1, in_maps)])

    lng = np.asarray(inp["L1_attn_b_ln_gamma"]).reshape(16, 128).astype(np.float32)
    lnb = np.asarray(inp["L1_attn_b_ln_beta"]).reshape(16, 128).astype(np.float32)
    x6fm = xfm
    cur = _run_s7_layer(0, x6fm, True, False, inp, slabs, None, lng, lnb)
    xfm = _run_s7_layer(1, np.ascontiguousarray(cur), False, True, inp, slabs, x6fm, lng, lnb)

    in_maps = [{"xs": slice_core(xfm, c), "gslab": slabs["gg"], "uslab": slabs["gu"],
                "dslab": slabs["gd"]} for c in range(NCORES)]
    xfm = scatter_full([r["y"] for r in _launch(ncf1, in_maps)])

    return np.ascontiguousarray(xfm.T).reshape(1, S, D).astype(np.float32)
